# version 09
# baseline (speedup 1.0000x reference)
import math
from contextlib import ExitStack

import numpy as np
import concourse.bass as bass
import concourse.mybir as mybir
from concourse.bass_utils import run_bass_kernel_spmd

F32 = mybir.dt.float32
BF16 = mybir.dt.bfloat16
AF = mybir.ActivationFunctionType
ALU = mybir.AluOpType
AX = mybir.AxisListType

NEG_BIG = -30000.0


def real_cfg():
    return dict(
        d=2048,
        hv=16, hk=8,
        dk=128, dv=128,
        tb=4096,
        tf=2048,
        i_sh=5632, i_e=1408, n_e=8,
        n_cores=8,
        pair_groups=[[0, 1], [2, 3], [4, 5], [6, 7]],
    )


def real_cfg_e():
    return dict(
        d=2048,
        hv=32, hk=16,
        dk=128, dv=128,
        tb=4096,
        tf=2048,
        hg=8,
        i_sh=5632, i_e=1408, n_e=8,
        n_cores=8,
        emode=True,
        pair_groups=[[0, 1], [2, 3], [4, 5], [6, 7]],
    )



class Buf:
    __slots__ = ("name", "last_w", "readers")

    def __init__(self, name):
        self.name = name
        self.last_w = None
        self.readers = []


class Prog:
    COMPUTE_SEM = {"tensor": "pe", "vector": "dve", "scalar": "act",
                   "gpsimd": "pool", "sync": "sp"}
    DMA_QUEUES = {"sync": "sdma", "gpsimd": "gdma", "scalar": "adma"}
    NLANES = 8
    ENGINES = ["tensor", "vector", "scalar", "gpsimd", "sync"]

    def __init__(self, nc):
        self.nc = nc
        self.cnt = {}
        self.sems = {}
        self.sat = {e: {} for e in self.ENGINES}
        self.steps = {e: [] for e in self.ENGINES}
        self._stack = ExitStack()
        keys = ["pe", "dve", "act", "pool", "sp", "cc"]
        for q in self.DMA_QUEUES.values():
            keys += [f"{q}{i}" for i in range(self.NLANES)]
        for key in keys:
            self.sems[key] = self._stack.enter_context(nc.semaphore(key))
            self.cnt[key] = 0
        self.qrr = {q: 0 for q in self.DMA_QUEUES.values()}
        self.nsteps = 0

    def buf(self, name):
        return Buf(name)

    def step(self, engine, fn, reads=(), writes=(), semkey=None, inc=None,
             extra_waits=()):
        if semkey is None:
            semkey = self.COMPUTE_SEM[engine]
            if inc is None:
                inc = 1
        waits = {}
        for k, c in extra_waits:
            if waits.get(k, 0) < c:
                waits[k] = c

        def need(dep):
            k, c = dep
            if waits.get(k, 0) < c:
                waits[k] = c

        for b in reads:
            if b.last_w is not None:
                need(b.last_w)
        for b in writes:
            if b.last_w is not None:
                need(b.last_w)
            for r in b.readers:
                need(r)
        sat = self.sat[engine]
        final_waits = []
        for k, c in sorted(waits.items()):
            if sat.get(k, 0) >= c:
                continue
            final_waits.append((self.sems[k], c))
            sat[k] = c
        rec = None
        if inc:
            self.cnt[semkey] += inc
            rec = (semkey, self.cnt[semkey])
        for b in writes:
            assert rec is not None, "writing step must inc a semaphore"
            b.last_w = rec
            b.readers = []
        if rec is not None:
            for b in reads:
                b.readers.append(rec)
        self.steps[engine].append(
            (final_waits, fn, (self.sems[semkey], inc) if inc else None))
        self.nsteps += 1

    def dma(self, queue_engine, fn, reads=(), writes=()):
        q = self.DMA_QUEUES[queue_engine]
        lane = self.qrr[q] % self.NLANES
        self.qrr[q] += 1
        semkey = f"{q}{lane}"
        prev = self.cnt[semkey]
        ew = [(semkey, prev)] if prev else []
        self.step(queue_engine, fn, reads=reads, writes=writes,
                  semkey=semkey, inc=16, extra_waits=ew)

    def barrier(self):
        snapshot = dict(self.cnt)
        for e in self.ENGINES:
            sat = self.sat[e]
            fw = []
            for k, c in snapshot.items():
                if c and sat.get(k, 0) < c:
                    fw.append((self.sems[k], c))
                    sat[k] = c
            if fw:
                self.steps[e].append((fw, None, None))

    def finalize(self):
        nc = self.nc

        def mk(engine_name):
            def run(eng):
                for waits, fn, inc in self.steps[engine_name]:
                    for sem, val in waits:
                        eng.wait_ge(sem, val)
                    if fn is not None:
                        ins = fn(eng)
                        if inc is not None:
                            assert ins is not None, f"step on {engine_name} returned no instruction"
                            ins.then_inc(inc[0], inc[1])
                    else:
                        assert inc is None
            return run

        with nc.Block() as block:
            block.tensor(mk("tensor"))
            block.vector(mk("vector"))
            block.scalar(mk("scalar"))
            block.gpsimd(mk("gpsimd"))
            block.sync(mk("sync"))

    def close(self):
        self._stack.close()



def make_consts(cfg):
    C = 128
    BS = 32
    hv = cfg.get("hg") or min(cfg["hv"], 16)
    lo = np.tril(np.ones((C, C), np.float32), -1)
    loD = np.zeros((C, C), np.float32)
    for s in range(C // BS):
        sl = slice(s * BS, (s + 1) * BS)
        loD[sl, sl] = np.tril(np.ones((BS, BS), np.float32), -1)
    loL = lo - loD
    upD = loD.T
    upL = loL.T
    up_incl = np.triu(np.ones((C, C), np.float32), 0)

    def maskT(m):
        return np.where(m > 0, 0.0, NEG_BIG).astype(np.float32).T.copy()

    sel16 = np.zeros((hv, hv * C), np.float32)
    for p in range(hv):
        sel16[p, p * C:(p + 1) * C] = 1.0
    repid = np.tile(np.eye(C, dtype=np.float32), (1, hv))
    colsel = np.zeros((6, 384), np.float32)
    for s in range(3):
        colsel[2 * s:2 * s + 2, s * 128:(s + 1) * 128] = 1.0

    return {
        "c_ident_f": np.eye(C, dtype=np.float32),
        "c_ident_b": np.eye(C, dtype=np.float32),
        "c_sel16": sel16,
        "c_colsel": colsel,
        "c_maskT_n": maskT(loD),
        "c_maskT_nt": maskT(upD),
        "c_maskT_LT": maskT(upL),
        "c_maskT_at": maskT(up_incl),
        "c_ones128": np.ones((C, 1), np.float32),
        "c_onesrow": np.ones((1, C), np.float32),
        "c_oneshv": np.ones((hv, C), np.float32),
        "c_repid_b": repid,
    }



MODE = "fused_e"


def kernel(**inputs):
    cfg = real_cfg_e() if MODE == "fused_e" else real_cfg()
    maps = shard_inputs(cfg, inputs)
    ncore = cfg["n_cores"]
    tf = cfg["tf"]
    if MODE == "fused_e":
        nc = bass.Bass("TRN2", target_bir_lowering=False)
        names = build_program(nc, cfg, "full")
        ins = [{k: m[k] for k in names} for m in maps]
        res = run_bass_kernel_spmd(nc, ins, core_ids=list(range(ncore)))
        return assemble_output(cfg, res)

    nc1 = bass.Bass("TRN2", target_bir_lowering=False)
    names1 = build_program(nc1, cfg, "attn")
    ins1 = [{k: m[k] for k in names1} for m in maps]
    res1 = run_bass_kernel_spmd(nc1, ins1, core_ids=list(range(ncore)))
    sparts = [np.asarray(res1.results[c]["spart_out"]) for c in range(ncore)]

    nc2 = bass.Bass("TRN2", target_bir_lowering=False)
    names2 = build_program(nc2, cfg, "ffn")
    ins2 = []
    for c in range(ncore):
        half = c % 2
        partner = c ^ 1
        m2 = {k: maps[c][k] for k in names2 if k not in ("s_half0", "s_half1")}
        m2["s_half0"] = np.ascontiguousarray(sparts[c][half * tf:(half + 1) * tf])
        m2["s_half1"] = np.ascontiguousarray(sparts[partner][half * tf:(half + 1) * tf])
        ins2.append(m2)
    res2 = run_bass_kernel_spmd(nc2, ins2, core_ids=list(range(ncore)))
    return assemble_output(cfg, res2)


def shard_inputs(cfg, inputs):
    d = cfg["d"]
    tb, tf = cfg["tb"], cfg["tf"]
    hv, hk, dk, dv = cfg["hv"], cfg["hk"], cfg["dk"], cfg["dv"]
    emode = bool(cfg.get("emode", False))
    HK_full = hk if emode else 2 * hk
    HV_full = hv if emode else 2 * hv
    key_dim = HK_full * dk
    val_dim = HV_full * dv
    x = np.ascontiguousarray(np.asarray(inputs["L0_attn_hidden_states"], dtype=np.float32))
    W_qkv = np.asarray(inputs["L0_attn_W_qkv"], dtype=np.float32)
    W_z = np.asarray(inputs["L0_attn_W_z"], dtype=np.float32)
    W_b = np.asarray(inputs["L0_attn_W_b"], dtype=np.float32)
    W_a = np.asarray(inputs["L0_attn_W_a"], dtype=np.float32)
    conv_w = np.asarray(inputs["L0_attn_conv_w"], dtype=np.float32)
    dt_bias = np.asarray(inputs["L0_attn_dt_bias"], dtype=np.float32)
    A_log = np.asarray(inputs["L0_attn_A_log"], dtype=np.float32)
    norm_w = np.asarray(inputs["L0_attn_norm_w"], dtype=np.float32)
    W_out = np.asarray(inputs["L0_attn_W_out"], dtype=np.float32)
    W_router = np.asarray(inputs["L0_ffn_W_router"], dtype=np.float32)
    W_sgate = np.asarray(inputs["L0_ffn_W_sgate"], dtype=np.float32)
    Ws_gu = np.asarray(inputs["L0_ffn_Ws_gate_up"], dtype=np.float32)
    Ws_dn = np.asarray(inputs["L0_ffn_Ws_down"], dtype=np.float32)
    We_gu = np.asarray(inputs["L0_ffn_We_gate_up"], dtype=np.float32)
    We_dn = np.asarray(inputs["L0_ffn_We_down"], dtype=np.float32)

    consts = make_consts(cfg)
    maps = []
    for c in range(cfg["n_cores"]):
        b, half = c // 2, c % 2
        if emode:
            qcols = slice(0, key_dim)
            kcols = slice(key_dim, 2 * key_dim)
            vcols = slice(2 * key_dim, 2 * key_dim + val_dim)
            vheads = slice(0, HV_full)
            warm = (x[b, 0:tb - tf] if half == 1
                    else np.zeros((tb - tf, d), np.float32))
            x_att = np.concatenate([warm, x[b, half * tf:(half + 1) * tf]], axis=0)
        else:
            qcols = slice(half * hk * dk, (half + 1) * hk * dk)
            kcols = slice(key_dim + half * hk * dk, key_dim + (half + 1) * hk * dk)
            vcols = slice(2 * key_dim + half * hv * dv, 2 * key_dim + (half + 1) * hv * dv)
            vheads = slice(half * hv, (half + 1) * hv)
            x_att = x[b]
        toks = slice(half * tf, (half + 1) * tf)
        wqkv_c = np.concatenate([W_qkv[:, qcols], W_qkv[:, kcols], W_qkv[:, vcols]], axis=1)
        convw_c = np.concatenate([conv_w[qcols.start:qcols.stop],
                                  conv_w[kcols.start:kcols.stop],
                                  conv_w[vcols.start:vcols.stop]], axis=0)
        m = {
            "x_nat": np.ascontiguousarray(x_att),
            "xF": np.ascontiguousarray(x[b, toks]),
            "xT_half": np.ascontiguousarray(x[b, toks].T),
            "wqkv": np.ascontiguousarray(wqkv_c),
            "convw": np.ascontiguousarray(convw_c),
            "wz": np.ascontiguousarray(W_z[:, vcols.start - 2 * key_dim:vcols.stop - 2 * key_dim]),
            "wb": np.ascontiguousarray(W_b[:, vheads]),
            "wa": np.ascontiguousarray(W_a[:, vheads]),
            "alog": np.ascontiguousarray(A_log[vheads][:, None]),
            "dtb": np.ascontiguousarray(dt_bias[vheads][:, None]),
            "normw": np.ascontiguousarray(norm_w[:, None]),
            "wout": np.ascontiguousarray(W_out[vcols.start - 2 * key_dim:vcols.stop - 2 * key_dim]),
            "wrt": np.ascontiguousarray(W_router.T),
            "wsg": np.ascontiguousarray(W_sgate.T),
            "wsgu": Ws_gu,
            "wsdn": Ws_dn,
            "wegu": We_gu,
            "wedn": We_dn,
        }
        m.update(consts)
        maps.append(m)
    return maps


def assemble_output(cfg, res):
    tb, tf, d = cfg["tb"], cfg["tf"], cfg["d"]
    nb = cfg["n_cores"] // 2
    out = np.zeros((nb, tb, d), dtype=np.float32)
    for c in range(cfg["n_cores"]):
        b, half = c // 2, c % 2
        outT = res.results[c]["outT"]
        out[b, half * tf:(half + 1) * tf, :] = np.asarray(outT).T
    return out



def ceil_div(a, b):
    return (a + b - 1) // b


def col_strips(total, width=512):
    out = []
    c = 0
    while c < total:
        w = min(width, total - c)
        out.append((c, w))
        c += w
    return out


def build_program(nc, cfg, part="full"):
    d, hv, hk = cfg["d"], cfg["hv"], cfg["hk"]
    dk, dv = cfg["dk"], cfg["dv"]
    tb, tf = cfg["tb"], cfg["tf"]
    i_sh, i_e, n_e = cfg["i_sh"], cfg["i_e"], cfg["n_e"]
    emode = bool(cfg.get("emode", False))
    DT = d // 128
    QT = 2 * hk + hv
    VT = hv
    NCH = tb // 128
    TH2 = tb // 2
    SH_T = i_sh // 128
    IE_T = i_e // 128
    HW = hv * 128
    HGC = cfg.get("hg") or min(hv, 16)
    HWGC = HGC * 128
    C = 128

    p = Prog(nc)

    declared_inputs = []

    def din(name, shape, dt=F32):
        declared_inputs.append(name)
        return nc.dram_tensor(name, list(shape), dt, kind="ExternalInput")

    A = part in ("full", "attn")
    F = part in ("full", "ffn")
    t_x = din("x_nat", (tb, d)) if A else None
    t_xF = din("xF", (tf, d)) if F else None
    t_xT = din("xT_half", (d, tf)) if F else None
    t_wqkv = din("wqkv", (d, QT * 128)) if A else None
    t_convw = din("convw", (QT * 128, 4)) if A else None
    t_wz = din("wz", (d, hv * dv)) if A else None
    t_wb = din("wb", (d, hv)) if A else None
    t_wa = din("wa", (d, hv)) if A else None
    t_alog = din("alog", (hv, 1)) if A else None
    t_dtb = din("dtb", (hv, 1)) if A else None
    t_normw = din("normw", (128, 1)) if A else None
    t_wout = din("wout", (hv * dv, d)) if A else None
    t_wrt = din("wrt", (d, 8)) if F else None
    t_wsg = din("wsg", (d, 1)) if F else None
    t_wsgu = din("wsgu", (d, 2 * i_sh)) if F else None
    t_wsdn = din("wsdn", (i_sh, d)) if F else None
    t_wegu = din("wegu", (n_e, d, 2 * i_e)) if F else None
    t_wedn = din("wedn", (n_e, i_e, d)) if F else None
    t_cidf = din("c_ident_f", (C, C))
    t_cidb = din("c_ident_b", (C, C))
    t_sel = din("c_sel16", (HGC, HWGC))
    t_colsel = din("c_colsel", (6, 384))
    t_mT_n = din("c_maskT_n", (C, C))
    t_mT_nt = din("c_maskT_nt", (C, C))
    t_mT_LT = din("c_maskT_LT", (C, C))
    t_mT_at = din("c_maskT_at", (C, C))
    t_ones = din("c_ones128", (C, 1))
    t_onesrow = din("c_onesrow", (1, C))
    t_oneshv = din("c_oneshv", (HGC, C))
    t_repid_b = din("c_repid_b", (C, HWGC))

    if part == "ffn":
        t_s0 = din("s_half0", (tf, d), BF16)
        t_s1 = din("s_half1", (tf, d), BF16)
    t_spart_out = None
    t_out = None
    if part == "attn":
        t_spart_out = nc.dram_tensor("spart_out", [tb, d], BF16, kind="ExternalOutput")
    else:
        t_out = nc.dram_tensor("outT", [d, tf], F32, kind="ExternalOutput")

    s_xn = nc.dram_tensor("xn_scr", [tb, d], BF16)
    s_xnf = nc.dram_tensor("xnf_scr", [tf, d], BF16)
    s_qkv = nc.dram_tensor("qkv_scr", [QT * 128, tb], BF16)
    s_zsil = nc.dram_tensor("zsil_scr", [hv * dv, tf if emode else tb], BF16)
    s_o3 = nc.dram_tensor("o3_scr", [tf if emode else tb, hv * dv], BF16)
    s_spart = nc.dram_tensor("spart_scr", [tb, d], BF16) if (part == "full" and not emode) else None
    if part == "attn":
        s_spart = t_spart_out
    s_sred = nc.dram_tensor("sred_scr", [tf, d], BF16)
    s_hsh = nc.dram_tensor("hsh_scr", [i_sh, tf], BF16)
    s_he = nc.dram_tensor("he_scr", [n_e, i_e, tf], BF16)
    s_shsc = nc.dram_tensor("shsc_scr", [d, tf], BF16)
    s_crow = nc.dram_tensor("crow_scr", [9, tf], F32)
    s_gates = nc.dram_tensor("gates_scr", [3, hv, tb], F32)
    s_hrow2 = nc.dram_tensor("hrow2_scr", [2, 3, 2, HGC, 128], BF16)

    b_xn = p.buf("xn_scr")
    b_xnf = p.buf("xnf_scr")
    b_qkv = p.buf("qkv_scr")
    b_zsil = p.buf("zsil_scr")
    b_o3scr = p.buf("o3_scr")
    b_spart = p.buf("spart_scr")
    b_sred = p.buf("sred_scr")
    b_hsh = p.buf("hsh_scr")
    b_he = p.buf("he_scr")
    b_shsc = p.buf("shsc_scr")
    b_crow = p.buf("crow_scr")
    b_gates = p.buf("gates_scr")
    b_hrw2D = [p.buf(f"hrow2_{i}") for i in range(2)]

    stack_global = ExitStack()

    def sb(name, shape, dt=F32):
        h = nc.alloc_sbuf_tensor(name, list(shape), dt)
        return h, p.buf(name)

    PSW = max(HWGC, 2048 if tb >= 4096 else max(TH2, tf, 512))
    PSW = max(PSW, 512)
    psA = nc.alloc_psum_tensor("psA", [128, PSW], F32)
    psB = nc.alloc_psum_tensor("psB", [128, PSW], F32)
    psA_r = [p.buf(f"psA{i}") for i in range(PSW // 512)]
    psB_r = [p.buf(f"psB{i}") for i in range(PSW // 512)]

    def regions(which, c0, w):
        rs = psA_r if which is psA else psB_r
        return [rs[i] for i in range(c0 // 512, ceil_div(c0 + w, 512))]

    h_const_id_f, b_cidf = sb("cidf", [C, C])
    h_const_id_b, b_cidb = sb("cidb", [C, C], BF16)
    h_selb, b_selb = sb("selb", [HGC, HWGC], BF16)
    h_colsel, b_colsel = sb("colselb", [6, 384], BF16)
    h_repid_b, b_repidb = sb("repidb", [C, HWGC], BF16)
    h_masksb, b_masksb = sb("masksb", [C, 4 * C], BF16)
    h_ones, b_ones = sb("ones", [C, 1])
    h_onesb, b_onesb = sb("onesb", [C, 1], BF16)
    h_onesr, b_onesr = sb("onesr", [1, C])
    h_oneshv, b_oneshv = sb("oneshv", [HGC, C])
    h_convw, b_convw = sb("convw_sb", [128, QT * 4])
    h_alog, b_alog = sb("alog_sb", [hv, 1])
    h_dtb, b_dtb = sb("dtb_sb", [hv, 1])
    h_negea, b_negea = sb("negea", [hv, 1])
    h_normw, b_normw = sb("normw_sb", [128, 1])

    h_S = nc.alloc_sbuf_tensor("S_state", [128, HW], BF16)
    b_S = [p.buf(f"S_state{g}") for g in range(max(1, hv // HGC))]
    h_bias, b_bias = sb("biasconst", [128, 2], F32)

    p.dma("sync", lambda e: e.dma_start(out=h_const_id_f[:, :], in_=t_cidf.ap()[:, :]), writes=[b_cidf])
    p.dma("gpsimd", lambda e: e.dma_start(out=h_const_id_b[:, :], in_=t_cidb.ap()[:, :]), writes=[b_cidb])
    p.dma("gpsimd", lambda e: e.dma_start(out=h_selb[:, :], in_=t_sel.ap()[:, :]), writes=[b_selb])
    p.dma("gpsimd", lambda e: e.dma_start(out=h_colsel[:, :], in_=t_colsel.ap()[:, :]), writes=[b_colsel])
    p.dma("gpsimd", lambda e: e.dma_start(out=h_repid_b[:, :], in_=t_repid_b.ap()[:, :]), writes=[b_repidb])
    for i, t in enumerate([t_mT_n, t_mT_nt, t_mT_LT, t_mT_at]):
        p.dma("gpsimd", lambda e, i=i, t=t: e.dma_start(out=h_masksb[:, i * C:(i + 1) * C], in_=t.ap()[:, :]), writes=[b_masksb])
    p.dma("sync", lambda e: e.dma_start(out=h_ones[:, :], in_=t_ones.ap()[:, :]), writes=[b_ones])
    p.dma("gpsimd", lambda e: e.dma_start(out=h_onesb[:, :], in_=t_ones.ap()[:, :]), writes=[b_onesb])
    p.dma("sync", lambda e: e.dma_start(out=h_onesr[:, :], in_=t_onesrow.ap()[:, :]), writes=[b_onesr])
    p.dma("sync", lambda e: e.dma_start(out=h_oneshv[:, :], in_=t_oneshv.ap()[:, :]), writes=[b_oneshv])
    if A:
        p.dma("sync", lambda e: e.dma_start(
            out=h_convw[:, :].rearrange("p (t k) -> p t k", k=4),
            in_=t_convw.ap().rearrange("(t p) k -> p t k", p=128)), writes=[b_convw])
        p.dma("sync", lambda e: e.dma_start(out=h_alog[:, :], in_=t_alog.ap()[:, :]), writes=[b_alog])
        p.dma("sync", lambda e: e.dma_start(out=h_dtb[:, :], in_=t_dtb.ap()[:, :]), writes=[b_dtb])
        p.dma("sync", lambda e: e.dma_start(out=h_normw[:, :], in_=t_normw.ap()[:, :]), writes=[b_normw])
        p.step("scalar", lambda e: e.activation(h_negea[:, :], h_alog[:, :], AF.Exp, bias=0.0, scale=1.0),
               reads=[b_alog], writes=[b_negea])
        p.step("vector", lambda e: e.tensor_scalar_mul(h_negea[:, :], h_negea[:, :], -1.0),
               reads=[b_negea], writes=[b_negea])
        p.step("vector", lambda e: e.memset(h_S[:, :], 0.0), writes=list(b_S))
    p.step("vector", lambda e: e.memset(h_bias[:, 0:1], 1.0), writes=[b_bias])
    p.step("vector", lambda e: e.memset(h_bias[:, 1:2], 1e-6), writes=[b_bias])

    with ExitStack() as ph:
        h_xc = [ph.enter_context(nc.sbuf_tensor(f"lnx{i}", [128, d], F32)) for i in range(2)]
        b_xc = [p.buf(f"lnx{i}") for i in range(2)]
        h_sq = ph.enter_context(nc.sbuf_tensor("lnsq", [128, d], F32))
        b_sq = p.buf("lnsq")
        h_st = ph.enter_context(nc.sbuf_tensor("lnst", [128, 8], F32))
        b_st = p.buf("lnst")
        h_xo = ph.enter_context(nc.sbuf_tensor("lnxo", [128, d], BF16))
        b_xo = p.buf("lnxo")

        def ln_pass(src, dst, b_dst, n_chunks):
            for i in range(n_chunks):
                bi = i % 2
                xap = src.ap()[i * 128:(i + 1) * 128, :]
                p.dma("sync", lambda e, xap=xap, bi=bi: e.dma_start(out=h_xc[bi][:, :], in_=xap),
                      writes=[b_xc[bi]])
                p.step("vector", lambda e, bi=bi: e.reduce_sum(h_st[:, 0:1], h_xc[bi][:, :], AX.X),
                       reads=[b_xc[bi]], writes=[b_st])
                p.step("scalar", lambda e, bi=bi: e.activation(h_sq[:, :], h_xc[bi][:, :], AF.Square, bias=0.0, scale=1.0),
                       reads=[b_xc[bi]], writes=[b_sq])
                p.step("vector", lambda e: e.reduce_sum(h_st[:, 1:2], h_sq[:, :], AX.X),
                       reads=[b_sq], writes=[b_st])
                p.step("vector", lambda e: e.tensor_scalar_mul(h_st[:, 2:3], h_st[:, 0:1], 1.0 / d),
                       reads=[b_st], writes=[b_st])
                p.step("vector", lambda e: e.tensor_mul(h_st[:, 3:4], h_st[:, 2:3], h_st[:, 2:3]),
                       reads=[b_st], writes=[b_st])
                p.step("vector", lambda e: e.scalar_tensor_tensor(h_st[:, 4:5], h_st[:, 1:2], 1.0 / d, h_st[:, 3:4], ALU.mult, ALU.subtract),
                       reads=[b_st], writes=[b_st])
                p.step("vector", lambda e: e.tensor_scalar_add(h_st[:, 4:5], h_st[:, 4:5], 1e-5),
                       reads=[b_st], writes=[b_st])
                p.step("vector", lambda e: e.reciprocal(h_st[:, 5:6], h_st[:, 4:5]),
                       reads=[b_st], writes=[b_st])
                p.step("scalar", lambda e: e.activation(h_st[:, 6:7], h_st[:, 5:6], AF.Sqrt, bias=0.0, scale=1.0),
                       reads=[b_st], writes=[b_st])
                p.step("vector", lambda e: e.scalar_tensor_tensor(h_st[:, 7:8], h_st[:, 2:3], -1.0, h_st[:, 6:7], ALU.mult, ALU.mult),
                       reads=[b_st], writes=[b_st])
                p.step("vector", lambda e, bi=bi: e.tensor_scalar(
                    h_xo[:, :], h_xc[bi][:, :], h_st[:, 6:7], h_st[:, 7:8], ALU.mult, ALU.add),
                    reads=[b_xc[bi], b_st], writes=[b_xo])
                p.dma("sync", lambda e, i=i, dst=dst: e.dma_start(out=dst.ap()[i * 128:(i + 1) * 128, :], in_=h_xo[:, :]),
                      reads=[b_xo], writes=[b_dst])

        if part in ("full", "attn"):
            ln_pass(t_x, s_xn, b_xn, tb // 128)
        if part in ("full", "ffn"):
            ln_pass(t_xF, s_xnf, b_xnf, tf // 128)
    p.barrier()

    if A:
        with ExitStack() as ph:
            h_xnT = ph.enter_context(nc.sbuf_tensor("xnTh", [128, DT * TH2], BF16))
            b_xnT = p.buf("xnTh")
            h_wsl = [ph.enter_context(nc.sbuf_tensor(f"wsl{i}", [128, DT * 128], BF16)) for i in range(2)]
            b_wsl = [p.buf(f"wsl{i}") for i in range(2)]
            h_mix = ph.enter_context(nc.sbuf_tensor("mix", [128, TH2 + 3], BF16))
            b_mix = p.buf("mix")
            h_cv = ph.enter_context(nc.sbuf_tensor("cvacc", [128, TH2], BF16))
            b_cv = p.buf("cvacc")
            h_row = ph.enter_context(nc.sbuf_tensor("rowout", [128, TH2], BF16))
            b_row = p.buf("rowout")
            h_hist = ph.enter_context(nc.sbuf_tensor("hist", [128, QT * 4], BF16))
            b_hist = p.buf("hist")
            h_wba = ph.enter_context(nc.sbuf_tensor("wba", [128, DT * 2 * hv], BF16))
            b_wba = p.buf("wba")
            h_sp = ph.enter_context(nc.sbuf_tensor("spbuf", [hv, 2 * TH2], F32))
            b_sp = p.buf("spbuf")
            h_gstg = ph.enter_context(nc.sbuf_tensor("gstg", [hv, TH2], F32))
            b_gstg = p.buf("gstg")

            p.step("vector", lambda e: e.memset(h_hist[:, :], 0.0), writes=[b_hist])
            p.dma("gpsimd", lambda e: e.dma_start(
                out=h_wba[:, 0:DT * hv].rearrange("p (k h) -> p k h", h=hv),
                in_=t_wb.ap().rearrange("(k p) h -> p k h", p=128)), writes=[b_wba])
            p.dma("gpsimd", lambda e: e.dma_start(
                out=h_wba[:, DT * hv:].rearrange("p (k h) -> p k h", h=hv),
                in_=t_wa.ap().rearrange("(k p) h -> p k h", p=128)), writes=[b_wba])

            for H in range(2):
                hsl = slice(H * TH2, (H + 1) * TH2)
                for K in range(DT):
                    p.dma("sync", lambda e, K=K, hsl=hsl: e.dma_start_transpose(
                        h_xnT[:, K * TH2:(K + 1) * TH2],
                        s_xn.ap()[hsl, K * 128:(K + 1) * 128]),
                        reads=[b_xn], writes=[b_xnT])
                n_tiles = QT + (VT if (not emode or H == 1) else 0)
                for ot in range(n_tiles):
                    is_z = ot >= QT
                    zt = ot - QT
                    zdst = (slice(hsl.start - TH2, hsl.stop - TH2)
                            if emode else hsl)
                    ps = psA if ot % 2 == 0 else psB
                    ps_r = regions(ps, 0, TH2)
                    wbuf = ot % 2
                    if not is_z:
                        wap = t_wqkv.ap()[:, ot * 128:(ot + 1) * 128]
                    else:
                        wap = t_wz.ap()[:, zt * 128:(zt + 1) * 128]
                    p.dma("gpsimd", lambda e, wap=wap, wbuf=wbuf: e.dma_start(
                        out=h_wsl[wbuf][:, :].rearrange("p (k c) -> p k c", c=128),
                        in_=wap.rearrange("(k p) c -> p k c", p=128)),
                        writes=[b_wsl[wbuf]])

                    def mmfn(e, ps=ps, wbuf=wbuf):
                        ins = None
                        for s0, w in col_strips(TH2):
                            for K in range(DT):
                                ins = e.matmul(ps[:, s0:s0 + w],
                                               h_wsl[wbuf][:, K * 128:(K + 1) * 128],
                                               h_xnT[:, K * TH2 + s0:K * TH2 + s0 + w],
                                               start=(K == 0), stop=(K == DT - 1))
                        return ins
                    p.step("tensor", mmfn, reads=[b_wsl[wbuf], b_xnT], writes=ps_r)

                    if is_z:
                        p.step("scalar", lambda e, ps=ps: e.activation(
                            h_row[:, :], ps[:, 0:TH2], AF.Silu, bias=0.0, scale=1.0),
                            reads=ps_r, writes=[b_row])
                        p.dma("sync", lambda e, zt=zt, zdst=zdst: e.dma_start(
                            out=s_zsil.ap()[zt * 128:(zt + 1) * 128, zdst], in_=h_row[:, :]),
                            reads=[b_row], writes=[b_zsil])
                    else:
                        p.step("vector", lambda e, ot=ot: e.tensor_copy(h_mix[:, 0:3], h_hist[:, ot * 4:ot * 4 + 3]),
                               reads=[b_hist], writes=[b_mix])
                        p.step("vector", lambda e, ps=ps: e.tensor_copy(h_mix[:, 3:3 + TH2], ps[:, 0:TH2]),
                               reads=ps_r, writes=[b_mix])
                        p.step("vector", lambda e, ot=ot: e.tensor_copy(h_hist[:, ot * 4:ot * 4 + 3], h_mix[:, TH2:TH2 + 3]),
                               reads=[b_mix], writes=[b_hist])
                        p.step("vector", lambda e, ot=ot: e.tensor_scalar_mul(
                            h_cv[:, :], h_mix[:, 0:TH2], h_convw[:, ot * 4:ot * 4 + 1]),
                            reads=[b_mix, b_convw], writes=[b_cv])
                        for tap in range(1, 4):
                            p.step("vector", lambda e, ot=ot, tap=tap: e.scalar_tensor_tensor(
                                h_cv[:, :], h_mix[:, tap:tap + TH2], h_convw[:, ot * 4 + tap:ot * 4 + tap + 1],
                                h_cv[:, :], ALU.mult, ALU.add),
                                reads=[b_mix, b_convw, b_cv], writes=[b_cv])
                        p.step("scalar", lambda e: e.activation(
                            h_row[:, :], h_cv[:, :], AF.Silu, bias=0.0, scale=1.0),
                            reads=[b_cv], writes=[b_row])
                        p.dma("sync", lambda e, ot=ot, hsl=hsl: e.dma_start(
                            out=s_qkv.ap()[ot * 128:(ot + 1) * 128, hsl], in_=h_row[:, :]),
                            reads=[b_row], writes=[b_qkv])

                for which in range(2):
                    ps = psA if which == 0 else psB
                    ps_r = regions(ps, 0, TH2)

                    def bafn(e, ps=ps, which=which):
                        ins = None
                        for s0, w in col_strips(TH2):
                            for K in range(DT):
                                ins = e.matmul(ps[0:hv, s0:s0 + w],
                                               h_wba[:, which * DT * hv + K * hv:which * DT * hv + (K + 1) * hv],
                                               h_xnT[:, K * TH2 + s0:K * TH2 + s0 + w],
                                               start=(K == 0), stop=(K == DT - 1))
                        return ins
                    p.step("tensor", bafn, reads=[b_wba, b_xnT], writes=ps_r)

                p.step("scalar", lambda e: e.activation(
                    h_gstg[:, :], psA[0:hv, 0:TH2], AF.Sigmoid, bias=0.0, scale=1.0),
                    reads=regions(psA, 0, TH2), writes=[b_gstg])
                p.dma("sync", lambda e, hsl=hsl: e.dma_start(
                    out=s_gates.ap()[2, :, hsl], in_=h_gstg[:, :]),
                    reads=[b_gstg], writes=[b_gates])
                p.step("scalar", lambda e: e.activation(
                    h_sp[:, 0:TH2], psA[0:hv, 0:TH2], AF.Abs, bias=0.0, scale=1.0),
                    reads=regions(psA, 0, TH2), writes=[b_sp])
                p.step("scalar", lambda e: e.activation(
                    h_sp[:, 0:TH2], h_sp[:, 0:TH2], AF.Exp, bias=0.0, scale=-1.0),
                    reads=[b_sp], writes=[b_sp])
                p.step("scalar", lambda e: e.activation(
                    h_sp[:, 0:TH2], h_sp[:, 0:TH2], AF.Ln, bias=h_bias[0:hv, 0:1], scale=1.0),
                    reads=[b_sp, b_bias], writes=[b_sp])
                p.step("vector", lambda e: e.tensor_scalar_min(
                    h_sp[:, TH2:2 * TH2], psA[0:hv, 0:TH2], 0.0),
                    reads=regions(psA, 0, TH2), writes=[b_sp])
                p.step("vector", lambda e: e.scalar_tensor_tensor(
                    h_gstg[:, :], h_sp[:, TH2:2 * TH2], 1.0, h_sp[:, 0:TH2], ALU.mult, ALU.subtract),
                    reads=[b_sp], writes=[b_gstg])
                p.dma("sync", lambda e, hsl=hsl: e.dma_start(
                    out=s_gates.ap()[1, :, hsl], in_=h_gstg[:, :]),
                    reads=[b_gstg], writes=[b_gates])
                p.step("scalar", lambda e: e.activation(
                    h_sp[:, 0:TH2], psB[0:hv, 0:TH2], AF.Abs, bias=h_dtb[:, 0:1], scale=1.0),
                    reads=regions(psB, 0, TH2) + [b_dtb], writes=[b_sp])
                p.step("scalar", lambda e: e.activation(
                    h_sp[:, 0:TH2], h_sp[:, 0:TH2], AF.Exp, bias=0.0, scale=-1.0),
                    reads=[b_sp], writes=[b_sp])
                p.step("scalar", lambda e: e.activation(
                    h_sp[:, 0:TH2], h_sp[:, 0:TH2], AF.Ln, bias=h_bias[0:hv, 0:1], scale=1.0),
                    reads=[b_sp, b_bias], writes=[b_sp])
                p.step("vector", lambda e: e.tensor_scalar(
                    h_sp[:, TH2:2 * TH2], psB[0:hv, 0:TH2], h_dtb[:, 0:1], 0.0, ALU.add, ALU.max),
                    reads=regions(psB, 0, TH2) + [b_dtb], writes=[b_sp])
                p.step("vector", lambda e: e.tensor_add(
                    h_sp[:, 0:TH2], h_sp[:, 0:TH2], h_sp[:, TH2:2 * TH2]),
                    reads=[b_sp], writes=[b_sp])
                p.step("vector", lambda e: e.tensor_scalar_mul(
                    h_gstg[:, :], h_sp[:, 0:TH2], h_negea[:, 0:1]),
                    reads=[b_sp, b_negea], writes=[b_gstg])
                p.dma("sync", lambda e, hsl=hsl: e.dma_start(
                    out=s_gates.ap()[0, :, hsl], in_=h_gstg[:, :]),
                    reads=[b_gstg], writes=[b_gates])
        p.barrier()

    if A:
        hg = HGC
        kg = hg // 2
        PG = hv // hg
        HWg = hg * 128
        NW = (tb - tf) // 128 if emode else 0
        GW = 128
        GC = GW // 128
        LN_HALF_DK = float(np.log(dk ** -0.5))

        with ExitStack() as ph:
            def psb(name, shape, dt=F32):
                h = ph.enter_context(nc.sbuf_tensor(name, list(shape), dt))
                return h, p.buf(name)

            def psb2(name, shape, dt=F32):
                hs, bs = [], []
                for i in range(2):
                    h, b = psb(f"{name}{i}", shape, dt)
                    hs.append(h)
                    bs.append(b)
                return hs, bs

            h_KKD, b_KKD = psb2("KKsb", [128, HWg], BF16)
            h_KQD, b_KQD = psb2("KQsb", [128, HWg], BF16)
            h_EtD, b_EtD = psb2("Etmp", [128, HWg], BF16)
            h_nD, b_nD = psb2("mat_n", [128, HWg], BF16)
            h_ntD, b_ntD = psb2("mat_nt", [128, HWg], BF16)
            h_LTD, b_LTD = psb2("mat_LT", [128, HWg], BF16)
            h_atD, b_atD = psb2("mat_at", [128, HWg], BF16)
            h_EbcD, b_EbcD = psb2("Ebc", [128, HWg], BF16)
            h_AqbD, b_AqbD = psb2("Aqb", [128, HWg], BF16)
            h_csbD, b_csbD = psb2("csb", [128, HWg], BF16)
            h_ACfbD, b_ACfbD = psb2("ACfb", [128, HWg], BF16)
            PWNAMES = ["nt2", "n2", "nt4", "n4", "nt8", "n8", "nt16", "n16"]
            PW = {nm: i for i, nm in enumerate(PWNAMES)}
            h_pwD, b_pwD = [], []
            for i in range(2):
                hl, bl = [], []
                for nm in PWNAMES:
                    hh, bb = psb(f"pw{i}_{nm}", [128, HWg], BF16)
                    hl.append(hh)
                    bl.append(bb)
                h_pwD.append(hl)
                b_pwD.append(bl)
            h_G0D, b_G0D = psb2("Gbuf0", [128, HWg], BF16)
            h_G1D, b_G1D = psb2("Gbuf1", [128, HWg], BF16)
            h_GtD, b_GtD = psb2("Gtmp", [128, HWg], BF16)
            h_DinvD, b_DinvD = psb2("Dinv", [128, HWg], BF16)
            h_RD, b_RD = psb2("Rstrip", [128, HWg], BF16)
            h_R2D, b_R2D = psb2("R2strip", [128, HWg], BF16)
            h_UD, b_UD = psb2("Ustrip", [128, HWg], BF16)
            h_bVD, b_bVD = psb2("bV", [128, HWg], BF16)
            h_oD, b_oD = psb2("ostrip", [128, HWg], F32)
            h_o3D, b_o3D = psb2("o3strip", [128, HWg], BF16)
            h_UscD, b_UscD = psb2("Usc", [128, HWg], BF16)
            h_SscD, b_SscD = psb2("Ssc", [128, HWg], BF16)
            h_bCfD, b_bCfD = psb2("bCf", [hg, 128], F32)
            h_bChD, b_bChD = psb2("bCh", [hg, 128], BF16)
            h_bClD, b_bClD = psb2("bCl", [hg, 128], BF16)
            h_ksqD, b_ksqD = psb2("ksq", [128, 2 * kg * 128], BF16)

            h_qkgD, b_qkgD = psb2("qkg", [128, 2 * hk * GW], BF16)
            h_vtD, b_vtD = psb2("vtok", [128, HWg], BF16)
            h_ktD, b_ktD = psb2("ktok", [128, kg * 128], BF16)
            h_ztD, b_ztD = psb2("ztok", [128, HWg], BF16)
            h_cmpAD, b_cmpAD = psb2("cmpA", [96, 128], F32)
            h_cTaD, b_cTaD = psb2("cTa", [128, 96], F32)
            h_cTkD, b_cTkD = psb2("cTk", [hg, 128], F32)
            h_cTqD, b_cTqD = psb2("cTq", [hg, 128], F32)
            h_hrowD, b_hrowD = psb2("hrow", [hg, 3 * 128], F32)
            h_hrBD, b_hrBD = psb2("hrB", [hg, 3 * 128], BF16)
            h_hrLD, b_hrLD = psb2("hrL", [hg, 3 * 128], BF16)
            h_colr6D, b_colr6D = psb2("colr6", [6, HWg], BF16)
            h_cexpD, b_cexpD = psb2("cexp", [128, 12 * hg], F32)
            h_gstg2D, b_gstg2D = psb2("gstg2", [hg, 256], F32)

            CE = {"lnks": 0, "lnqs": hg, "rstd": 9 * hg, "tmp": 10 * hg, "tmp2": 11 * hg}

            for i in range(2):
                p.step("vector", lambda e, i=i: e.memset(h_cmpAD[i][:, :], 0.0), writes=[b_cmpAD[i]])

            for n_pg in range(NCH * PG):
                n, pg = divmod(n_pg, PG)
                full = (not emode) or (n >= NW)
                par = n_pg % 2
                gpar = (n // GC) % 2
                gi = n % GC
                tok0 = gi * 128
                nsl = slice(n * 128, (n + 1) * 128)
                msl = slice((n - NW) * 128, (n - NW + 1) * 128)
                wf = 384 if full else 256
                kdep = 2 * (wf // 128)
                b_Sg = b_S[pg]
                Sg = h_S[:, pg * HWg:(pg + 1) * HWg]
                psX = psA if par == 0 else psB
                pslot = [0]
                nslots = max(1, PSW // HWg)

                def psoff():
                    off = (pslot[0] % nslots) * HWg
                    pslot[0] += 1
                    return off
                h_qkg, b_qkg = h_qkgD[gpar], b_qkgD[gpar]
                h_vt, b_vt = h_vtD[par], b_vtD[par]
                h_kt, b_kt = h_ktD[par], b_ktD[par]
                h_zt, b_zt = h_ztD[par], b_ztD[par]
                h_cmpA, b_cmpA = h_cmpAD[par], b_cmpAD[par]
                h_cTa, b_cTa = h_cTaD[par], b_cTaD[par]
                h_cTk, b_cTk = h_cTkD[par], b_cTkD[par]
                h_cTq, b_cTq = h_cTqD[par], b_cTqD[par]
                h_hrow, b_hrow = h_hrowD[par], b_hrowD[par]
                h_hrB, b_hrB = h_hrBD[par], b_hrBD[par]
                h_hrL, b_hrL = h_hrLD[par], b_hrLD[par]
                h_colr6, b_colr6 = h_colr6D[par], b_colr6D[par]
                h_cexp, b_cexp = h_cexpD[par], b_cexpD[par]
                h_gstg2, b_gstg2 = h_gstg2D[par], b_gstg2D[par]
                h_KK, b_KK = h_KKD[par], b_KKD[par]
                h_KQ, b_KQ = h_KQD[par], b_KQD[par]
                h_Et, b_Et = h_EtD[par], b_EtD[par]
                h_n, b_n = h_nD[par], b_nD[par]
                h_nt, b_nt = h_ntD[par], b_ntD[par]
                h_LT, b_LT = h_LTD[par], b_LTD[par]
                h_at, b_at = h_atD[par], b_atD[par]
                h_Ebc, b_Ebc = h_EbcD[par], b_EbcD[par]
                h_Aqb, b_Aqb = h_AqbD[par], b_AqbD[par]
                h_csb, b_csb = h_csbD[par], b_csbD[par]
                h_ACfb, b_ACfb = h_ACfbD[par], b_ACfbD[par]
                h_pw, b_pw = h_pwD[par], b_pwD[par]
                h_G0, b_G0 = h_G0D[par], b_G0D[par]
                h_G1, b_G1 = h_G1D[par], b_G1D[par]
                h_Gt, b_Gt = h_GtD[par], b_GtD[par]
                h_Dinv, b_Dinv = h_DinvD[par], b_DinvD[par]
                h_R, b_R = h_RD[par], b_RD[par]
                h_R2, b_R2 = h_R2D[par], b_R2D[par]
                h_U, b_U = h_UD[par], b_UD[par]
                h_bV, b_bV = h_bVD[par], b_bVD[par]
                h_o, b_o = h_oD[par], b_oD[par]
                h_o3, b_o3 = h_o3D[par], b_o3D[par]
                h_Usc, b_Usc = h_UscD[par], b_UscD[par]
                h_Ssc, b_Ssc = h_SscD[par], b_SscD[par]
                h_bCf, b_bCf = h_bCfD[par], b_bCfD[par]
                h_bCh, b_bCh = h_bChD[par], b_bChD[par]
                h_bCl, b_bCl = h_bClD[par], b_bClD[par]
                h_ksq, b_ksq = h_ksqD[par], b_ksqD[par]
                VT_BUF = {"n": (h_n, b_n), "nt": (h_nt, b_nt), "LT": (h_LT, b_LT), "at": (h_at, b_at)}
                b_hrw2 = b_hrw2D[par]

                def ce(nm, w=hg, _h=h_cexp):
                    c0 = CE[nm]
                    return _h[:, c0:c0 + w]

                def ce_col(nm, pp, _h=h_cexp):
                    c0 = CE[nm]
                    return _h[:, c0 + pp:c0 + pp + 1]

                qk3 = h_qkg[:, :].rearrange("p (t w) -> p t w", w=GW)

                def betaT_col(pp, _h=h_cTa):
                    return _h[:, 64 + pp:65 + pp]

                if gi == 0 and pg == 0:
                    g0 = n * 128
                    p.dma("sync", lambda e, g0=g0, qk3=qk3: e.dma_start(
                        out=qk3,
                        in_=s_qkv.ap()[0:2 * hk * 128, g0:g0 + GW].rearrange("(t p) w -> p t w", p=128)),
                        reads=[b_qkv], writes=[b_qkg])

                p.dma("sync", lambda e, nsl=nsl, pg=pg, h_vt=h_vt: e.dma_start_transpose(
                    h_vt[:, :], s_qkv.ap()[2 * hk * 128 + pg * HWg:2 * hk * 128 + (pg + 1) * HWg, nsl]),
                    reads=[b_qkv], writes=[b_vt])
                p.dma("sync", lambda e, nsl=nsl, pg=pg, h_kt=h_kt: e.dma_start_transpose(
                    h_kt[:, :], s_qkv.ap()[(hk + pg * kg) * 128:(hk + (pg + 1) * kg) * 128, nsl]),
                    reads=[b_qkv], writes=[b_kt])
                if full:
                    zcols = nsl if not emode else msl
                    p.dma("sync", lambda e, zcols=zcols, pg=pg, h_zt=h_zt: e.dma_start_transpose(
                        h_zt[:, :], s_zsil.ap()[pg * HWg:(pg + 1) * HWg, zcols]),
                        reads=[b_zsil], writes=[b_zt])

                def qtile(h, tok0=tok0, pg=pg, qk3=qk3):
                    return qk3[:, pg * kg + h, tok0:tok0 + 128]

                def ktile(h, tok0=tok0, pg=pg, qk3=qk3):
                    return qk3[:, hk + pg * kg + h, tok0:tok0 + 128]

                p.step("vector", lambda e, tok0=tok0, pg=pg, qk3=qk3, h_ksq=h_ksq: e.tensor_mul(
                    h_ksq[:, 0:kg * 128].rearrange("p (t c) -> p t c", c=128),
                    qk3[:, pg * kg:(pg + 1) * kg, tok0:tok0 + 128],
                    qk3[:, pg * kg:(pg + 1) * kg, tok0:tok0 + 128]),
                    reads=[b_qkg], writes=[b_ksq])
                p.step("vector", lambda e, tok0=tok0, pg=pg, qk3=qk3, h_ksq=h_ksq: e.tensor_mul(
                    h_ksq[:, kg * 128:2 * kg * 128].rearrange("p (t c) -> p t c", c=128),
                    qk3[:, hk + pg * kg:hk + (pg + 1) * kg, tok0:tok0 + 128],
                    qk3[:, hk + pg * kg:hk + (pg + 1) * kg, tok0:tok0 + 128]),
                    reads=[b_qkg], writes=[b_ksq])

                off0 = psoff()

                def ssqfn(e, h_ksq=h_ksq, psX=psX, off0=off0):
                    ins = None
                    for h in range(2 * kg):
                        ins = e.matmul(psX[:, off0 + h:off0 + h + 1], h_ksq[:, h * 128:(h + 1) * 128],
                                       h_onesb[:, :], start=True, stop=True)
                    return ins
                p.step("tensor", ssqfn, reads=[b_ksq, b_onesb], writes=regions(psX, off0, 2 * kg))
                p.step("scalar", lambda e, ce=ce, psX=psX, off0=off0: e.activation(
                    ce("tmp", 2 * kg), psX[:, off0:off0 + 2 * kg], AF.Ln, bias=h_bias[:, 1:2], scale=1.0),
                    reads=regions(psX, off0, 2 * kg) + [b_bias], writes=[b_cexp])
                lnq3 = h_cexp[:, CE["lnqs"]:CE["lnqs"] + hg].rearrange("p (a b) -> p a b", b=2)
                lnk3 = h_cexp[:, CE["lnks"]:CE["lnks"] + hg].rearrange("p (a b) -> p a b", b=2)
                for r in range(2):
                    p.step("vector", lambda e, r=r, lnq3=lnq3, h_cexp=h_cexp: e.tensor_scalar(
                        lnq3[:, :, r:r + 1],
                        h_cexp[:, CE["tmp"]:CE["tmp"] + kg, None],
                        -0.5, LN_HALF_DK, ALU.mult, ALU.add),
                        reads=[b_cexp], writes=[b_cexp])
                    p.step("vector", lambda e, r=r, lnk3=lnk3, h_cexp=h_cexp: e.tensor_scalar(
                        lnk3[:, :, r:r + 1],
                        h_cexp[:, CE["tmp"] + kg:CE["tmp"] + 2 * kg, None],
                        -0.5, 0.0, ALU.mult, ALU.add),
                        reads=[b_cexp], writes=[b_cexp])

                hsl_g = slice(pg * hg, (pg + 1) * hg)
                p.dma("sync", lambda e, nsl=nsl, hsl_g=hsl_g, h_gstg2=h_gstg2: e.dma_start(
                    out=h_gstg2[:, 0:128], in_=s_gates.ap()[0, hsl_g, nsl]),
                    reads=[b_gates], writes=[b_gstg2])
                p.dma("sync", lambda e, nsl=nsl, hsl_g=hsl_g, h_cmpA=h_cmpA: e.dma_start(
                    out=h_cmpA[32:32 + hg, :], in_=s_gates.ap()[1, hsl_g, nsl]),
                    reads=[b_gates], writes=[b_cmpA])
                p.dma("sync", lambda e, nsl=nsl, hsl_g=hsl_g, h_gstg2=h_gstg2: e.dma_start(
                    out=h_gstg2[:, 128:256], in_=s_gates.ap()[1, hsl_g, nsl]),
                    reads=[b_gates], writes=[b_gstg2])
                p.dma("sync", lambda e, nsl=nsl, hsl_g=hsl_g, h_cmpA=h_cmpA: e.dma_start(
                    out=h_cmpA[64:64 + hg, :], in_=s_gates.ap()[2, hsl_g, nsl]),
                    reads=[b_gates], writes=[b_cmpA])
                p.step("vector", lambda e, h_cmpA=h_cmpA, h_gstg2=h_gstg2: e.tensor_tensor_scan(
                    h_cmpA[0:hg, :], h_gstg2[:, 0:128], h_gstg2[:, 0:128], 0.0, ALU.add, ALU.bypass),
                    reads=[b_gstg2], writes=[b_cmpA])
                offT = psoff()
                p.step("tensor", lambda e, h_cmpA=h_cmpA, psX=psX, offT=offT: e.transpose(
                    psX[:, offT:offT + 96], h_cmpA[:, :], h_const_id_f[0:96, 0:96]),
                    reads=[b_cmpA, b_cidf], writes=regions(psX, offT, 96))
                p.step("vector", lambda e, h_cTa=h_cTa, psX=psX, offT=offT: e.tensor_copy(
                    h_cTa[:, :], psX[:, offT:offT + 96]),
                    reads=regions(psX, offT, 96), writes=[b_cTa])
                offT = psoff()
                p.step("tensor", lambda e, ce=ce, psX=psX, offT=offT: e.transpose(
                    psX[0:hg, offT:offT + 128], ce("lnks"), h_const_id_f[:, :]),
                    reads=[b_cexp, b_cidf], writes=regions(psX, offT, 128))
                p.step("vector", lambda e, h_cTk=h_cTk, psX=psX, offT=offT: e.tensor_copy(
                    h_cTk[:, :], psX[0:hg, offT:offT + 128]),
                    reads=regions(psX, offT, 128), writes=[b_cTk])
                if full:
                    offT = psoff()
                    p.step("tensor", lambda e, ce=ce, psX=psX, offT=offT: e.transpose(
                        psX[0:hg, offT:offT + 128], ce("lnqs"), h_const_id_f[:, :]),
                        reads=[b_cexp, b_cidf], writes=regions(psX, offT, 128))
                    p.step("vector", lambda e, h_cTq=h_cTq, psX=psX, offT=offT: e.tensor_copy(
                        h_cTq[:, :], psX[0:hg, offT:offT + 128]),
                        reads=regions(psX, offT, 128), writes=[b_cTq])

                p.step("vector", lambda e, h_hrow=h_hrow, h_cmpA=h_cmpA, h_gstg2=h_gstg2: e.tensor_add(
                    h_hrow[:, 0:128], h_cmpA[0:hg, :], h_gstg2[:, 128:256]),
                    reads=[b_cmpA, b_gstg2], writes=[b_hrow])
                p.step("vector", lambda e, h_hrow=h_hrow, h_cTk=h_cTk: e.tensor_add(
                    h_hrow[:, 0:128], h_hrow[:, 0:128], h_cTk[:, :]),
                    reads=[b_hrow, b_cTk], writes=[b_hrow])
                p.step("vector", lambda e, h_hrow=h_hrow, h_cmpA=h_cmpA, h_cTk=h_cTk: e.scalar_tensor_tensor(
                    h_hrow[:, 128:256], h_cmpA[0:hg, :], -1.0, h_cTk[:, :], ALU.mult, ALU.add),
                    reads=[b_cmpA, b_cTk], writes=[b_hrow])
                if full:
                    p.step("vector", lambda e, h_hrow=h_hrow, h_cmpA=h_cmpA, h_cTq=h_cTq: e.tensor_add(
                        h_hrow[:, 256:384], h_cmpA[0:hg, :], h_cTq[:, :]),
                        reads=[b_cmpA, b_cTq], writes=[b_hrow])

                p.step("vector", lambda e, h_hrB=h_hrB, h_hrow=h_hrow, wf=wf: e.tensor_copy(
                    h_hrB[:, 0:wf], h_hrow[:, 0:wf]),
                    reads=[b_hrow], writes=[b_hrB])
                p.step("vector", lambda e, h_hrL=h_hrL, h_hrow=h_hrow, h_hrB=h_hrB, wf=wf: e.scalar_tensor_tensor(
                    h_hrL[:, 0:wf], h_hrow[:, 0:wf], 1.0, h_hrB[:, 0:wf], ALU.mult, ALU.subtract),
                    reads=[b_hrow, b_hrB], writes=[b_hrL])
                p.step("vector", lambda e, h_cmpA=h_cmpA, h_bCf=h_bCf: e.tensor_scalar_mul(
                    h_bCf[:, :], h_oneshv[:, :], h_cmpA[0:hg, 127:128]),
                    reads=[b_oneshv, b_cmpA], writes=[b_bCf])
                p.step("vector", lambda e, h_bCh=h_bCh, h_bCf=h_bCf: e.tensor_copy(h_bCh[:, :], h_bCf[:, :]),
                       reads=[b_bCf], writes=[b_bCh])
                p.step("vector", lambda e, h_bCl=h_bCl, h_bCf=h_bCf, h_bCh=h_bCh: e.scalar_tensor_tensor(
                    h_bCl[:, :], h_bCf[:, :], 1.0, h_bCh[:, :], ALU.mult, ALU.subtract),
                    reads=[b_bCf, b_bCh], writes=[b_bCl])

                for s in range(wf // 128):
                    p.dma("sync", lambda e, par=par, s=s, h_hrB=h_hrB: e.dma_start(
                        out=s_hrow2.ap()[par, s, 0], in_=h_hrB[:, s * 128:(s + 1) * 128]),
                        reads=[b_hrB], writes=[b_hrw2])
                    p.dma("sync", lambda e, par=par, s=s, h_hrL=h_hrL: e.dma_start(
                        out=s_hrow2.ap()[par, s, 1], in_=h_hrL[:, s * 128:(s + 1) * 128]),
                        reads=[b_hrL], writes=[b_hrw2])
                p.dma("sync", lambda e, par=par, h_colr6=h_colr6, wf=wf, kdep=kdep: e.dma_start(
                    out=h_colr6[0:kdep, :],
                    in_=s_hrow2.ap()[par, 0:wf // 128].rearrange("s r h c -> (s r) (h c)")),
                    reads=[b_hrw2], writes=[b_colr6])

                offK = psoff()

                def kkfn(e, ktile=ktile, psX=psX, offK=offK):
                    ins = None
                    for pp in range(hg):
                        ins = e.matmul(psX[:, offK + pp * 128:offK + (pp + 1) * 128],
                                       ktile(pp // 2), ktile(pp // 2), start=True, stop=True)
                    return ins
                p.step("tensor", kkfn, reads=[b_qkg], writes=regions(psX, offK, HWg))
                p.step("scalar", lambda e, h_KK=h_KK, psX=psX, offK=offK: e.activation(
                    h_KK[:, :], psX[:, offK:offK + HWg], AF.Copy, bias=0.0, scale=1.0),
                    reads=regions(psX, offK, HWg), writes=[b_KK])

                if full:
                    offQ = psoff()

                    def kqfn(e, ktile=ktile, qtile=qtile, psX=psX, offQ=offQ):
                        ins = None
                        for pp in range(hg):
                            ins = e.matmul(psX[:, offQ + pp * 128:offQ + (pp + 1) * 128],
                                           ktile(pp // 2), qtile(pp // 2), start=True, stop=True)
                        return ins
                    p.step("tensor", kqfn, reads=[b_qkg], writes=regions(psX, offQ, HWg))
                    p.step("scalar", lambda e, h_KQ=h_KQ, psX=psX, offQ=offQ: e.activation(
                        h_KQ[:, :], psX[:, offQ:offQ + HWg], AF.Copy, bias=0.0, scale=1.0),
                        reads=regions(psX, offQ, HWg), writes=[b_KQ])

                def rowbc(e, ps, off, blks, useC, h_hrB=h_hrB, h_hrL=h_hrL, h_bCh=h_bCh, h_bCl=h_bCl):
                    ins = None
                    for s0, w in col_strips(HWg):
                        ops = []
                        for roff in blks:
                            ops.append(h_hrB[:, roff:roff + 128])
                            ops.append(h_hrL[:, roff:roff + 128])
                        if useC:
                            ops.append(h_bCh[:, :])
                            ops.append(h_bCl[:, :])
                        for i, lt in enumerate(ops):
                            ins = e.matmul(ps[:, off + s0:off + s0 + w], lt, h_selb[:, s0:s0 + w],
                                           start=(i == 0), stop=(i == len(ops) - 1))
                    return ins

                for dst_h, dst_b, blks, useC, cond in [
                    (h_Ebc, b_Ebc, [0], False, True),
                    (h_csb, b_csb, [128], True, True),
                    (h_ACfb, b_ACfb, [], True, True),
                    (h_Aqb, b_Aqb, [256], False, full),
                ]:
                    if not cond:
                        continue
                    offE = psoff()
                    p.step("tensor", lambda e, rowbc=rowbc, psX=psX, offE=offE, blks=blks, useC=useC:
                           rowbc(e, psX, offE, blks, useC),
                           reads=[b_hrB, b_hrL, b_bCh, b_bCl, b_selb], writes=regions(psX, offE, HWg))
                    p.step("scalar", lambda e, dst_h=dst_h, psX=psX, offE=offE: e.activation(
                        dst_h[:, :], psX[:, offE:offE + HWg], AF.Exp, bias=0.0, scale=1.0),
                        reads=regions(psX, offE, HWg), writes=[dst_b])

                variants = [
                    (0, 1, 0, -1.0, "n", h_KK, b_KK),
                    (128, 0, 1, -1.0, "nt", h_KK, b_KK),
                    (128, 0, 2, 1.0, "LT", h_KK, b_KK),
                    (128, 2, 3, 1.0, "at", h_KQ, b_KQ),
                ]
                for roff, sblk, mi, sgn, vname, h_kk, b_kk in variants:
                    if (not full) and vname == "at":
                        continue
                    h_tgt, b_tgt = VT_BUF[vname]
                    offV = psoff()

                    def dfn(e, roff=roff, sblk=sblk, mi=mi, psX=psX, offV=offV, kdep=kdep,
                            h_hrB=h_hrB, h_hrL=h_hrL, h_colr6=h_colr6):
                        ins = None
                        for s0, w in col_strips(HWg):
                            ins = e.matmul(psX[:, offV + s0:offV + s0 + w], h_hrB[:, roff:roff + 128],
                                           h_selb[:, s0:s0 + w], start=True, stop=False)
                            ins = e.matmul(psX[:, offV + s0:offV + s0 + w], h_hrL[:, roff:roff + 128],
                                           h_selb[:, s0:s0 + w], start=False, stop=False)
                            ins = e.matmul(psX[:, offV + s0:offV + s0 + w], h_colsel[0:kdep, sblk * 128:(sblk + 1) * 128],
                                           h_colr6[0:kdep, s0:s0 + w], start=False, stop=False)
                            ins = e.matmul(psX[:, offV + s0:offV + s0 + w], h_masksb[:, mi * 128:(mi + 1) * 128],
                                           h_repid_b[:, s0:s0 + w], start=False, stop=True)
                        return ins
                    p.step("tensor", dfn,
                           reads=[b_hrB, b_hrL, b_selb, b_colsel, b_colr6, b_masksb, b_repidb],
                           writes=regions(psX, offV, HWg))
                    p.step("scalar", lambda e, h_Et=h_Et, psX=psX, offV=offV: e.activation(
                        h_Et[:, :], psX[:, offV:offV + HWg], AF.Exp, bias=0.0, scale=1.0),
                        reads=regions(psX, offV, HWg), writes=[b_Et])
                    p.step("vector", lambda e, h_tgt=h_tgt, h_kk=h_kk, sgn=sgn, h_Et=h_Et: e.scalar_tensor_tensor(
                        h_tgt[:, :], h_Et[:, :], sgn, h_kk[:, :], ALU.mult, ALU.mult),
                        reads=[b_Et, b_kk], writes=[b_tgt])

                sq_plan = [
                    ("nt2", h_n, b_n, h_nt, b_nt),
                    ("n2", h_nt, b_nt, h_n, b_n),
                    ("nt4", h_pw[PW["n2"]], b_pw[PW["n2"]], h_pw[PW["nt2"]], b_pw[PW["nt2"]]),
                    ("n4", h_pw[PW["nt2"]], b_pw[PW["nt2"]], h_pw[PW["n2"]], b_pw[PW["n2"]]),
                    ("nt8", h_pw[PW["n4"]], b_pw[PW["n4"]], h_pw[PW["nt4"]], b_pw[PW["nt4"]]),
                    ("n8", h_pw[PW["nt4"]], b_pw[PW["nt4"]], h_pw[PW["n4"]], b_pw[PW["n4"]]),
                    ("nt16", h_pw[PW["n8"]], b_pw[PW["n8"]], h_pw[PW["nt8"]], b_pw[PW["nt8"]]),
                    ("n16", h_pw[PW["nt8"]], b_pw[PW["nt8"]], h_pw[PW["n8"]], b_pw[PW["n8"]]),
                ]
                for nm, hl, bl, hr, br in sq_plan:
                    offS = psoff()

                    def sqfn(e, hl=hl, hr=hr, psX=psX, offS=offS):
                        ins = None
                        for pp in range(hg):
                            sl2 = slice(pp * 128, (pp + 1) * 128)
                            ins = e.matmul(psX[:, offS + pp * 128:offS + (pp + 1) * 128],
                                           hl[:, sl2], hr[:, sl2], start=True, stop=True)
                        return ins
                    p.step("tensor", sqfn, reads=[bl, br], writes=regions(psX, offS, HWg))
                    p.step("scalar", lambda e, nm=nm, psX=psX, offS=offS, h_pw=h_pw: e.activation(
                        h_pw[PW[nm]][:, :], psX[:, offS:offS + HWg], AF.Copy, bias=0.0, scale=1.0),
                        reads=regions(psX, offS, HWg), writes=[b_pw[PW[nm]]])

                offG = psoff()

                def g1mm(e, h_nt=h_nt, h_pw=h_pw, psX=psX, offG=offG):
                    ins = None
                    for pp in range(hg):
                        sl2 = slice(pp * 128, (pp + 1) * 128)
                        ins = e.matmul(psX[:, offG + pp * 128:offG + (pp + 1) * 128],
                                       h_pw[PW["n2"]][:, sl2], h_nt[:, sl2], start=True, stop=True)
                    return ins
                p.step("tensor", g1mm, reads=[b_pw[PW["n2"]], b_nt], writes=regions(psX, offG, HWg))
                p.step("vector", lambda e, h_Gt=h_Gt, h_pw=h_pw, psX=psX, offG=offG: e.scalar_tensor_tensor(
                    h_Gt[:, :], psX[:, offG:offG + HWg], 1.0, h_pw[PW["nt2"]][:, :], ALU.mult, ALU.add),
                    reads=regions(psX, offG, HWg) + [b_pw[PW["nt2"]]], writes=[b_Gt])
                p.step("gpsimd", lambda e, h_nt=h_nt, h_G0=h_G0, h_Gt=h_Gt: e.tensor_add(
                    h_G0[:, :], h_Gt[:, :], h_nt[:, :]),
                    reads=[b_Gt, b_nt], writes=[b_G0])
                gsrc, gsrcb = h_G0, b_G0
                gdst, gdstb = h_G1, b_G1
                for lv in ["4", "8", "16"]:
                    nlv, ntlv = PW["n" + lv], PW["nt" + lv]
                    offG = psoff()

                    def glmm(e, nlv=nlv, gsrc=gsrc, h_pw=h_pw, psX=psX, offG=offG):
                        ins = None
                        for pp in range(hg):
                            sl2 = slice(pp * 128, (pp + 1) * 128)
                            ins = e.matmul(psX[:, offG + pp * 128:offG + (pp + 1) * 128],
                                           h_pw[nlv][:, sl2], gsrc[:, sl2], start=True, stop=True)
                        return ins
                    p.step("tensor", glmm, reads=[b_pw[nlv], gsrcb], writes=regions(psX, offG, HWg))
                    p.step("vector", lambda e, ntlv=ntlv, h_Gt=h_Gt, h_pw=h_pw, psX=psX, offG=offG: e.scalar_tensor_tensor(
                        h_Gt[:, :], psX[:, offG:offG + HWg], 1.0, h_pw[ntlv][:, :], ALU.mult, ALU.add),
                        reads=regions(psX, offG, HWg) + [b_pw[ntlv]], writes=[b_Gt])
                    p.step("gpsimd", lambda e, gdst=gdst, gsrc=gsrc, h_Gt=h_Gt: e.tensor_add(
                        gdst[:, :], h_Gt[:, :], gsrc[:, :]),
                        reads=[b_Gt, gsrcb], writes=[gdstb])
                    gsrc, gsrcb, gdst, gdstb = gdst, gdstb, gsrc, gsrcb
                p.step("gpsimd", lambda e, gsrc=gsrc, h_Dinv=h_Dinv: e.tensor_add(
                    h_Dinv[:, :], gsrc[:, :], h_repid_b[:, :]),
                    reads=[gsrcb, b_repidb], writes=[b_Dinv])

                offR = psoff()

                def ks0fn(e, ktile=ktile, Sg=Sg, psX=psX, offR=offR):
                    ins = None
                    for pp in range(hg):
                        ins = e.matmul(psX[:, offR + pp * 128:offR + (pp + 1) * 128], ktile(pp // 2),
                                       Sg[:, pp * 128:(pp + 1) * 128], start=True, stop=True)
                    return ins
                p.step("tensor", ks0fn, reads=[b_qkg, b_Sg], writes=regions(psX, offR, HWg))
                for pp in range(hg):
                    p.step("scalar", lambda e, pp=pp, h_vt=h_vt, h_bV=h_bV, betaT_col=betaT_col: e.activation(
                        h_bV[:, pp * 128:(pp + 1) * 128], h_vt[:, pp * 128:(pp + 1) * 128],
                        AF.Copy, bias=0.0, scale=betaT_col(pp)),
                        reads=[b_vt, b_cTa], writes=[b_bV])
                p.step("vector", lambda e, h_R2=h_R2, h_Ebc=h_Ebc, psX=psX, offR=offR: e.tensor_mul(
                    h_R2[:, :], h_Ebc[:, :], psX[:, offR:offR + HWg]),
                    reads=[b_Ebc] + regions(psX, offR, HWg), writes=[b_R2])
                p.step("vector", lambda e, h_R=h_R, h_R2=h_R2, h_bV=h_bV: e.scalar_tensor_tensor(
                    h_R[:, :], h_R2[:, :], -1.0, h_bV[:, :], ALU.mult, ALU.add),
                    reads=[b_R2, b_bV], writes=[b_R])

                def dinvfn(e, src, psX, off, h_Dinv=h_Dinv):
                    ins = None
                    for pp in range(hg):
                        sl2 = slice(pp * 128, (pp + 1) * 128)
                        ins = e.matmul(psX[:, off + pp * 128:off + (pp + 1) * 128],
                                       h_Dinv[:, sl2], src[:, sl2], start=True, stop=True)
                    return ins
                offU = psoff()
                p.step("tensor", lambda e, dinvfn=dinvfn, h_R=h_R, psX=psX, offU=offU: dinvfn(e, h_R, psX, offU),
                       reads=[b_Dinv, b_R], writes=regions(psX, offU, HWg))
                p.step("scalar", lambda e, h_U=h_U, psX=psX, offU=offU: e.activation(
                    h_U[:, :], psX[:, offU:offU + HWg], AF.Copy, bias=0.0, scale=1.0),
                    reads=regions(psX, offU, HWg), writes=[b_U])
                for it in range(3):
                    offL = psoff()

                    def ltfn(e, h_LT=h_LT, h_U=h_U, psX=psX, offL=offL):
                        ins = None
                        for pp in range(hg):
                            sl2 = slice(pp * 128, (pp + 1) * 128)
                            ins = e.matmul(psX[:, offL + pp * 128:offL + (pp + 1) * 128],
                                           h_LT[:, sl2], h_U[:, sl2], start=True, stop=True)
                        return ins
                    p.step("tensor", ltfn, reads=[b_LT, b_U], writes=regions(psX, offL, HWg))
                    p.step("vector", lambda e, h_R2=h_R2, h_R=h_R, psX=psX, offL=offL: e.scalar_tensor_tensor(
                        h_R2[:, :], psX[:, offL:offL + HWg], -1.0, h_R[:, :], ALU.mult, ALU.add),
                        reads=regions(psX, offL, HWg) + [b_R], writes=[b_R2])
                    offU = psoff()
                    p.step("tensor", lambda e, dinvfn=dinvfn, h_R2=h_R2, psX=psX, offU=offU: dinvfn(e, h_R2, psX, offU),
                           reads=[b_Dinv, b_R2], writes=regions(psX, offU, HWg))
                    p.step("scalar", lambda e, h_U=h_U, psX=psX, offU=offU: e.activation(
                        h_U[:, :], psX[:, offU:offU + HWg], AF.Copy, bias=0.0, scale=1.0),
                        reads=regions(psX, offU, HWg), writes=[b_U])

                if full:
                    offO = psoff()
                    if nslots >= 2:
                        psAt, offA = psX, psoff()
                    else:
                        psAt, offA = (psB if par == 0 else psA), 0

                    def qs0fn(e, qtile=qtile, Sg=Sg, psX=psX, offO=offO):
                        ins = None
                        for pp in range(hg):
                            ins = e.matmul(psX[:, offO + pp * 128:offO + (pp + 1) * 128], qtile(pp // 2),
                                           Sg[:, pp * 128:(pp + 1) * 128], start=True, stop=True)
                        return ins
                    p.step("tensor", qs0fn, reads=[b_qkg, b_Sg], writes=regions(psX, offO, HWg))

                    def attufn(e, h_at=h_at, h_U=h_U, psAt=psAt, offA=offA):
                        ins = None
                        for pp in range(hg):
                            sl2 = slice(pp * 128, (pp + 1) * 128)
                            ins = e.matmul(psAt[:, offA + pp * 128:offA + (pp + 1) * 128],
                                           h_at[:, sl2], h_U[:, sl2], start=True, stop=True)
                        return ins
                    p.step("tensor", attufn, reads=[b_at, b_U], writes=regions(psAt, offA, HWg))
                    p.step("vector", lambda e, h_o=h_o, h_Aqb=h_Aqb, psX=psX, offO=offO: e.tensor_mul(
                        h_o[:, :], h_Aqb[:, :], psX[:, offO:offO + HWg]),
                        reads=[b_Aqb] + regions(psX, offO, HWg), writes=[b_o])
                    p.step("vector", lambda e, h_o=h_o, psAt=psAt, offA=offA: e.tensor_add(
                        h_o[:, :], h_o[:, :], psAt[:, offA:offA + HWg]),
                        reads=[b_o] + regions(psAt, offA, HWg), writes=[b_o])

                    p.step("scalar", lambda e, h_bV=h_bV, h_o=h_o: e.activation(
                        h_bV[:, :], h_o[:, :], AF.Square, bias=0.0, scale=1.0),
                        reads=[b_o], writes=[b_bV])
                    p.step("vector", lambda e, ce=ce, h_bV=h_bV: e.tensor_reduce(
                        ce("tmp"), h_bV[:, :].rearrange("p (t c) -> p t c", c=128), AX.X, ALU.add),
                        reads=[b_bV], writes=[b_cexp])
                    p.step("vector", lambda e, ce=ce: e.tensor_scalar(
                        ce("tmp"), ce("tmp"), 1.0 / dv, 1e-6, ALU.mult, ALU.add),
                        reads=[b_cexp], writes=[b_cexp])
                    p.step("vector", lambda e, ce=ce: e.reciprocal(ce("tmp2"), ce("tmp")),
                           reads=[b_cexp], writes=[b_cexp])
                    p.step("scalar", lambda e, ce=ce: e.activation(ce("rstd"), ce("tmp2"), AF.Sqrt, bias=0.0, scale=1.0),
                           reads=[b_cexp], writes=[b_cexp])
                    for pp in range(hg):
                        p.step("scalar", lambda e, pp=pp, ce_col=ce_col, h_o3=h_o3, h_o=h_o: e.activation(
                            h_o3[:, pp * 128:(pp + 1) * 128], h_o[:, pp * 128:(pp + 1) * 128],
                            AF.Copy, bias=0.0, scale=ce_col("rstd", pp)),
                            reads=[b_o, b_cexp], writes=[b_o3])
                    p.step("vector", lambda e, h_zt=h_zt, h_o3=h_o3: e.tensor_mul(
                        h_o3[:, :], h_o3[:, :], h_zt[:, :]),
                        reads=[b_o3, b_zt], writes=[b_o3])
                    osl = msl if emode else nsl
                    p.dma("sync", lambda e, osl=osl, pg=pg, h_o3=h_o3: e.dma_start(
                        out=s_o3.ap()[osl, pg * HWg:(pg + 1) * HWg], in_=h_o3[:, :]),
                        reads=[b_o3], writes=[b_o3scr])

                p.step("vector", lambda e, h_Usc=h_Usc, h_U=h_U, h_csb=h_csb: e.tensor_mul(
                    h_Usc[:, :], h_U[:, :], h_csb[:, :]),
                    reads=[b_U, b_csb], writes=[b_Usc])
                p.step("vector", lambda e, Sg=Sg, h_Ssc=h_Ssc, h_ACfb=h_ACfb: e.tensor_mul(
                    h_Ssc[:, :], Sg[:, :], h_ACfb[:, :]),
                    reads=[b_Sg, b_ACfb], writes=[b_Ssc])
                offZ = psoff()

                def statefn(e, h_kt=h_kt, h_Usc=h_Usc, psX=psX, offZ=offZ):
                    ins = None
                    for pp in range(hg):
                        ins = e.matmul(psX[:, offZ + pp * 128:offZ + (pp + 1) * 128],
                                       h_kt[:, (pp // 2) * 128:(pp // 2 + 1) * 128],
                                       h_Usc[:, pp * 128:(pp + 1) * 128], start=True, stop=True)
                    return ins
                p.step("tensor", statefn, reads=[b_kt, b_Usc], writes=regions(psX, offZ, HWg))
                p.step("vector", lambda e, Sg=Sg, h_Ssc=h_Ssc, psX=psX, offZ=offZ: e.scalar_tensor_tensor(
                    Sg[:, :], psX[:, offZ:offZ + HWg], 1.0, h_Ssc[:, :], ALU.mult, ALU.add),
                    reads=regions(psX, offZ, HWg) + [b_Ssc], writes=[b_Sg])
        p.barrier()

    if A:
        with ExitStack() as ph:
            h_wo = ph.enter_context(nc.sbuf_tensor("wo", [128, VT * d], BF16))
            b_wo = p.buf("wo")
            h_o3T = ph.enter_context(nc.sbuf_tensor("o3T", [128, VT * 512], BF16))
            b_o3T = p.buf("o3T")
            h_sev = ph.enter_context(nc.sbuf_tensor("sev", [128, 512], BF16))
            b_sev = p.buf("sev")

            for vt in range(VT):
                p.dma("gpsimd", lambda e, vt=vt: e.dma_start(
                    out=h_wo[:, vt * d:(vt + 1) * d],
                    in_=t_wout.ap()[vt * 128:(vt + 1) * 128, :]), writes=[b_wo])
                p.step("vector", lambda e, vt=vt: e.tensor_scalar_mul(
                    h_wo[:, vt * d:(vt + 1) * d], h_wo[:, vt * d:(vt + 1) * d], h_normw[:, 0:1]),
                    reads=[b_wo, b_normw], writes=[b_wo])

            TWOUT = tf if emode else tb
            for t0, tw in col_strips(TWOUT):
                for vt in range(VT):
                    p.dma("sync", lambda e, t0=t0, tw=tw, vt=vt: e.dma_start_transpose(
                        h_o3T[:, vt * 512:vt * 512 + tw],
                        s_o3.ap()[t0:t0 + tw, vt * 128:(vt + 1) * 128]),
                        reads=[b_o3scr], writes=[b_o3T])
                for tt in range(tw // 128):
                    for oc0, ow in col_strips(d):
                        ps = psA if (tt + oc0 // 512) % 2 == 0 else psB

                        def smm(e, ps=ps, tt=tt, oc0=oc0, ow=ow):
                            ins = None
                            for vt in range(VT):
                                ins = e.matmul(ps[:, 0:ow],
                                               h_o3T[:, vt * 512 + tt * 128:vt * 512 + (tt + 1) * 128],
                                               h_wo[:, vt * d + oc0:vt * d + oc0 + ow],
                                               start=(vt == 0), stop=(vt == VT - 1))
                            return ins
                        p.step("tensor", smm, reads=[b_o3T, b_wo], writes=regions(ps, 0, ow))
                        p.step("vector", lambda e, ps=ps, ow=ow: e.tensor_copy(h_sev[:, 0:ow], ps[:, 0:ow]),
                               reads=regions(ps, 0, ow), writes=[b_sev])
                        wdst = s_sred if emode else s_spart
                        wdstb = b_sred if emode else b_spart
                        p.dma("sync", lambda e, t0=t0, tt=tt, oc0=oc0, ow=ow, wdst=wdst: e.dma_start(
                            out=wdst.ap()[t0 + tt * 128:t0 + (tt + 1) * 128, oc0:oc0 + ow],
                            in_=h_sev[:, 0:ow]),
                            reads=[b_sev], writes=[wdstb])

        if part == "full" and not emode:
            p.step("gpsimd", lambda e: e.collective_compute(
                "ReduceScatter", ALU.add, replica_groups=cfg["pair_groups"],
                ins=[s_spart.ap().opt()], outs=[s_sred.ap().opt()]),
                reads=[b_spart], writes=[b_sred], semkey="cc", inc=16)
        p.barrier()

    if part == "ffn":
        with ExitStack() as ph:
            h_sa = ph.enter_context(nc.sbuf_tensor("ssum_a", [128, d], BF16))
            b_sa = p.buf("ssum_a")
            h_sb = ph.enter_context(nc.sbuf_tensor("ssum_b", [128, d], BF16))
            b_sb = p.buf("ssum_b")
            h_so = ph.enter_context(nc.sbuf_tensor("ssum_o", [128, d], BF16))
            b_so = p.buf("ssum_o")
            for i in range(tf // 128):
                isl = slice(i * 128, (i + 1) * 128)
                p.dma("sync", lambda e, isl=isl: e.dma_start(out=h_sa[:, :], in_=t_s0.ap()[isl, :]),
                      writes=[b_sa])
                p.dma("sync", lambda e, isl=isl: e.dma_start(out=h_sb[:, :], in_=t_s1.ap()[isl, :]),
                      writes=[b_sb])
                p.step("vector", lambda e: e.tensor_add(h_so[:, :], h_sa[:, :], h_sb[:, :]),
                       reads=[b_sa, b_sb], writes=[b_so])
                p.dma("sync", lambda e, isl=isl: e.dma_start(out=s_sred.ap()[isl, :], in_=h_so[:, :]),
                      reads=[b_so], writes=[b_sred])
        p.barrier()

    if F:
        TS_R = tf // 128
        with ExitStack() as ph:
            h_xfT = ph.enter_context(nc.sbuf_tensor("xfT", [128, DT * tf], BF16))
            b_xfT = p.buf("xfT")
            h_wrtb = ph.enter_context(nc.sbuf_tensor("wrtb", [128, DT * 8], BF16))
            b_wrtb = p.buf("wrtb")
            h_wsgb = ph.enter_context(nc.sbuf_tensor("wsgb", [128, DT], BF16))
            b_wsgb = p.buf("wsgb")
            h_comb = ph.enter_context(nc.sbuf_tensor("comb", [128, 32], F32))
            b_comb = p.buf("comb")
            h_cT = ph.enter_context(nc.sbuf_tensor("cT", [9, tf], F32))
            b_cT = p.buf("cT")
            h_wg = [ph.enter_context(nc.sbuf_tensor(f"wgu{i}", [128, DT * 128], BF16)) for i in range(2)]
            b_wg = [p.buf(f"wgu{i}") for i in range(2)]
            h_wu = [ph.enter_context(nc.sbuf_tensor(f"wup{i}", [128, DT * 128], BF16)) for i in range(2)]
            b_wu = [p.buf(f"wup{i}") for i in range(2)]
            h_gs = ph.enter_context(nc.sbuf_tensor("gsil", [128, tf], BF16))
            b_gs = p.buf("gsil")
            h_hb = ph.enter_context(nc.sbuf_tensor("hbuf", [128, tf], BF16))
            b_hb = p.buf("hbuf")
            h_Bc = ph.enter_context(nc.sbuf_tensor("Bc", [128, tf], BF16))
            b_Bc = p.buf("Bc")
            h_crow1g = ph.enter_context(nc.sbuf_tensor("crow1g", [1, tf], F32))
            b_crow1g = p.buf("crow1g")

            for K in range(DT):
                p.dma("sync", lambda e, K=K: e.dma_start_transpose(
                    h_xfT[:, K * tf:(K + 1) * tf],
                    s_xnf.ap()[:, K * 128:(K + 1) * 128]),
                    reads=[b_xnf], writes=[b_xfT])
            p.dma("gpsimd", lambda e: e.dma_start(
                out=h_wrtb[:, :].rearrange("p (k c) -> p k c", c=8),
                in_=t_wrt.ap().rearrange("(k p) c -> p k c", p=128)), writes=[b_wrtb])
            p.dma("gpsimd", lambda e: e.dma_start(
                out=h_wsgb[:, :].rearrange("p (k c) -> p k c", c=1),
                in_=t_wsg.ap().rearrange("(k p) c -> p k c", p=128),
                allow_slow_non_contiguous=True), writes=[b_wsgb])

            for ti in range(TS_R):
                def rtfn(e, ti=ti):
                    ins = None
                    for K in range(DT):
                        ins = e.matmul(psA[:, 0:8], h_xfT[:, K * tf + ti * 128:K * tf + (ti + 1) * 128],
                                       h_wrtb[:, K * 8:(K + 1) * 8], start=(K == 0), stop=(K == DT - 1))
                    for K in range(DT):
                        ins = e.matmul(psA[:, 8:9], h_xfT[:, K * tf + ti * 128:K * tf + (ti + 1) * 128],
                                       h_wsgb[:, K:K + 1], start=(K == 0), stop=(K == DT - 1))
                    return ins
                p.step("tensor", rtfn, reads=[b_xfT, b_wrtb, b_wsgb], writes=[psA_r[0]])
                p.step("vector", lambda e: e.reduce_max(h_comb[:, 16:17], psA[:, 0:8], AX.X),
                       reads=[psA_r[0]], writes=[b_comb])
                p.step("vector", lambda e: e.scalar_tensor_tensor(
                    h_comb[:, 0:8], psA[:, 0:8], 1.0, h_comb[:, 16:17].broadcast_to([128, 8]), ALU.mult, ALU.subtract),
                    reads=[psA_r[0], b_comb], writes=[b_comb])
                p.step("scalar", lambda e: e.activation(h_comb[:, 0:8], h_comb[:, 0:8], AF.Exp, bias=0.0, scale=1.0),
                       reads=[b_comb], writes=[b_comb])
                p.step("vector", lambda e: e.reduce_sum(h_comb[:, 17:18], h_comb[:, 0:8], AX.X),
                       reads=[b_comb], writes=[b_comb])
                p.step("vector", lambda e: e.reciprocal(h_comb[:, 18:19], h_comb[:, 17:18]),
                       reads=[b_comb], writes=[b_comb])
                p.step("vector", lambda e: e.tensor_scalar_mul(h_comb[:, 0:8], h_comb[:, 0:8], h_comb[:, 18:19]),
                       reads=[b_comb], writes=[b_comb])
                p.step("vector", lambda e: e.reduce_max(h_comb[:, 19:20], h_comb[:, 0:8], AX.X),
                       reads=[b_comb], writes=[b_comb])
                p.step("vector", lambda e: e.tensor_scalar(
                    h_comb[:, 8:16], h_comb[:, 0:8], h_comb[:, 19:20], None, ALU.is_ge),
                    reads=[b_comb], writes=[b_comb])
                p.step("vector", lambda e: e.scalar_tensor_tensor(
                    h_comb[:, 0:8], h_comb[:, 8:16], NEG_BIG, h_comb[:, 0:8], ALU.mult, ALU.add),
                    reads=[b_comb], writes=[b_comb])
                p.step("vector", lambda e: e.reduce_max(h_comb[:, 20:21], h_comb[:, 0:8], AX.X),
                       reads=[b_comb], writes=[b_comb])
                p.step("vector", lambda e: e.scalar_tensor_tensor(
                    h_comb[:, 0:8], h_comb[:, 8:16], -NEG_BIG, h_comb[:, 0:8], ALU.mult, ALU.add),
                    reads=[b_comb], writes=[b_comb])
                p.step("vector", lambda e: e.tensor_scalar(
                    h_comb[:, 8:16], h_comb[:, 0:8], h_comb[:, 20:21], None, ALU.is_ge),
                    reads=[b_comb], writes=[b_comb])
                p.step("vector", lambda e: e.tensor_mul(h_comb[:, 0:8], h_comb[:, 0:8], h_comb[:, 8:16]),
                       reads=[b_comb], writes=[b_comb])
                p.step("scalar", lambda e: e.activation(h_comb[:, 8:9], psA[:, 8:9], AF.Sigmoid, bias=0.0, scale=1.0),
                       reads=[psA_r[0]], writes=[b_comb])
                p.step("tensor", lambda e: e.transpose(psB[0:16, 0:128], h_comb[:, 0:16], h_const_id_f[:, :]),
                       reads=[b_comb, b_cidf], writes=[psB_r[0]])
                p.step("vector", lambda e, ti=ti: e.tensor_copy(h_cT[:, ti * 128:(ti + 1) * 128], psB[0:9, 0:128]),
                       reads=[psB_r[0]], writes=[b_cT])

            for r in range(9):
                p.dma("sync", lambda e, r=r: e.dma_start(
                    out=s_crow.ap()[r:r + 1, :], in_=h_cT[r:r + 1, :]),
                    reads=[b_cT], writes=[b_crow])

            def gu_tile(dst_scr, b_dst, gate_ap, up_ap, it, bc=None, b_bc=None):
                wb = it % 2
                p.dma("gpsimd", lambda e, gate_ap=gate_ap, wb=wb: e.dma_start(
                    out=h_wg[wb][:, :].rearrange("p (k c) -> p k c", c=128),
                    in_=gate_ap.rearrange("(k p) c -> p k c", p=128)), writes=[b_wg[wb]])
                p.dma("gpsimd", lambda e, up_ap=up_ap, wb=wb: e.dma_start(
                    out=h_wu[wb][:, :].rearrange("p (k c) -> p k c", c=128),
                    in_=up_ap.rearrange("(k p) c -> p k c", p=128)), writes=[b_wu[wb]])

                def gmm(e, wb=wb, which=0):
                    ins = None
                    hw_ = h_wg[wb] if which == 0 else h_wu[wb]
                    ps = psA if which == 0 else psB
                    for s0, w in col_strips(tf):
                        for K in range(DT):
                            ins = e.matmul(ps[:, s0:s0 + w], hw_[:, K * 128:(K + 1) * 128],
                                           h_xfT[:, K * tf + s0:K * tf + s0 + w],
                                           start=(K == 0), stop=(K == DT - 1))
                    return ins
                p.step("tensor", lambda e, wb=wb: gmm(e, wb, 0), reads=[b_wg[wb], b_xfT], writes=regions(psA, 0, tf))
                p.step("tensor", lambda e, wb=wb: gmm(e, wb, 1), reads=[b_wu[wb], b_xfT], writes=regions(psB, 0, tf))
                p.step("scalar", lambda e: e.activation(h_gs[:, :], psA[:, 0:tf], AF.Silu, bias=0.0, scale=1.0),
                       reads=regions(psA, 0, tf), writes=[b_gs])
                if bc is not None:
                    p.step("gpsimd", lambda e, bc=bc: e.tensor_mul(h_gs[:, :], h_gs[:, :], bc[:, :]),
                           reads=[b_gs, b_bc], writes=[b_gs])
                p.step("vector", lambda e: e.tensor_mul(h_hb[:, :], h_gs[:, :], psB[:, 0:tf]),
                       reads=[b_gs] + regions(psB, 0, tf), writes=[b_hb])
                p.dma("sync", lambda e, dst_scr=dst_scr, it=it: e.dma_start(
                    out=dst_scr[it * 128:(it + 1) * 128, :], in_=h_hb[:, :]),
                    reads=[b_hb], writes=[b_dst])

            for it in range(SH_T):
                gu_tile(s_hsh.ap(), b_hsh,
                        t_wsgu.ap()[:, it * 128:(it + 1) * 128],
                        t_wsgu.ap()[:, i_sh + it * 128:i_sh + (it + 1) * 128], it)

            for ei in range(n_e):
                p.dma("sync", lambda e, ei=ei: e.dma_start(
                    out=h_crow1g[0:1, :], in_=s_crow.ap()[ei:ei + 1, :]),
                    reads=[b_crow], writes=[b_crow1g])

                def bcfn(e, ei=ei):
                    ins = None
                    for s0, w in col_strips(tf):
                        ins = e.matmul(psA[:, s0:s0 + w], h_onesr[:, :],
                                       h_crow1g[0:1, s0:s0 + w],
                                       start=True, stop=True)
                    return ins
                p.step("tensor", bcfn, reads=[b_onesr, b_crow1g], writes=regions(psA, 0, tf))
                p.step("vector", lambda e: e.tensor_copy(h_Bc[:, :], psA[:, 0:tf]),
                       reads=regions(psA, 0, tf), writes=[b_Bc])
                for it in range(IE_T):
                    gu_tile(s_he.ap()[ei], b_he,
                            t_wegu.ap()[ei][:, it * 128:(it + 1) * 128],
                            t_wegu.ap()[ei][:, i_e + it * 128:i_e + (it + 1) * 128],
                            it, bc=h_Bc, b_bc=b_Bc)
        p.barrier()

    if F:
        TQ = tf // 2
        with ExitStack() as ph:
            h_hs = ph.enter_context(nc.sbuf_tensor("hsres", [128, SH_T * TQ], BF16))
            b_hs = p.buf("hsres")
            h_wds = ph.enter_context(nc.sbuf_tensor("wds", [128, SH_T * 128], BF16))
            b_wds = p.buf("wds")
            h_Bsg = ph.enter_context(nc.sbuf_tensor("Bsg", [128, TQ], BF16))
            b_Bsg = p.buf("Bsg")
            h_ss = ph.enter_context(nc.sbuf_tensor("ssout", [128, 512], BF16))
            b_ss = p.buf("ssout")
            h_crow1s = ph.enter_context(nc.sbuf_tensor("crow1s", [1, tf], F32))
            b_crow1s = p.buf("crow1s")

            for half in range(2):
                qsl = slice(half * TQ, (half + 1) * TQ)
                p.dma("sync", lambda e, qsl=qsl: e.dma_start(
                    out=h_hs[:, :].rearrange("p (k w) -> p k w", w=TQ),
                    in_=s_hsh.ap()[:, qsl].rearrange("(k p) w -> p k w", p=128)),
                    reads=[b_hsh], writes=[b_hs])

                p.dma("sync", lambda e, half=half: e.dma_start(
                    out=h_crow1s[0:1, 0:TQ], in_=s_crow.ap()[8:9, half * TQ:(half + 1) * TQ]),
                    reads=[b_crow], writes=[b_crow1s])

                def bsgfn(e, half=half):
                    ins = None
                    for s0, w in col_strips(TQ):
                        ins = e.matmul(psB[:, s0:s0 + w], h_onesr[:, :],
                                       h_crow1s[0:1, s0:s0 + w],
                                       start=True, stop=True)
                    return ins
                p.step("tensor", bsgfn, reads=[b_onesr, b_crow1s], writes=regions(psB, 0, TQ))
                p.step("vector", lambda e: e.tensor_copy(h_Bsg[:, :], psB[:, 0:TQ]),
                       reads=regions(psB, 0, TQ), writes=[b_Bsg])

                for ot in range(DT):
                    p.dma("gpsimd", lambda e, ot=ot: e.dma_start(
                        out=h_wds[:, :].rearrange("p (k c) -> p k c", c=128),
                        in_=t_wsdn.ap()[:, ot * 128:(ot + 1) * 128].rearrange("(k p) c -> p k c", p=128)),
                        writes=[b_wds])
                    for s0, w in col_strips(TQ):
                        def sdmm(e, s0=s0, w=w):
                            ins = None
                            for K in range(SH_T):
                                ins = e.matmul(psA[:, 0:w], h_wds[:, K * 128:(K + 1) * 128],
                                               h_hs[:, K * TQ + s0:K * TQ + s0 + w],
                                               start=(K == 0), stop=(K == SH_T - 1))
                            return ins
                        p.step("tensor", sdmm, reads=[b_wds, b_hs], writes=regions(psA, 0, w))
                        p.step("vector", lambda e, s0=s0, w=w: e.tensor_mul(
                            h_ss[:, 0:w], psA[:, 0:w], h_Bsg[:, s0:s0 + w]),
                            reads=regions(psA, 0, w) + [b_Bsg], writes=[b_ss])
                        p.dma("sync", lambda e, ot=ot, half=half, s0=s0, w=w: e.dma_start(
                            out=s_shsc.ap()[ot * 128:(ot + 1) * 128, half * TQ + s0:half * TQ + s0 + w],
                            in_=h_ss[:, 0:w]),
                            reads=[b_ss], writes=[b_shsc])
        p.barrier()

    if F:
        TW = 256 if tf >= 1024 else min(tf, 256)
        OG = 4
        with ExitStack() as ph:
            def fsb2(name, shape, dt):
                hs, bs = [], []
                for i in range(2):
                    hs.append(ph.enter_context(nc.sbuf_tensor(f"{name}{i}", list(shape), dt)))
                    bs.append(p.buf(f"{name}{i}"))
                return hs, bs
            h_heqD, b_heqD = fsb2("heq", [128, n_e * IE_T * TW], BF16)
            h_wd = ph.enter_context(nc.sbuf_tensor("wdq", [128, n_e * IE_T * OG * 128], BF16))
            b_wd = p.buf("wdq")
            h_shsD, b_shsD = fsb2("shs", [128, TW], BF16)
            h_svD, b_svD = fsb2("sv", [128, TW], BF16)
            h_xt2D, b_xt2D = fsb2("xt2", [128, TW], F32)
            h_t1D, b_t1D = fsb2("t1f", [128, TW], F32)
            h_t2D, b_t2D = fsb2("t2b", [128, TW], BF16)
            h_ofD, b_ofD = fsb2("ofin", [128, TW], F32)

            for ocg in range(ceil_div(DT, OG)):
                oc0g = ocg * OG * 128
                ow_g = min(OG * 128, d - oc0g)
                not_g = ow_g // 128
                for ei in range(n_e):
                    p.dma("gpsimd", lambda e, ei=ei, oc0g=oc0g, ow_g=ow_g: e.dma_start(
                        out=h_wd[:, ei * IE_T * OG * 128:ei * IE_T * OG * 128 + IE_T * ow_g].rearrange(
                            "p (k c) -> p k c", c=ow_g),
                        in_=t_wedn.ap()[ei][:, oc0g:oc0g + ow_g].rearrange("(k p) c -> p k c", p=128)),
                        writes=[b_wd])
                for ts in range(tf // TW):
                    tsl = slice(ts * TW, (ts + 1) * TW)
                    hp = ts % 2
                    h_heq, b_heq = h_heqD[hp], b_heqD[hp]
                    for ei in range(n_e):
                        p.dma("sync", lambda e, ei=ei, tsl=tsl, h_heq=h_heq: e.dma_start(
                            out=h_heq[:, ei * IE_T * TW:(ei + 1) * IE_T * TW].rearrange("p (k w) -> p k w", w=TW),
                            in_=s_he.ap()[ei][:, tsl].rearrange("(k p) w -> p k w", p=128)),
                            reads=[b_he], writes=[b_heq])
                    for ot4 in range(not_g):
                        ot = ocg * OG + ot4
                        osl = slice(ot * 128, (ot + 1) * 128)
                        ps = psA if ot4 % 2 == 0 else psB
                        ap2 = (ts * not_g + ot4) % 2
                        h_shs, b_shs = h_shsD[ap2], b_shsD[ap2]
                        h_sv, b_sv = h_svD[ap2], b_svD[ap2]
                        h_xt2, b_xt2 = h_xt2D[ap2], b_xt2D[ap2]
                        h_t1, b_t1 = h_t1D[ap2], b_t1D[ap2]
                        h_t2, b_t2 = h_t2D[ap2], b_t2D[ap2]
                        h_of, b_of = h_ofD[ap2], b_ofD[ap2]

                        def edmm(e, ps=ps, ot4=ot4, ow_g=ow_g, h_heq=h_heq):
                            ins = None
                            first = True
                            for ei in range(n_e):
                                for K in range(IE_T):
                                    ins = e.matmul(
                                        ps[:, 0:TW],
                                        h_wd[:, ei * IE_T * OG * 128 + K * ow_g + ot4 * 128:
                                             ei * IE_T * OG * 128 + K * ow_g + ot4 * 128 + 128],
                                        h_heq[:, (ei * IE_T + K) * TW:(ei * IE_T + K + 1) * TW],
                                        start=first, stop=(ei == n_e - 1 and K == IE_T - 1))
                                    first = False
                            return ins
                        p.step("tensor", edmm, reads=[b_wd, b_heq], writes=regions(ps, 0, TW))
                        p.dma("sync", lambda e, osl=osl, tsl=tsl, h_shs=h_shs: e.dma_start(
                            out=h_shs[:, :], in_=s_shsc.ap()[osl, tsl]),
                            reads=[b_shsc], writes=[b_shs])
                        p.dma("sync", lambda e, osl=osl, tsl=tsl, h_sv=h_sv: e.dma_start_transpose(
                            h_sv[:, :], s_sred.ap()[tsl, osl]),
                            reads=[b_sred], writes=[b_sv])
                        p.dma("sync", lambda e, osl=osl, tsl=tsl, h_xt2=h_xt2: e.dma_start(
                            out=h_xt2[:, :], in_=t_xT.ap()[osl, tsl]),
                            writes=[b_xt2])
                        p.step("vector", lambda e, ps=ps, h_t1=h_t1, h_shs=h_shs: e.tensor_add(
                            h_t1[:, :], ps[:, 0:TW], h_shs[:, :]),
                            reads=regions(ps, 0, TW) + [b_shs], writes=[b_t1])
                        p.step("scalar", lambda e, h_t1=h_t1, h_t2=h_t2: e.activation(
                            h_t2[:, :], h_t1[:, :], AF.Silu, bias=0.0, scale=1.0),
                            reads=[b_t1], writes=[b_t2])
                        p.step("vector", lambda e, h_t2=h_t2, h_sv=h_sv: e.tensor_mul(
                            h_t2[:, :], h_t2[:, :], h_sv[:, :]),
                            reads=[b_t2, b_sv], writes=[b_t2])
                        p.step("vector", lambda e, h_t2=h_t2, h_xt2=h_xt2, h_of=h_of: e.tensor_add(
                            h_of[:, :], h_t2[:, :], h_xt2[:, :]),
                            reads=[b_t2, b_xt2], writes=[b_of])
                        p.dma("sync", lambda e, osl=osl, tsl=tsl, h_of=h_of: e.dma_start(
                            out=t_out.ap()[osl, tsl], in_=h_of[:, :]),
                            reads=[b_of], writes=[])

    p.barrier()
    p.finalize()
    p.close()
    stack_global.close()
    return declared_inputs



# version 10
# speedup vs baseline: 1.1884x; 1.1884x over previous
import math
from contextlib import ExitStack
from collections import deque

import numpy as np
import concourse.bass as bass
import concourse.mybir as mybir
from concourse.bass_utils import run_bass_kernel_spmd

F32 = mybir.dt.float32
BF16 = mybir.dt.bfloat16
AF = mybir.ActivationFunctionType
ALU = mybir.AluOpType
AX = mybir.AxisListType

NEG_BIG = -30000.0


def real_cfg():
    return dict(
        d=2048,
        hv=16, hk=8,
        dk=128, dv=128,
        tb=4096,
        tf=2048,
        i_sh=5632, i_e=1408, n_e=8,
        n_cores=8,
        pair_groups=[[0, 1], [2, 3], [4, 5], [6, 7]],
    )


def real_cfg_e():
    return dict(
        d=2048,
        hv=32, hk=16,
        dk=128, dv=128,
        tb=4096,
        tf=2048,
        hg=8,
        i_sh=5632, i_e=1408, n_e=8,
        n_cores=8,
        emode=True,
        pair_groups=[[0, 1], [2, 3], [4, 5], [6, 7]],
    )



class Buf:
    __slots__ = ("name", "last_w", "readers")

    def __init__(self, name):
        self.name = name
        self.last_w = None
        self.readers = []


class Prog:
    COMPUTE_SEM = {"tensor": "pe", "vector": "dve", "scalar": "act",
                   "gpsimd": "pool", "sync": "sp"}
    DMA_QUEUES = {"sync": "sdma", "gpsimd": "gdma", "scalar": "adma"}
    NLANES = 8
    ENGINES = ["tensor", "vector", "scalar", "gpsimd", "sync"]

    def __init__(self, nc):
        self.nc = nc
        self.cnt = {}
        self.sems = {}
        self.sat = {e: {} for e in self.ENGINES}
        self.steps = {e: [] for e in self.ENGINES}
        self._stack = ExitStack()
        keys = ["pe", "dve", "act", "pool", "sp", "cc"]
        for q in self.DMA_QUEUES.values():
            keys += [f"{q}{i}" for i in range(self.NLANES)]
        for key in keys:
            self.sems[key] = self._stack.enter_context(nc.semaphore(key))
            self.cnt[key] = 0
        self.qrr = {q: 0 for q in self.DMA_QUEUES.values()}
        self.nsteps = 0

    def buf(self, name):
        return Buf(name)

    def step(self, engine, fn, reads=(), writes=(), semkey=None, inc=None,
             extra_waits=()):
        if semkey is None:
            semkey = self.COMPUTE_SEM[engine]
            if inc is None:
                inc = 1
        waits = {}
        for k, c in extra_waits:
            if waits.get(k, 0) < c:
                waits[k] = c

        def need(dep):
            k, c = dep
            if waits.get(k, 0) < c:
                waits[k] = c

        for b in reads:
            if b.last_w is not None:
                need(b.last_w)
        for b in writes:
            if b.last_w is not None:
                need(b.last_w)
            for r in b.readers:
                need(r)
        sat = self.sat[engine]
        final_waits = []
        for k, c in sorted(waits.items()):
            if sat.get(k, 0) >= c:
                continue
            final_waits.append((self.sems[k], c))
            sat[k] = c
        rec = None
        if inc:
            self.cnt[semkey] += inc
            rec = (semkey, self.cnt[semkey])
        for b in writes:
            assert rec is not None, "writing step must inc a semaphore"
            b.last_w = rec
            b.readers = []
        if rec is not None:
            for b in reads:
                b.readers.append(rec)
        self.steps[engine].append(
            (final_waits, fn, (self.sems[semkey], inc) if inc else None))
        self.nsteps += 1

    def dma(self, queue_engine, fn, reads=(), writes=()):
        q = self.DMA_QUEUES[queue_engine]
        lane = self.qrr[q] % self.NLANES
        self.qrr[q] += 1
        semkey = f"{q}{lane}"
        prev = self.cnt[semkey]
        ew = [(semkey, prev)] if prev else []
        self.step(queue_engine, fn, reads=reads, writes=writes,
                  semkey=semkey, inc=16, extra_waits=ew)

    def barrier(self):
        snapshot = dict(self.cnt)
        for e in self.ENGINES:
            sat = self.sat[e]
            fw = []
            for k, c in snapshot.items():
                if c and sat.get(k, 0) < c:
                    fw.append((self.sems[k], c))
                    sat[k] = c
            if fw:
                self.steps[e].append((fw, None, None))

    def finalize(self):
        nc = self.nc

        def mk(engine_name):
            def run(eng):
                for waits, fn, inc in self.steps[engine_name]:
                    for sem, val in waits:
                        eng.wait_ge(sem, val)
                    if fn is not None:
                        ins = fn(eng)
                        if inc is not None:
                            assert ins is not None, f"step on {engine_name} returned no instruction"
                            ins.then_inc(inc[0], inc[1])
                    else:
                        assert inc is None
            return run

        with nc.Block() as block:
            block.tensor(mk("tensor"))
            block.vector(mk("vector"))
            block.scalar(mk("scalar"))
            block.gpsimd(mk("gpsimd"))
            block.sync(mk("sync"))

    def close(self):
        self._stack.close()



def make_consts(cfg):
    C = 128
    BS = 32
    hv = cfg.get("hg") or min(cfg["hv"], 16)
    lo = np.tril(np.ones((C, C), np.float32), -1)
    loD = np.zeros((C, C), np.float32)
    for s in range(C // BS):
        sl = slice(s * BS, (s + 1) * BS)
        loD[sl, sl] = np.tril(np.ones((BS, BS), np.float32), -1)
    loL = lo - loD
    upD = loD.T
    upL = loL.T
    up_incl = np.triu(np.ones((C, C), np.float32), 0)

    def maskT(m):
        return np.where(m > 0, 0.0, NEG_BIG).astype(np.float32).T.copy()

    sel16 = np.zeros((hv, hv * C), np.float32)
    for p in range(hv):
        sel16[p, p * C:(p + 1) * C] = 1.0
    repid = np.tile(np.eye(C, dtype=np.float32), (1, hv))
    colsel = np.zeros((6, 384), np.float32)
    for s in range(3):
        colsel[2 * s:2 * s + 2, s * 128:(s + 1) * 128] = 1.0

    return {
        "c_ident_f": np.eye(C, dtype=np.float32),
        "c_ident_b": np.eye(C, dtype=np.float32),
        "c_sel16": sel16,
        "c_colsel": colsel,
        "c_maskT_n": maskT(loD),
        "c_maskT_nt": maskT(upD),
        "c_maskT_LT": maskT(upL),
        "c_maskT_at": maskT(up_incl),
        "c_ones128": np.ones((C, 1), np.float32),
        "c_onesrow": np.ones((1, C), np.float32),
        "c_oneshv": np.ones((hv, C), np.float32),
        "c_repid_b": repid,
    }



MODE = "fused_e"


def kernel(**inputs):
    cfg = real_cfg_e() if MODE == "fused_e" else real_cfg()
    maps = shard_inputs(cfg, inputs)
    ncore = cfg["n_cores"]
    tf = cfg["tf"]
    if MODE == "fused_e":
        nc = bass.Bass("TRN2", target_bir_lowering=False)
        names = build_program(nc, cfg, "full")
        ins = [{k: m[k] for k in names} for m in maps]
        res = run_bass_kernel_spmd(nc, ins, core_ids=list(range(ncore)))
        return assemble_output(cfg, res)

    nc1 = bass.Bass("TRN2", target_bir_lowering=False)
    names1 = build_program(nc1, cfg, "attn")
    ins1 = [{k: m[k] for k in names1} for m in maps]
    res1 = run_bass_kernel_spmd(nc1, ins1, core_ids=list(range(ncore)))
    sparts = [np.asarray(res1.results[c]["spart_out"]) for c in range(ncore)]

    nc2 = bass.Bass("TRN2", target_bir_lowering=False)
    names2 = build_program(nc2, cfg, "ffn")
    ins2 = []
    for c in range(ncore):
        half = c % 2
        partner = c ^ 1
        m2 = {k: maps[c][k] for k in names2 if k not in ("s_half0", "s_half1")}
        m2["s_half0"] = np.ascontiguousarray(sparts[c][half * tf:(half + 1) * tf])
        m2["s_half1"] = np.ascontiguousarray(sparts[partner][half * tf:(half + 1) * tf])
        ins2.append(m2)
    res2 = run_bass_kernel_spmd(nc2, ins2, core_ids=list(range(ncore)))
    return assemble_output(cfg, res2)


def shard_inputs(cfg, inputs):
    d = cfg["d"]
    tb, tf = cfg["tb"], cfg["tf"]
    hv, hk, dk, dv = cfg["hv"], cfg["hk"], cfg["dk"], cfg["dv"]
    emode = bool(cfg.get("emode", False))
    HK_full = hk if emode else 2 * hk
    HV_full = hv if emode else 2 * hv
    key_dim = HK_full * dk
    val_dim = HV_full * dv
    x = np.ascontiguousarray(np.asarray(inputs["L0_attn_hidden_states"], dtype=np.float32))
    W_qkv = np.asarray(inputs["L0_attn_W_qkv"], dtype=np.float32)
    W_z = np.asarray(inputs["L0_attn_W_z"], dtype=np.float32)
    W_b = np.asarray(inputs["L0_attn_W_b"], dtype=np.float32)
    W_a = np.asarray(inputs["L0_attn_W_a"], dtype=np.float32)
    conv_w = np.asarray(inputs["L0_attn_conv_w"], dtype=np.float32)
    dt_bias = np.asarray(inputs["L0_attn_dt_bias"], dtype=np.float32)
    A_log = np.asarray(inputs["L0_attn_A_log"], dtype=np.float32)
    norm_w = np.asarray(inputs["L0_attn_norm_w"], dtype=np.float32)
    W_out = np.asarray(inputs["L0_attn_W_out"], dtype=np.float32)
    W_router = np.asarray(inputs["L0_ffn_W_router"], dtype=np.float32)
    W_sgate = np.asarray(inputs["L0_ffn_W_sgate"], dtype=np.float32)
    Ws_gu = np.asarray(inputs["L0_ffn_Ws_gate_up"], dtype=np.float32)
    Ws_dn = np.asarray(inputs["L0_ffn_Ws_down"], dtype=np.float32)
    We_gu = np.asarray(inputs["L0_ffn_We_gate_up"], dtype=np.float32)
    We_dn = np.asarray(inputs["L0_ffn_We_down"], dtype=np.float32)

    consts = make_consts(cfg)
    maps = []
    for c in range(cfg["n_cores"]):
        b, half = c // 2, c % 2
        if emode:
            qcols = slice(0, key_dim)
            kcols = slice(key_dim, 2 * key_dim)
            vcols = slice(2 * key_dim, 2 * key_dim + val_dim)
            vheads = slice(0, HV_full)
            warm = (x[b, 0:tb - tf] if half == 1
                    else np.zeros((tb - tf, d), np.float32))
            x_att = np.concatenate([warm, x[b, half * tf:(half + 1) * tf]], axis=0)
        else:
            qcols = slice(half * hk * dk, (half + 1) * hk * dk)
            kcols = slice(key_dim + half * hk * dk, key_dim + (half + 1) * hk * dk)
            vcols = slice(2 * key_dim + half * hv * dv, 2 * key_dim + (half + 1) * hv * dv)
            vheads = slice(half * hv, (half + 1) * hv)
            x_att = x[b]
        toks = slice(half * tf, (half + 1) * tf)
        wqkv_c = np.concatenate([W_qkv[:, qcols], W_qkv[:, kcols], W_qkv[:, vcols]], axis=1)
        convw_c = np.concatenate([conv_w[qcols.start:qcols.stop],
                                  conv_w[kcols.start:kcols.stop],
                                  conv_w[vcols.start:vcols.stop]], axis=0)
        m = {
            "x_nat": np.ascontiguousarray(x_att),
            "xF": np.ascontiguousarray(x[b, toks]),
            "xT_half": np.ascontiguousarray(x[b, toks].T),
            "wqkv": np.ascontiguousarray(wqkv_c),
            "convw": np.ascontiguousarray(convw_c),
            "wz": np.ascontiguousarray(W_z[:, vcols.start - 2 * key_dim:vcols.stop - 2 * key_dim]),
            "wb": np.ascontiguousarray(W_b[:, vheads]),
            "wa": np.ascontiguousarray(W_a[:, vheads]),
            "alog": np.ascontiguousarray(A_log[vheads][:, None]),
            "dtb": np.ascontiguousarray(dt_bias[vheads][:, None]),
            "normw": np.ascontiguousarray(norm_w[:, None]),
            "wout": np.ascontiguousarray(W_out[vcols.start - 2 * key_dim:vcols.stop - 2 * key_dim]),
            "wrt": np.ascontiguousarray(W_router.T),
            "wsg": np.ascontiguousarray(W_sgate.T),
            "wsgu": Ws_gu,
            "wsdn": Ws_dn,
            "wegu": We_gu,
            "wedn": We_dn,
        }
        m.update(consts)
        maps.append(m)
    return maps


def assemble_output(cfg, res):
    tb, tf, d = cfg["tb"], cfg["tf"], cfg["d"]
    nb = cfg["n_cores"] // 2
    out = np.zeros((nb, tb, d), dtype=np.float32)
    for c in range(cfg["n_cores"]):
        b, half = c // 2, c % 2
        outT = res.results[c]["outT"]
        out[b, half * tf:(half + 1) * tf, :] = np.asarray(outT).T
    return out



def ceil_div(a, b):
    return (a + b - 1) // b


def col_strips(total, width=512):
    out = []
    c = 0
    while c < total:
        w = min(width, total - c)
        out.append((c, w))
        c += w
    return out


def build_program(nc, cfg, part="full"):
    d, hv, hk = cfg["d"], cfg["hv"], cfg["hk"]
    dk, dv = cfg["dk"], cfg["dv"]
    tb, tf = cfg["tb"], cfg["tf"]
    i_sh, i_e, n_e = cfg["i_sh"], cfg["i_e"], cfg["n_e"]
    emode = bool(cfg.get("emode", False))
    DT = d // 128
    QT = 2 * hk + hv
    VT = hv
    NCH = tb // 128
    TH2 = tb // 2
    SH_T = i_sh // 128
    IE_T = i_e // 128
    HW = hv * 128
    HGC = cfg.get("hg") or min(hv, 16)
    HWGC = HGC * 128
    C = 128

    p = Prog(nc)

    declared_inputs = []

    def din(name, shape, dt=F32):
        declared_inputs.append(name)
        return nc.dram_tensor(name, list(shape), dt, kind="ExternalInput")

    A = part in ("full", "attn")
    F = part in ("full", "ffn")
    t_x = din("x_nat", (tb, d)) if A else None
    t_xF = din("xF", (tf, d)) if F else None
    t_xT = din("xT_half", (d, tf)) if F else None
    t_wqkv = din("wqkv", (d, QT * 128)) if A else None
    t_convw = din("convw", (QT * 128, 4)) if A else None
    t_wz = din("wz", (d, hv * dv)) if A else None
    t_wb = din("wb", (d, hv)) if A else None
    t_wa = din("wa", (d, hv)) if A else None
    t_alog = din("alog", (hv, 1)) if A else None
    t_dtb = din("dtb", (hv, 1)) if A else None
    t_normw = din("normw", (128, 1)) if A else None
    t_wout = din("wout", (hv * dv, d)) if A else None
    t_wrt = din("wrt", (d, 8)) if F else None
    t_wsg = din("wsg", (d, 1)) if F else None
    t_wsgu = din("wsgu", (d, 2 * i_sh)) if F else None
    t_wsdn = din("wsdn", (i_sh, d)) if F else None
    t_wegu = din("wegu", (n_e, d, 2 * i_e)) if F else None
    t_wedn = din("wedn", (n_e, i_e, d)) if F else None
    t_cidf = din("c_ident_f", (C, C))
    t_cidb = din("c_ident_b", (C, C))
    t_sel = din("c_sel16", (HGC, HWGC))
    t_colsel = din("c_colsel", (6, 384))
    t_mT_n = din("c_maskT_n", (C, C))
    t_mT_nt = din("c_maskT_nt", (C, C))
    t_mT_LT = din("c_maskT_LT", (C, C))
    t_mT_at = din("c_maskT_at", (C, C))
    t_ones = din("c_ones128", (C, 1))
    t_onesrow = din("c_onesrow", (1, C))
    t_oneshv = din("c_oneshv", (HGC, C))
    t_repid_b = din("c_repid_b", (C, HWGC))

    if part == "ffn":
        t_s0 = din("s_half0", (tf, d), BF16)
        t_s1 = din("s_half1", (tf, d), BF16)
    t_spart_out = None
    t_out = None
    if part == "attn":
        t_spart_out = nc.dram_tensor("spart_out", [tb, d], BF16, kind="ExternalOutput")
    else:
        t_out = nc.dram_tensor("outT", [d, tf], F32, kind="ExternalOutput")

    s_xn = nc.dram_tensor("xn_scr", [tb, d], BF16)
    s_xnf = nc.dram_tensor("xnf_scr", [tf, d], BF16)
    s_qkv = nc.dram_tensor("qkv_scr", [QT * 128, tb], BF16)
    s_zsil = nc.dram_tensor("zsil_scr", [hv * dv, tf if emode else tb], BF16)
    s_o3 = nc.dram_tensor("o3_scr", [tf if emode else tb, hv * dv], BF16)
    s_spart = nc.dram_tensor("spart_scr", [tb, d], BF16) if (part == "full" and not emode) else None
    if part == "attn":
        s_spart = t_spart_out
    s_sred = nc.dram_tensor("sred_scr", [tf, d], BF16)
    s_hsh = nc.dram_tensor("hsh_scr", [i_sh, tf], BF16)
    s_he = nc.dram_tensor("he_scr", [n_e, i_e, tf], BF16)
    s_shsc = nc.dram_tensor("shsc_scr", [d, tf], BF16)
    s_crow = nc.dram_tensor("crow_scr", [9, tf], F32)
    s_gates = nc.dram_tensor("gates_scr", [3, hv, tb], F32)
    s_hrow2 = nc.dram_tensor("hrow2_scr", [2, 3, 2, HGC, 128], BF16)
    import os as _os
    DBG_DELTA = bool(_os.environ.get("DBG_DELTA"))
    s_dbg = nc.dram_tensor("dbg_scr", [64, 6, 128, HWGC], F32) if DBG_DELTA else None

    b_xn = p.buf("xn_scr")
    b_xnf = p.buf("xnf_scr")
    b_qkv = p.buf("qkv_scr")
    b_zsil = p.buf("zsil_scr")
    b_o3scr = p.buf("o3_scr")
    b_spart = p.buf("spart_scr")
    b_sred = p.buf("sred_scr")
    b_hsh = p.buf("hsh_scr")
    b_he = p.buf("he_scr")
    b_shsc = p.buf("shsc_scr")
    b_crow = p.buf("crow_scr")
    b_gates = p.buf("gates_scr")
    b_hrw2D = [p.buf(f"hrow2_{i}") for i in range(2)]
    b_dbg = p.buf("dbg_scr")

    stack_global = ExitStack()

    def sb(name, shape, dt=F32):
        h = nc.alloc_sbuf_tensor(name, list(shape), dt)
        return h, p.buf(name)

    PSW = max(HWGC, 2048 if tb >= 4096 else max(TH2, tf, 512))
    PSW = max(PSW, 512)
    psA = nc.alloc_psum_tensor("psA", [128, PSW], F32)
    psB = nc.alloc_psum_tensor("psB", [128, PSW], F32)
    psA_r = [p.buf(f"psA{i}") for i in range(PSW // 512)]
    psB_r = [p.buf(f"psB{i}") for i in range(PSW // 512)]

    def regions(which, c0, w):
        rs = psA_r if which is psA else psB_r
        return [rs[i] for i in range(c0 // 512, ceil_div(c0 + w, 512))]

    h_const_id_f, b_cidf = sb("cidf", [C, C])
    h_const_id_b, b_cidb = sb("cidb", [C, C], BF16)
    h_selb, b_selb = sb("selb", [HGC, HWGC], BF16)
    h_colsel, b_colsel = sb("colselb", [6, 384], BF16)
    h_repid_b, b_repidb = sb("repidb", [C, HWGC], BF16)
    h_masksb, b_masksb = sb("masksb", [C, 4 * C], BF16)
    h_ones, b_ones = sb("ones", [C, 1])
    h_onesb, b_onesb = sb("onesb", [C, 1], BF16)
    h_onesr, b_onesr = sb("onesr", [1, C])
    h_oneshv, b_oneshv = sb("oneshv", [HGC, C])
    h_convw, b_convw = sb("convw_sb", [128, QT * 4])
    h_alog, b_alog = sb("alog_sb", [hv, 1])
    h_dtb, b_dtb = sb("dtb_sb", [hv, 1])
    h_negea, b_negea = sb("negea", [hv, 1])
    h_normw, b_normw = sb("normw_sb", [128, 1])

    h_S = nc.alloc_sbuf_tensor("S_state", [128, HW], BF16)
    b_S = [p.buf(f"S_state{g}") for g in range(max(1, hv // HGC))]
    h_bias, b_bias = sb("biasconst", [128, 2], F32)

    p.dma("sync", lambda e: e.dma_start(out=h_const_id_f[:, :], in_=t_cidf.ap()[:, :]), writes=[b_cidf])
    p.dma("gpsimd", lambda e: e.dma_start(out=h_const_id_b[:, :], in_=t_cidb.ap()[:, :]), writes=[b_cidb])
    p.dma("gpsimd", lambda e: e.dma_start(out=h_selb[:, :], in_=t_sel.ap()[:, :]), writes=[b_selb])
    p.dma("gpsimd", lambda e: e.dma_start(out=h_colsel[:, :], in_=t_colsel.ap()[:, :]), writes=[b_colsel])
    p.dma("gpsimd", lambda e: e.dma_start(out=h_repid_b[:, :], in_=t_repid_b.ap()[:, :]), writes=[b_repidb])
    for i, t in enumerate([t_mT_n, t_mT_nt, t_mT_LT, t_mT_at]):
        p.dma("gpsimd", lambda e, i=i, t=t: e.dma_start(out=h_masksb[:, i * C:(i + 1) * C], in_=t.ap()[:, :]), writes=[b_masksb])
    p.dma("sync", lambda e: e.dma_start(out=h_ones[:, :], in_=t_ones.ap()[:, :]), writes=[b_ones])
    p.dma("gpsimd", lambda e: e.dma_start(out=h_onesb[:, :], in_=t_ones.ap()[:, :]), writes=[b_onesb])
    p.dma("sync", lambda e: e.dma_start(out=h_onesr[:, :], in_=t_onesrow.ap()[:, :]), writes=[b_onesr])
    p.dma("sync", lambda e: e.dma_start(out=h_oneshv[:, :], in_=t_oneshv.ap()[:, :]), writes=[b_oneshv])
    if A:
        p.dma("sync", lambda e: e.dma_start(
            out=h_convw[:, :].rearrange("p (t k) -> p t k", k=4),
            in_=t_convw.ap().rearrange("(t p) k -> p t k", p=128)), writes=[b_convw])
        p.dma("sync", lambda e: e.dma_start(out=h_alog[:, :], in_=t_alog.ap()[:, :]), writes=[b_alog])
        p.dma("sync", lambda e: e.dma_start(out=h_dtb[:, :], in_=t_dtb.ap()[:, :]), writes=[b_dtb])
        p.dma("sync", lambda e: e.dma_start(out=h_normw[:, :], in_=t_normw.ap()[:, :]), writes=[b_normw])
        p.step("scalar", lambda e: e.activation(h_negea[:, :], h_alog[:, :], AF.Exp, bias=0.0, scale=1.0),
               reads=[b_alog], writes=[b_negea])
        p.step("vector", lambda e: e.tensor_scalar_mul(h_negea[:, :], h_negea[:, :], -1.0),
               reads=[b_negea], writes=[b_negea])
        p.step("vector", lambda e: e.memset(h_S[:, :], 0.0), writes=list(b_S))
    p.step("vector", lambda e: e.memset(h_bias[:, 0:1], 1.0), writes=[b_bias])
    p.step("vector", lambda e: e.memset(h_bias[:, 1:2], 1e-6), writes=[b_bias])

    with ExitStack() as ph:
        h_xc = [ph.enter_context(nc.sbuf_tensor(f"lnx{i}", [128, d], F32)) for i in range(2)]
        b_xc = [p.buf(f"lnx{i}") for i in range(2)]
        h_sq = ph.enter_context(nc.sbuf_tensor("lnsq", [128, d], F32))
        b_sq = p.buf("lnsq")
        h_st = ph.enter_context(nc.sbuf_tensor("lnst", [128, 8], F32))
        b_st = p.buf("lnst")
        h_xo = ph.enter_context(nc.sbuf_tensor("lnxo", [128, d], BF16))
        b_xo = p.buf("lnxo")

        def ln_pass(src, dst, b_dst, n_chunks):
            for i in range(n_chunks):
                bi = i % 2
                xap = src.ap()[i * 128:(i + 1) * 128, :]
                p.dma("sync", lambda e, xap=xap, bi=bi: e.dma_start(out=h_xc[bi][:, :], in_=xap),
                      writes=[b_xc[bi]])
                p.step("vector", lambda e, bi=bi: e.reduce_sum(h_st[:, 0:1], h_xc[bi][:, :], AX.X),
                       reads=[b_xc[bi]], writes=[b_st])
                p.step("scalar", lambda e, bi=bi: e.activation(h_sq[:, :], h_xc[bi][:, :], AF.Square, bias=0.0, scale=1.0),
                       reads=[b_xc[bi]], writes=[b_sq])
                p.step("vector", lambda e: e.reduce_sum(h_st[:, 1:2], h_sq[:, :], AX.X),
                       reads=[b_sq], writes=[b_st])
                p.step("vector", lambda e: e.tensor_scalar_mul(h_st[:, 2:3], h_st[:, 0:1], 1.0 / d),
                       reads=[b_st], writes=[b_st])
                p.step("vector", lambda e: e.tensor_mul(h_st[:, 3:4], h_st[:, 2:3], h_st[:, 2:3]),
                       reads=[b_st], writes=[b_st])
                p.step("vector", lambda e: e.scalar_tensor_tensor(h_st[:, 4:5], h_st[:, 1:2], 1.0 / d, h_st[:, 3:4], ALU.mult, ALU.subtract),
                       reads=[b_st], writes=[b_st])
                p.step("vector", lambda e: e.tensor_scalar_add(h_st[:, 4:5], h_st[:, 4:5], 1e-5),
                       reads=[b_st], writes=[b_st])
                p.step("vector", lambda e: e.reciprocal(h_st[:, 5:6], h_st[:, 4:5]),
                       reads=[b_st], writes=[b_st])
                p.step("scalar", lambda e: e.activation(h_st[:, 6:7], h_st[:, 5:6], AF.Sqrt, bias=0.0, scale=1.0),
                       reads=[b_st], writes=[b_st])
                p.step("vector", lambda e: e.scalar_tensor_tensor(h_st[:, 7:8], h_st[:, 2:3], -1.0, h_st[:, 6:7], ALU.mult, ALU.mult),
                       reads=[b_st], writes=[b_st])
                p.step("vector", lambda e, bi=bi: e.tensor_scalar(
                    h_xo[:, :], h_xc[bi][:, :], h_st[:, 6:7], h_st[:, 7:8], ALU.mult, ALU.add),
                    reads=[b_xc[bi], b_st], writes=[b_xo])
                p.dma("sync", lambda e, i=i, dst=dst: e.dma_start(out=dst.ap()[i * 128:(i + 1) * 128, :], in_=h_xo[:, :]),
                      reads=[b_xo], writes=[b_dst])

        if part in ("full", "attn"):
            ln_pass(t_x, s_xn, b_xn, tb // 128)
        if part in ("full", "ffn"):
            ln_pass(t_xF, s_xnf, b_xnf, tf // 128)
    p.barrier()

    if A:
        with ExitStack() as ph:
            h_xnT = ph.enter_context(nc.sbuf_tensor("xnTh", [128, DT * TH2], BF16))
            b_xnT = p.buf("xnTh")
            h_wsl = [ph.enter_context(nc.sbuf_tensor(f"wsl{i}", [128, DT * 128], BF16)) for i in range(2)]
            b_wsl = [p.buf(f"wsl{i}") for i in range(2)]
            h_mix = ph.enter_context(nc.sbuf_tensor("mix", [128, TH2 + 3], BF16))
            b_mix = p.buf("mix")
            h_cv = ph.enter_context(nc.sbuf_tensor("cvacc", [128, TH2], BF16))
            b_cv = p.buf("cvacc")
            h_row = ph.enter_context(nc.sbuf_tensor("rowout", [128, TH2], BF16))
            b_row = p.buf("rowout")
            h_hist = ph.enter_context(nc.sbuf_tensor("hist", [128, QT * 4], BF16))
            b_hist = p.buf("hist")
            h_wba = ph.enter_context(nc.sbuf_tensor("wba", [128, DT * 2 * hv], BF16))
            b_wba = p.buf("wba")
            h_sp = ph.enter_context(nc.sbuf_tensor("spbuf", [hv, 2 * TH2], F32))
            b_sp = p.buf("spbuf")
            h_gstg = ph.enter_context(nc.sbuf_tensor("gstg", [hv, TH2], F32))
            b_gstg = p.buf("gstg")

            p.step("vector", lambda e: e.memset(h_hist[:, :], 0.0), writes=[b_hist])
            p.dma("gpsimd", lambda e: e.dma_start(
                out=h_wba[:, 0:DT * hv].rearrange("p (k h) -> p k h", h=hv),
                in_=t_wb.ap().rearrange("(k p) h -> p k h", p=128)), writes=[b_wba])
            p.dma("gpsimd", lambda e: e.dma_start(
                out=h_wba[:, DT * hv:].rearrange("p (k h) -> p k h", h=hv),
                in_=t_wa.ap().rearrange("(k p) h -> p k h", p=128)), writes=[b_wba])

            for H in range(2):
                hsl = slice(H * TH2, (H + 1) * TH2)
                for K in range(DT):
                    p.dma("sync", lambda e, K=K, hsl=hsl: e.dma_start_transpose(
                        h_xnT[:, K * TH2:(K + 1) * TH2],
                        s_xn.ap()[hsl, K * 128:(K + 1) * 128]),
                        reads=[b_xn], writes=[b_xnT])
                n_tiles = QT + (VT if (not emode or H == 1) else 0)
                for ot in range(n_tiles):
                    is_z = ot >= QT
                    zt = ot - QT
                    zdst = (slice(hsl.start - TH2, hsl.stop - TH2)
                            if emode else hsl)
                    ps = psA if ot % 2 == 0 else psB
                    ps_r = regions(ps, 0, TH2)
                    wbuf = ot % 2
                    if not is_z:
                        wap = t_wqkv.ap()[:, ot * 128:(ot + 1) * 128]
                    else:
                        wap = t_wz.ap()[:, zt * 128:(zt + 1) * 128]
                    p.dma("gpsimd", lambda e, wap=wap, wbuf=wbuf: e.dma_start(
                        out=h_wsl[wbuf][:, :].rearrange("p (k c) -> p k c", c=128),
                        in_=wap.rearrange("(k p) c -> p k c", p=128)),
                        writes=[b_wsl[wbuf]])

                    def mmfn(e, ps=ps, wbuf=wbuf):
                        ins = None
                        for s0, w in col_strips(TH2):
                            for K in range(DT):
                                ins = e.matmul(ps[:, s0:s0 + w],
                                               h_wsl[wbuf][:, K * 128:(K + 1) * 128],
                                               h_xnT[:, K * TH2 + s0:K * TH2 + s0 + w],
                                               start=(K == 0), stop=(K == DT - 1))
                        return ins
                    p.step("tensor", mmfn, reads=[b_wsl[wbuf], b_xnT], writes=ps_r)

                    if is_z:
                        p.step("scalar", lambda e, ps=ps: e.activation(
                            h_row[:, :], ps[:, 0:TH2], AF.Silu, bias=0.0, scale=1.0),
                            reads=ps_r, writes=[b_row])
                        p.dma("sync", lambda e, zt=zt, zdst=zdst: e.dma_start(
                            out=s_zsil.ap()[zt * 128:(zt + 1) * 128, zdst], in_=h_row[:, :]),
                            reads=[b_row], writes=[b_zsil])
                    else:
                        p.step("vector", lambda e, ot=ot: e.tensor_copy(h_mix[:, 0:3], h_hist[:, ot * 4:ot * 4 + 3]),
                               reads=[b_hist], writes=[b_mix])
                        p.step("vector", lambda e, ps=ps: e.tensor_copy(h_mix[:, 3:3 + TH2], ps[:, 0:TH2]),
                               reads=ps_r, writes=[b_mix])
                        p.step("vector", lambda e, ot=ot: e.tensor_copy(h_hist[:, ot * 4:ot * 4 + 3], h_mix[:, TH2:TH2 + 3]),
                               reads=[b_mix], writes=[b_hist])
                        p.step("vector", lambda e, ot=ot: e.tensor_scalar_mul(
                            h_cv[:, :], h_mix[:, 0:TH2], h_convw[:, ot * 4:ot * 4 + 1]),
                            reads=[b_mix, b_convw], writes=[b_cv])
                        for tap in range(1, 4):
                            p.step("vector", lambda e, ot=ot, tap=tap: e.scalar_tensor_tensor(
                                h_cv[:, :], h_mix[:, tap:tap + TH2], h_convw[:, ot * 4 + tap:ot * 4 + tap + 1],
                                h_cv[:, :], ALU.mult, ALU.add),
                                reads=[b_mix, b_convw, b_cv], writes=[b_cv])
                        p.step("scalar", lambda e: e.activation(
                            h_row[:, :], h_cv[:, :], AF.Silu, bias=0.0, scale=1.0),
                            reads=[b_cv], writes=[b_row])
                        p.dma("sync", lambda e, ot=ot, hsl=hsl: e.dma_start(
                            out=s_qkv.ap()[ot * 128:(ot + 1) * 128, hsl], in_=h_row[:, :]),
                            reads=[b_row], writes=[b_qkv])

                for which in range(2):
                    ps = psA if which == 0 else psB
                    ps_r = regions(ps, 0, TH2)

                    def bafn(e, ps=ps, which=which):
                        ins = None
                        for s0, w in col_strips(TH2):
                            for K in range(DT):
                                ins = e.matmul(ps[0:hv, s0:s0 + w],
                                               h_wba[:, which * DT * hv + K * hv:which * DT * hv + (K + 1) * hv],
                                               h_xnT[:, K * TH2 + s0:K * TH2 + s0 + w],
                                               start=(K == 0), stop=(K == DT - 1))
                        return ins
                    p.step("tensor", bafn, reads=[b_wba, b_xnT], writes=ps_r)

                p.step("scalar", lambda e: e.activation(
                    h_gstg[:, :], psA[0:hv, 0:TH2], AF.Sigmoid, bias=0.0, scale=1.0),
                    reads=regions(psA, 0, TH2), writes=[b_gstg])
                p.dma("sync", lambda e, hsl=hsl: e.dma_start(
                    out=s_gates.ap()[2, :, hsl], in_=h_gstg[:, :]),
                    reads=[b_gstg], writes=[b_gates])
                p.step("scalar", lambda e: e.activation(
                    h_sp[:, 0:TH2], psA[0:hv, 0:TH2], AF.Abs, bias=0.0, scale=1.0),
                    reads=regions(psA, 0, TH2), writes=[b_sp])
                p.step("scalar", lambda e: e.activation(
                    h_sp[:, 0:TH2], h_sp[:, 0:TH2], AF.Exp, bias=0.0, scale=-1.0),
                    reads=[b_sp], writes=[b_sp])
                p.step("scalar", lambda e: e.activation(
                    h_sp[:, 0:TH2], h_sp[:, 0:TH2], AF.Ln, bias=h_bias[0:hv, 0:1], scale=1.0),
                    reads=[b_sp, b_bias], writes=[b_sp])
                p.step("vector", lambda e: e.tensor_scalar_min(
                    h_sp[:, TH2:2 * TH2], psA[0:hv, 0:TH2], 0.0),
                    reads=regions(psA, 0, TH2), writes=[b_sp])
                p.step("vector", lambda e: e.scalar_tensor_tensor(
                    h_gstg[:, :], h_sp[:, TH2:2 * TH2], 1.0, h_sp[:, 0:TH2], ALU.mult, ALU.subtract),
                    reads=[b_sp], writes=[b_gstg])
                p.dma("sync", lambda e, hsl=hsl: e.dma_start(
                    out=s_gates.ap()[1, :, hsl], in_=h_gstg[:, :]),
                    reads=[b_gstg], writes=[b_gates])
                p.step("scalar", lambda e: e.activation(
                    h_sp[:, 0:TH2], psB[0:hv, 0:TH2], AF.Abs, bias=h_dtb[:, 0:1], scale=1.0),
                    reads=regions(psB, 0, TH2) + [b_dtb], writes=[b_sp])
                p.step("scalar", lambda e: e.activation(
                    h_sp[:, 0:TH2], h_sp[:, 0:TH2], AF.Exp, bias=0.0, scale=-1.0),
                    reads=[b_sp], writes=[b_sp])
                p.step("scalar", lambda e: e.activation(
                    h_sp[:, 0:TH2], h_sp[:, 0:TH2], AF.Ln, bias=h_bias[0:hv, 0:1], scale=1.0),
                    reads=[b_sp, b_bias], writes=[b_sp])
                p.step("vector", lambda e: e.tensor_scalar(
                    h_sp[:, TH2:2 * TH2], psB[0:hv, 0:TH2], h_dtb[:, 0:1], 0.0, ALU.add, ALU.max),
                    reads=regions(psB, 0, TH2) + [b_dtb], writes=[b_sp])
                p.step("vector", lambda e: e.tensor_add(
                    h_sp[:, 0:TH2], h_sp[:, 0:TH2], h_sp[:, TH2:2 * TH2]),
                    reads=[b_sp], writes=[b_sp])
                p.step("vector", lambda e: e.tensor_scalar_mul(
                    h_gstg[:, :], h_sp[:, 0:TH2], h_negea[:, 0:1]),
                    reads=[b_sp, b_negea], writes=[b_gstg])
                p.dma("sync", lambda e, hsl=hsl: e.dma_start(
                    out=s_gates.ap()[0, :, hsl], in_=h_gstg[:, :]),
                    reads=[b_gstg], writes=[b_gates])
        p.barrier()

    if A:
        hg = HGC
        kg = hg // 2
        PG = hv // hg
        HWg = hg * 128
        NW = (tb - tf) // 128 if emode else 0
        GW = 128
        GC = GW // 128
        LN_HALF_DK = float(np.log(dk ** -0.5))

        with ExitStack() as ph:
            def psb(name, shape, dt=F32):
                h = ph.enter_context(nc.sbuf_tensor(name, list(shape), dt))
                return h, p.buf(name)

            def psb2(name, shape, dt=F32):
                hs, bs = [], []
                for i in range(2):
                    h, b = psb(f"{name}{i}", shape, dt)
                    hs.append(h)
                    bs.append(b)
                return hs, bs

            h_KKD, b_KKD = psb2("KKsb", [128, HWg], BF16)
            h_KQD, b_KQD = psb2("KQsb", [128, HWg], BF16)
            h_EtD, b_EtD = psb2("Etmp", [128, HWg], BF16)
            h_nD, b_nD = psb2("mat_n", [128, HWg], BF16)
            h_ntD, b_ntD = psb2("mat_nt", [128, HWg], BF16)
            h_LTD, b_LTD = psb2("mat_LT", [128, HWg], BF16)
            h_atD, b_atD = psb2("mat_at", [128, HWg], BF16)
            h_EbcD, b_EbcD = psb2("Ebc", [128, HWg], BF16)
            h_AqbD, b_AqbD = psb2("Aqb", [128, HWg], BF16)
            h_csbD, b_csbD = psb2("csb", [128, HWg], BF16)
            h_ACfbD, b_ACfbD = psb2("ACfb", [128, HWg], BF16)
            PWNAMES = ["nt2", "n2", "nt4", "n4", "nt8", "n8", "nt16", "n16"]
            PW = {nm: i for i, nm in enumerate(PWNAMES)}
            h_pwD, b_pwD = [], []
            for i in range(2):
                hl, bl = [], []
                for nm in PWNAMES:
                    hh, bb = psb(f"pw{i}_{nm}", [128, HWg], BF16)
                    hl.append(hh)
                    bl.append(bb)
                h_pwD.append(hl)
                b_pwD.append(bl)
            h_G0D, b_G0D = psb2("Gbuf0", [128, HWg], BF16)
            h_G1D, b_G1D = psb2("Gbuf1", [128, HWg], BF16)
            h_GtD, b_GtD = psb2("Gtmp", [128, HWg], BF16)
            h_DinvD, b_DinvD = psb2("Dinv", [128, HWg], BF16)
            h_RD, b_RD = psb2("Rstrip", [128, HWg], BF16)
            h_R2D, b_R2D = psb2("R2strip", [128, HWg], BF16)
            h_UD, b_UD = psb2("Ustrip", [128, HWg], BF16)
            h_bVD, b_bVD = psb2("bV", [128, HWg], BF16)
            h_oD, b_oD = psb2("ostrip", [128, HWg], F32)
            h_o3D, b_o3D = psb2("o3strip", [128, HWg], BF16)
            h_UscD, b_UscD = psb2("Usc", [128, HWg], BF16)
            h_SscD, b_SscD = psb2("Ssc", [128, HWg], BF16)
            h_bCfD, b_bCfD = psb2("bCf", [hg, 128], F32)
            h_bChD, b_bChD = psb2("bCh", [hg, 128], BF16)
            h_bClD, b_bClD = psb2("bCl", [hg, 128], BF16)
            h_ksqD, b_ksqD = psb2("ksq", [128, 2 * kg * 128], BF16)

            h_qkgD, b_qkgD = psb2("qkg", [128, 2 * hk * GW], BF16)
            h_vtD, b_vtD = psb2("vtok", [128, HWg], BF16)
            h_ktD, b_ktD = psb2("ktok", [128, kg * 128], BF16)
            h_ztD, b_ztD = psb2("ztok", [128, HWg], BF16)
            h_cmpAD, b_cmpAD = psb2("cmpA", [96, 128], F32)
            h_cTaD, b_cTaD = psb2("cTa", [128, 96], F32)
            h_cTkD, b_cTkD = psb2("cTk", [hg, 128], F32)
            h_cTqD, b_cTqD = psb2("cTq", [hg, 128], F32)
            h_hrowD, b_hrowD = psb2("hrow", [hg, 3 * 128], F32)
            h_hrBD, b_hrBD = psb2("hrB", [hg, 3 * 128], BF16)
            h_hrLD, b_hrLD = psb2("hrL", [hg, 3 * 128], BF16)
            h_colr6D, b_colr6D = psb2("colr6", [6, HWg], BF16)
            h_cexpD, b_cexpD = psb2("cexp", [128, 12 * hg], F32)
            h_gstg2D, b_gstg2D = psb2("gstg2", [hg, 256], F32)

            CE = {"lnks": 0, "lnqs": hg, "rstd": 9 * hg, "tmp": 10 * hg, "tmp2": 11 * hg}

            for i in range(2):
                p.step("vector", lambda e, i=i: e.memset(h_cmpAD[i][:, :], 0.0), writes=[b_cmpAD[i]])

            scripts = []

            class _Q:
                def __init__(self):
                    self.calls = []

                def step(self, *a, **k):
                    self.calls.append((0, a, k))

                def dma(self, *a, **k):
                    self.calls.append((1, a, k))

            for n_pg in range(NCH * PG):
                pq = _Q()
                n, pg = divmod(n_pg, PG)
                full = (not emode) or (n >= NW)
                par = n_pg % 2
                gpar = (n // GC) % 2
                gi = n % GC
                tok0 = gi * 128
                nsl = slice(n * 128, (n + 1) * 128)
                msl = slice((n - NW) * 128, (n - NW + 1) * 128)
                wf = 384 if full else 256
                kdep = 2 * (wf // 128)
                b_Sg = b_S[pg]
                Sg = h_S[:, pg * HWg:(pg + 1) * HWg]
                psX = psA if par == 0 else psB
                pslot = [0]
                nslots = max(1, PSW // HWg)

                def psoff():
                    off = (pslot[0] % nslots) * HWg
                    pslot[0] += 1
                    return off
                h_qkg, b_qkg = h_qkgD[gpar], b_qkgD[gpar]
                h_vt, b_vt = h_vtD[par], b_vtD[par]
                h_kt, b_kt = h_ktD[par], b_ktD[par]
                h_zt, b_zt = h_ztD[par], b_ztD[par]
                h_cmpA, b_cmpA = h_cmpAD[par], b_cmpAD[par]
                h_cTa, b_cTa = h_cTaD[par], b_cTaD[par]
                h_cTk, b_cTk = h_cTkD[par], b_cTkD[par]
                h_cTq, b_cTq = h_cTqD[par], b_cTqD[par]
                h_hrow, b_hrow = h_hrowD[par], b_hrowD[par]
                h_hrB, b_hrB = h_hrBD[par], b_hrBD[par]
                h_hrL, b_hrL = h_hrLD[par], b_hrLD[par]
                h_colr6, b_colr6 = h_colr6D[par], b_colr6D[par]
                h_cexp, b_cexp = h_cexpD[par], b_cexpD[par]
                h_gstg2, b_gstg2 = h_gstg2D[par], b_gstg2D[par]
                h_KK, b_KK = h_KKD[par], b_KKD[par]
                h_KQ, b_KQ = h_KQD[par], b_KQD[par]
                h_Et, b_Et = h_EtD[par], b_EtD[par]
                h_n, b_n = h_nD[par], b_nD[par]
                h_nt, b_nt = h_ntD[par], b_ntD[par]
                h_LT, b_LT = h_LTD[par], b_LTD[par]
                h_at, b_at = h_atD[par], b_atD[par]
                h_Ebc, b_Ebc = h_EbcD[par], b_EbcD[par]
                h_Aqb, b_Aqb = h_AqbD[par], b_AqbD[par]
                h_csb, b_csb = h_csbD[par], b_csbD[par]
                h_ACfb, b_ACfb = h_ACfbD[par], b_ACfbD[par]
                h_pw, b_pw = h_pwD[par], b_pwD[par]
                h_G0, b_G0 = h_G0D[par], b_G0D[par]
                h_G1, b_G1 = h_G1D[par], b_G1D[par]
                h_Gt, b_Gt = h_GtD[par], b_GtD[par]
                h_Dinv, b_Dinv = h_DinvD[par], b_DinvD[par]
                h_R, b_R = h_RD[par], b_RD[par]
                h_R2, b_R2 = h_R2D[par], b_R2D[par]
                h_U, b_U = h_UD[par], b_UD[par]
                h_bV, b_bV = h_bVD[par], b_bVD[par]
                h_o, b_o = h_oD[par], b_oD[par]
                h_o3, b_o3 = h_o3D[par], b_o3D[par]
                h_Usc, b_Usc = h_UscD[par], b_UscD[par]
                h_Ssc, b_Ssc = h_SscD[par], b_SscD[par]
                h_bCf, b_bCf = h_bCfD[par], b_bCfD[par]
                h_bCh, b_bCh = h_bChD[par], b_bChD[par]
                h_bCl, b_bCl = h_bClD[par], b_bClD[par]
                h_ksq, b_ksq = h_ksqD[par], b_ksqD[par]
                VT_BUF = {"n": (h_n, b_n), "nt": (h_nt, b_nt), "LT": (h_LT, b_LT), "at": (h_at, b_at)}
                b_hrw2 = b_hrw2D[par]

                def ce(nm, w=hg, _h=h_cexp):
                    c0 = CE[nm]
                    return _h[:, c0:c0 + w]

                def ce_col(nm, pp, _h=h_cexp):
                    c0 = CE[nm]
                    return _h[:, c0 + pp:c0 + pp + 1]

                qk3 = h_qkg[:, :].rearrange("p (t w) -> p t w", w=GW)

                def betaT_col(pp, _h=h_cTa):
                    return _h[:, 64 + pp:65 + pp]

                if gi == 0 and pg == 0:
                    g0 = n * 128
                    pq.dma("sync", lambda e, g0=g0, qk3=qk3: e.dma_start(
                        out=qk3,
                        in_=s_qkv.ap()[0:2 * hk * 128, g0:g0 + GW].rearrange("(t p) w -> p t w", p=128)),
                        reads=[b_qkv], writes=[b_qkg])

                pq.dma("sync", lambda e, nsl=nsl, pg=pg, h_vt=h_vt: e.dma_start_transpose(
                    h_vt[:, :], s_qkv.ap()[2 * hk * 128 + pg * HWg:2 * hk * 128 + (pg + 1) * HWg, nsl]),
                    reads=[b_qkv], writes=[b_vt])
                pq.dma("sync", lambda e, nsl=nsl, pg=pg, h_kt=h_kt: e.dma_start_transpose(
                    h_kt[:, :], s_qkv.ap()[(hk + pg * kg) * 128:(hk + (pg + 1) * kg) * 128, nsl]),
                    reads=[b_qkv], writes=[b_kt])
                if full:
                    zcols = nsl if not emode else msl
                    pq.dma("sync", lambda e, zcols=zcols, pg=pg, h_zt=h_zt: e.dma_start_transpose(
                        h_zt[:, :], s_zsil.ap()[pg * HWg:(pg + 1) * HWg, zcols]),
                        reads=[b_zsil], writes=[b_zt])

                def qtile(h, tok0=tok0, pg=pg, qk3=qk3):
                    return qk3[:, pg * kg + h, tok0:tok0 + 128]

                def ktile(h, tok0=tok0, pg=pg, qk3=qk3):
                    return qk3[:, hk + pg * kg + h, tok0:tok0 + 128]

                pq.step("vector", lambda e, tok0=tok0, pg=pg, qk3=qk3, h_ksq=h_ksq: e.tensor_mul(
                    h_ksq[:, 0:kg * 128].rearrange("p (t c) -> p t c", c=128),
                    qk3[:, pg * kg:(pg + 1) * kg, tok0:tok0 + 128],
                    qk3[:, pg * kg:(pg + 1) * kg, tok0:tok0 + 128]),
                    reads=[b_qkg], writes=[b_ksq])
                pq.step("vector", lambda e, tok0=tok0, pg=pg, qk3=qk3, h_ksq=h_ksq: e.tensor_mul(
                    h_ksq[:, kg * 128:2 * kg * 128].rearrange("p (t c) -> p t c", c=128),
                    qk3[:, hk + pg * kg:hk + (pg + 1) * kg, tok0:tok0 + 128],
                    qk3[:, hk + pg * kg:hk + (pg + 1) * kg, tok0:tok0 + 128]),
                    reads=[b_qkg], writes=[b_ksq])

                off0 = psoff()

                def ssqfn(e, h_ksq=h_ksq, psX=psX, off0=off0):
                    ins = None
                    for h in range(2 * kg):
                        ins = e.matmul(psX[:, off0 + h:off0 + h + 1], h_ksq[:, h * 128:(h + 1) * 128],
                                       h_onesb[:, :], start=True, stop=True)
                    return ins
                pq.step("tensor", ssqfn, reads=[b_ksq, b_onesb], writes=regions(psX, off0, 2 * kg))
                pq.step("scalar", lambda e, ce=ce, psX=psX, off0=off0: e.activation(
                    ce("tmp", 2 * kg), psX[:, off0:off0 + 2 * kg], AF.Ln, bias=h_bias[:, 1:2], scale=1.0),
                    reads=regions(psX, off0, 2 * kg) + [b_bias], writes=[b_cexp])
                lnq3 = h_cexp[:, CE["lnqs"]:CE["lnqs"] + hg].rearrange("p (a b) -> p a b", b=2)
                lnk3 = h_cexp[:, CE["lnks"]:CE["lnks"] + hg].rearrange("p (a b) -> p a b", b=2)
                for r in range(2):
                    pq.step("vector", lambda e, r=r, lnq3=lnq3, h_cexp=h_cexp: e.tensor_scalar(
                        lnq3[:, :, r:r + 1],
                        h_cexp[:, CE["tmp"]:CE["tmp"] + kg, None],
                        -0.5, LN_HALF_DK, ALU.mult, ALU.add),
                        reads=[b_cexp], writes=[b_cexp])
                    pq.step("vector", lambda e, r=r, lnk3=lnk3, h_cexp=h_cexp: e.tensor_scalar(
                        lnk3[:, :, r:r + 1],
                        h_cexp[:, CE["tmp"] + kg:CE["tmp"] + 2 * kg, None],
                        -0.5, 0.0, ALU.mult, ALU.add),
                        reads=[b_cexp], writes=[b_cexp])

                hsl_g = slice(pg * hg, (pg + 1) * hg)
                pq.dma("sync", lambda e, nsl=nsl, hsl_g=hsl_g, h_gstg2=h_gstg2: e.dma_start(
                    out=h_gstg2[:, 0:128], in_=s_gates.ap()[0, hsl_g, nsl]),
                    reads=[b_gates], writes=[b_gstg2])
                pq.dma("sync", lambda e, nsl=nsl, hsl_g=hsl_g, h_cmpA=h_cmpA: e.dma_start(
                    out=h_cmpA[32:32 + hg, :], in_=s_gates.ap()[1, hsl_g, nsl]),
                    reads=[b_gates], writes=[b_cmpA])
                pq.dma("sync", lambda e, nsl=nsl, hsl_g=hsl_g, h_gstg2=h_gstg2: e.dma_start(
                    out=h_gstg2[:, 128:256], in_=s_gates.ap()[1, hsl_g, nsl]),
                    reads=[b_gates], writes=[b_gstg2])
                pq.dma("sync", lambda e, nsl=nsl, hsl_g=hsl_g, h_cmpA=h_cmpA: e.dma_start(
                    out=h_cmpA[64:64 + hg, :], in_=s_gates.ap()[2, hsl_g, nsl]),
                    reads=[b_gates], writes=[b_cmpA])
                pq.step("vector", lambda e, h_cmpA=h_cmpA, h_gstg2=h_gstg2: e.tensor_tensor_scan(
                    h_cmpA[0:hg, :], h_gstg2[:, 0:128], h_gstg2[:, 0:128], 0.0, ALU.add, ALU.bypass),
                    reads=[b_gstg2], writes=[b_cmpA])
                offT = psoff()
                pq.step("tensor", lambda e, h_cmpA=h_cmpA, psX=psX, offT=offT: e.transpose(
                    psX[:, offT:offT + 96], h_cmpA[:, :], h_const_id_f[0:96, 0:96]),
                    reads=[b_cmpA, b_cidf], writes=regions(psX, offT, 96))
                pq.step("vector", lambda e, h_cTa=h_cTa, psX=psX, offT=offT: e.tensor_copy(
                    h_cTa[:, :], psX[:, offT:offT + 96]),
                    reads=regions(psX, offT, 96), writes=[b_cTa])
                offT = psoff()
                pq.step("tensor", lambda e, ce=ce, psX=psX, offT=offT: e.transpose(
                    psX[0:hg, offT:offT + 128], ce("lnks"), h_const_id_f[:, :]),
                    reads=[b_cexp, b_cidf], writes=regions(psX, offT, 128))
                pq.step("vector", lambda e, h_cTk=h_cTk, psX=psX, offT=offT: e.tensor_copy(
                    h_cTk[:, :], psX[0:hg, offT:offT + 128]),
                    reads=regions(psX, offT, 128), writes=[b_cTk])
                if full:
                    offT = psoff()
                    pq.step("tensor", lambda e, ce=ce, psX=psX, offT=offT: e.transpose(
                        psX[0:hg, offT:offT + 128], ce("lnqs"), h_const_id_f[:, :]),
                        reads=[b_cexp, b_cidf], writes=regions(psX, offT, 128))
                    pq.step("vector", lambda e, h_cTq=h_cTq, psX=psX, offT=offT: e.tensor_copy(
                        h_cTq[:, :], psX[0:hg, offT:offT + 128]),
                        reads=regions(psX, offT, 128), writes=[b_cTq])

                pq.step("vector", lambda e, h_hrow=h_hrow, h_cmpA=h_cmpA, h_gstg2=h_gstg2: e.tensor_add(
                    h_hrow[:, 0:128], h_cmpA[0:hg, :], h_gstg2[:, 128:256]),
                    reads=[b_cmpA, b_gstg2], writes=[b_hrow])
                pq.step("vector", lambda e, h_hrow=h_hrow, h_cTk=h_cTk: e.tensor_add(
                    h_hrow[:, 0:128], h_hrow[:, 0:128], h_cTk[:, :]),
                    reads=[b_hrow, b_cTk], writes=[b_hrow])
                pq.step("vector", lambda e, h_hrow=h_hrow, h_cmpA=h_cmpA, h_cTk=h_cTk: e.scalar_tensor_tensor(
                    h_hrow[:, 128:256], h_cmpA[0:hg, :], -1.0, h_cTk[:, :], ALU.mult, ALU.add),
                    reads=[b_cmpA, b_cTk], writes=[b_hrow])
                if full:
                    pq.step("vector", lambda e, h_hrow=h_hrow, h_cmpA=h_cmpA, h_cTq=h_cTq: e.tensor_add(
                        h_hrow[:, 256:384], h_cmpA[0:hg, :], h_cTq[:, :]),
                        reads=[b_cmpA, b_cTq], writes=[b_hrow])

                pq.step("vector", lambda e, h_hrB=h_hrB, h_hrow=h_hrow, wf=wf: e.tensor_copy(
                    h_hrB[:, 0:wf], h_hrow[:, 0:wf]),
                    reads=[b_hrow], writes=[b_hrB])
                pq.step("vector", lambda e, h_hrL=h_hrL, h_hrow=h_hrow, h_hrB=h_hrB, wf=wf: e.scalar_tensor_tensor(
                    h_hrL[:, 0:wf], h_hrow[:, 0:wf], 1.0, h_hrB[:, 0:wf], ALU.mult, ALU.subtract),
                    reads=[b_hrow, b_hrB], writes=[b_hrL])
                pq.step("vector", lambda e, h_cmpA=h_cmpA, h_bCf=h_bCf: e.tensor_scalar_mul(
                    h_bCf[:, :], h_oneshv[:, :], h_cmpA[0:hg, 127:128]),
                    reads=[b_oneshv, b_cmpA], writes=[b_bCf])
                pq.step("vector", lambda e, h_bCh=h_bCh, h_bCf=h_bCf: e.tensor_copy(h_bCh[:, :], h_bCf[:, :]),
                       reads=[b_bCf], writes=[b_bCh])
                pq.step("vector", lambda e, h_bCl=h_bCl, h_bCf=h_bCf, h_bCh=h_bCh: e.scalar_tensor_tensor(
                    h_bCl[:, :], h_bCf[:, :], 1.0, h_bCh[:, :], ALU.mult, ALU.subtract),
                    reads=[b_bCf, b_bCh], writes=[b_bCl])

                for s in range(wf // 128):
                    pq.dma("sync", lambda e, par=par, s=s, h_hrB=h_hrB: e.dma_start(
                        out=s_hrow2.ap()[par, s, 0], in_=h_hrB[:, s * 128:(s + 1) * 128]),
                        reads=[b_hrB], writes=[b_hrw2])
                    pq.dma("sync", lambda e, par=par, s=s, h_hrL=h_hrL: e.dma_start(
                        out=s_hrow2.ap()[par, s, 1], in_=h_hrL[:, s * 128:(s + 1) * 128]),
                        reads=[b_hrL], writes=[b_hrw2])
                pq.dma("sync", lambda e, par=par, h_colr6=h_colr6, wf=wf, kdep=kdep: e.dma_start(
                    out=h_colr6[0:kdep, :],
                    in_=s_hrow2.ap()[par, 0:wf // 128].rearrange("s r h c -> (s r) (h c)")),
                    reads=[b_hrw2], writes=[b_colr6])

                offK = psoff()

                def kkfn(e, ktile=ktile, psX=psX, offK=offK):
                    ins = None
                    for pp in range(hg):
                        ins = e.matmul(psX[:, offK + pp * 128:offK + (pp + 1) * 128],
                                       ktile(pp // 2), ktile(pp // 2), start=True, stop=True)
                    return ins
                pq.step("tensor", kkfn, reads=[b_qkg], writes=regions(psX, offK, HWg))
                pq.step("scalar", lambda e, h_KK=h_KK, psX=psX, offK=offK: e.activation(
                    h_KK[:, :], psX[:, offK:offK + HWg], AF.Copy, bias=0.0, scale=1.0),
                    reads=regions(psX, offK, HWg), writes=[b_KK])

                if full:
                    offQ = psoff()

                    def kqfn(e, ktile=ktile, qtile=qtile, psX=psX, offQ=offQ):
                        ins = None
                        for pp in range(hg):
                            ins = e.matmul(psX[:, offQ + pp * 128:offQ + (pp + 1) * 128],
                                           ktile(pp // 2), qtile(pp // 2), start=True, stop=True)
                        return ins
                    pq.step("tensor", kqfn, reads=[b_qkg], writes=regions(psX, offQ, HWg))
                    pq.step("scalar", lambda e, h_KQ=h_KQ, psX=psX, offQ=offQ: e.activation(
                        h_KQ[:, :], psX[:, offQ:offQ + HWg], AF.Copy, bias=0.0, scale=1.0),
                        reads=regions(psX, offQ, HWg), writes=[b_KQ])

                def rowbc(e, ps, off, blks, useC, h_hrB=h_hrB, h_hrL=h_hrL, h_bCh=h_bCh, h_bCl=h_bCl):
                    ins = None
                    for s0, w in col_strips(HWg):
                        ops = []
                        for roff in blks:
                            ops.append(h_hrB[:, roff:roff + 128])
                            ops.append(h_hrL[:, roff:roff + 128])
                        if useC:
                            ops.append(h_bCh[:, :])
                            ops.append(h_bCl[:, :])
                        for i, lt in enumerate(ops):
                            ins = e.matmul(ps[:, off + s0:off + s0 + w], lt, h_selb[:, s0:s0 + w],
                                           start=(i == 0), stop=(i == len(ops) - 1))
                    return ins

                for dst_h, dst_b, blks, useC, cond in [
                    (h_Ebc, b_Ebc, [0], False, True),
                    (h_csb, b_csb, [128], True, True),
                    (h_ACfb, b_ACfb, [], True, True),
                    (h_Aqb, b_Aqb, [256], False, full),
                ]:
                    if not cond:
                        continue
                    offE = psoff()
                    pq.step("tensor", lambda e, rowbc=rowbc, psX=psX, offE=offE, blks=blks, useC=useC:
                           rowbc(e, psX, offE, blks, useC),
                           reads=[b_hrB, b_hrL, b_bCh, b_bCl, b_selb], writes=regions(psX, offE, HWg))
                    pq.step("scalar", lambda e, dst_h=dst_h, psX=psX, offE=offE: e.activation(
                        dst_h[:, :], psX[:, offE:offE + HWg], AF.Exp, bias=0.0, scale=1.0),
                        reads=regions(psX, offE, HWg), writes=[dst_b])

                variants = [
                    (0, 1, 0, -1.0, "n", h_KK, b_KK),
                    (128, 0, 1, -1.0, "nt", h_KK, b_KK),
                    (128, 0, 2, 1.0, "LT", h_KK, b_KK),
                    (128, 2, 3, 1.0, "at", h_KQ, b_KQ),
                ]
                for roff, sblk, mi, sgn, vname, h_kk, b_kk in variants:
                    if (not full) and vname == "at":
                        continue
                    h_tgt, b_tgt = VT_BUF[vname]
                    offV = psoff()

                    def dfn(e, roff=roff, sblk=sblk, mi=mi, psX=psX, offV=offV, kdep=kdep,
                            h_hrB=h_hrB, h_hrL=h_hrL, h_colr6=h_colr6):
                        ins = None
                        for s0, w in col_strips(HWg):
                            ins = e.matmul(psX[:, offV + s0:offV + s0 + w], h_hrB[:, roff:roff + 128],
                                           h_selb[:, s0:s0 + w], start=True, stop=False)
                            ins = e.matmul(psX[:, offV + s0:offV + s0 + w], h_hrL[:, roff:roff + 128],
                                           h_selb[:, s0:s0 + w], start=False, stop=False)
                            ins = e.matmul(psX[:, offV + s0:offV + s0 + w], h_colsel[0:kdep, sblk * 128:(sblk + 1) * 128],
                                           h_colr6[0:kdep, s0:s0 + w], start=False, stop=False)
                            ins = e.matmul(psX[:, offV + s0:offV + s0 + w], h_masksb[:, mi * 128:(mi + 1) * 128],
                                           h_repid_b[:, s0:s0 + w], start=False, stop=True)
                        return ins
                    pq.step("tensor", dfn,
                           reads=[b_hrB, b_hrL, b_selb, b_colsel, b_colr6, b_masksb, b_repidb],
                           writes=regions(psX, offV, HWg))
                    pq.step("scalar", lambda e, h_Et=h_Et, psX=psX, offV=offV: e.activation(
                        h_Et[:, :], psX[:, offV:offV + HWg], AF.Exp, bias=0.0, scale=1.0),
                        reads=regions(psX, offV, HWg), writes=[b_Et])
                    pq.step("vector", lambda e, h_tgt=h_tgt, h_kk=h_kk, sgn=sgn, h_Et=h_Et: e.scalar_tensor_tensor(
                        h_tgt[:, :], h_Et[:, :], sgn, h_kk[:, :], ALU.mult, ALU.mult),
                        reads=[b_Et, b_kk], writes=[b_tgt])

                sq_plan = [
                    ("nt2", h_n, b_n, h_nt, b_nt),
                    ("n2", h_nt, b_nt, h_n, b_n),
                    ("nt4", h_pw[PW["n2"]], b_pw[PW["n2"]], h_pw[PW["nt2"]], b_pw[PW["nt2"]]),
                    ("n4", h_pw[PW["nt2"]], b_pw[PW["nt2"]], h_pw[PW["n2"]], b_pw[PW["n2"]]),
                    ("nt8", h_pw[PW["n4"]], b_pw[PW["n4"]], h_pw[PW["nt4"]], b_pw[PW["nt4"]]),
                    ("n8", h_pw[PW["nt4"]], b_pw[PW["nt4"]], h_pw[PW["n4"]], b_pw[PW["n4"]]),
                    ("nt16", h_pw[PW["n8"]], b_pw[PW["n8"]], h_pw[PW["nt8"]], b_pw[PW["nt8"]]),
                    ("n16", h_pw[PW["nt8"]], b_pw[PW["nt8"]], h_pw[PW["n8"]], b_pw[PW["n8"]]),
                ]
                for nm, hl, bl, hr, br in sq_plan:
                    offS = psoff()

                    def sqfn(e, hl=hl, hr=hr, psX=psX, offS=offS):
                        ins = None
                        for pp in range(hg):
                            sl2 = slice(pp * 128, (pp + 1) * 128)
                            ins = e.matmul(psX[:, offS + pp * 128:offS + (pp + 1) * 128],
                                           hl[:, sl2], hr[:, sl2], start=True, stop=True)
                        return ins
                    pq.step("tensor", sqfn, reads=[bl, br], writes=regions(psX, offS, HWg))
                    pq.step("scalar", lambda e, nm=nm, psX=psX, offS=offS, h_pw=h_pw: e.activation(
                        h_pw[PW[nm]][:, :], psX[:, offS:offS + HWg], AF.Copy, bias=0.0, scale=1.0),
                        reads=regions(psX, offS, HWg), writes=[b_pw[PW[nm]]])

                offG = psoff()

                def g1mm(e, h_nt=h_nt, h_pw=h_pw, psX=psX, offG=offG):
                    ins = None
                    for pp in range(hg):
                        sl2 = slice(pp * 128, (pp + 1) * 128)
                        ins = e.matmul(psX[:, offG + pp * 128:offG + (pp + 1) * 128],
                                       h_pw[PW["n2"]][:, sl2], h_nt[:, sl2], start=True, stop=True)
                    return ins
                pq.step("tensor", g1mm, reads=[b_pw[PW["n2"]], b_nt], writes=regions(psX, offG, HWg))
                pq.step("vector", lambda e, h_Gt=h_Gt, h_pw=h_pw, psX=psX, offG=offG: e.scalar_tensor_tensor(
                    h_Gt[:, :], psX[:, offG:offG + HWg], 1.0, h_pw[PW["nt2"]][:, :], ALU.mult, ALU.add),
                    reads=regions(psX, offG, HWg) + [b_pw[PW["nt2"]]], writes=[b_Gt])
                pq.step("gpsimd", lambda e, h_nt=h_nt, h_G0=h_G0, h_Gt=h_Gt: e.tensor_add(
                    h_G0[:, :], h_Gt[:, :], h_nt[:, :]),
                    reads=[b_Gt, b_nt], writes=[b_G0])
                gsrc, gsrcb = h_G0, b_G0
                gdst, gdstb = h_G1, b_G1
                for lv in ["4", "8", "16"]:
                    nlv, ntlv = PW["n" + lv], PW["nt" + lv]
                    offG = psoff()

                    def glmm(e, nlv=nlv, gsrc=gsrc, h_pw=h_pw, psX=psX, offG=offG):
                        ins = None
                        for pp in range(hg):
                            sl2 = slice(pp * 128, (pp + 1) * 128)
                            ins = e.matmul(psX[:, offG + pp * 128:offG + (pp + 1) * 128],
                                           h_pw[nlv][:, sl2], gsrc[:, sl2], start=True, stop=True)
                        return ins
                    pq.step("tensor", glmm, reads=[b_pw[nlv], gsrcb], writes=regions(psX, offG, HWg))
                    pq.step("vector", lambda e, ntlv=ntlv, h_Gt=h_Gt, h_pw=h_pw, psX=psX, offG=offG: e.scalar_tensor_tensor(
                        h_Gt[:, :], psX[:, offG:offG + HWg], 1.0, h_pw[ntlv][:, :], ALU.mult, ALU.add),
                        reads=regions(psX, offG, HWg) + [b_pw[ntlv]], writes=[b_Gt])
                    pq.step("gpsimd", lambda e, gdst=gdst, gsrc=gsrc, h_Gt=h_Gt: e.tensor_add(
                        gdst[:, :], h_Gt[:, :], gsrc[:, :]),
                        reads=[b_Gt, gsrcb], writes=[gdstb])
                    gsrc, gsrcb, gdst, gdstb = gdst, gdstb, gsrc, gsrcb
                pq.step("gpsimd", lambda e, gsrc=gsrc, h_Dinv=h_Dinv: e.tensor_add(
                    h_Dinv[:, :], gsrc[:, :], h_repid_b[:, :]),
                    reads=[gsrcb, b_repidb], writes=[b_Dinv])

                offR = psoff()

                def ks0fn(e, ktile=ktile, Sg=Sg, psX=psX, offR=offR):
                    ins = None
                    for pp in range(hg):
                        ins = e.matmul(psX[:, offR + pp * 128:offR + (pp + 1) * 128], ktile(pp // 2),
                                       Sg[:, pp * 128:(pp + 1) * 128], start=True, stop=True)
                    return ins
                pq.step("tensor", ks0fn, reads=[b_qkg, b_Sg], writes=regions(psX, offR, HWg))
                for pp in range(hg):
                    pq.step("scalar", lambda e, pp=pp, h_vt=h_vt, h_bV=h_bV, betaT_col=betaT_col: e.activation(
                        h_bV[:, pp * 128:(pp + 1) * 128], h_vt[:, pp * 128:(pp + 1) * 128],
                        AF.Copy, bias=0.0, scale=betaT_col(pp)),
                        reads=[b_vt, b_cTa], writes=[b_bV])
                pq.step("vector", lambda e, h_R2=h_R2, h_Ebc=h_Ebc, psX=psX, offR=offR: e.tensor_mul(
                    h_R2[:, :], h_Ebc[:, :], psX[:, offR:offR + HWg]),
                    reads=[b_Ebc] + regions(psX, offR, HWg), writes=[b_R2])
                pq.step("vector", lambda e, h_R=h_R, h_R2=h_R2, h_bV=h_bV: e.scalar_tensor_tensor(
                    h_R[:, :], h_R2[:, :], -1.0, h_bV[:, :], ALU.mult, ALU.add),
                    reads=[b_R2, b_bV], writes=[b_R])

                def dinvfn(e, src, psX, off, h_Dinv=h_Dinv):
                    ins = None
                    for pp in range(hg):
                        sl2 = slice(pp * 128, (pp + 1) * 128)
                        ins = e.matmul(psX[:, off + pp * 128:off + (pp + 1) * 128],
                                       h_Dinv[:, sl2], src[:, sl2], start=True, stop=True)
                    return ins
                offU = psoff()
                pq.step("tensor", lambda e, dinvfn=dinvfn, h_R=h_R, psX=psX, offU=offU: dinvfn(e, h_R, psX, offU),
                       reads=[b_Dinv, b_R], writes=regions(psX, offU, HWg))
                pq.step("scalar", lambda e, h_U=h_U, psX=psX, offU=offU: e.activation(
                    h_U[:, :], psX[:, offU:offU + HWg], AF.Copy, bias=0.0, scale=1.0),
                    reads=regions(psX, offU, HWg), writes=[b_U])
                for it in range(3):
                    offL = psoff()

                    def ltfn(e, h_LT=h_LT, h_U=h_U, psX=psX, offL=offL):
                        ins = None
                        for pp in range(hg):
                            sl2 = slice(pp * 128, (pp + 1) * 128)
                            ins = e.matmul(psX[:, offL + pp * 128:offL + (pp + 1) * 128],
                                           h_LT[:, sl2], h_U[:, sl2], start=True, stop=True)
                        return ins
                    pq.step("tensor", ltfn, reads=[b_LT, b_U], writes=regions(psX, offL, HWg))
                    pq.step("vector", lambda e, h_R2=h_R2, h_R=h_R, psX=psX, offL=offL: e.scalar_tensor_tensor(
                        h_R2[:, :], psX[:, offL:offL + HWg], -1.0, h_R[:, :], ALU.mult, ALU.add),
                        reads=regions(psX, offL, HWg) + [b_R], writes=[b_R2])
                    offU = psoff()
                    pq.step("tensor", lambda e, dinvfn=dinvfn, h_R2=h_R2, psX=psX, offU=offU: dinvfn(e, h_R2, psX, offU),
                           reads=[b_Dinv, b_R2], writes=regions(psX, offU, HWg))
                    pq.step("scalar", lambda e, h_U=h_U, psX=psX, offU=offU: e.activation(
                        h_U[:, :], psX[:, offU:offU + HWg], AF.Copy, bias=0.0, scale=1.0),
                        reads=regions(psX, offU, HWg), writes=[b_U])

                if full:
                    offO = psoff()
                    if nslots >= 2:
                        psAt, offA = psX, psoff()
                    else:
                        psAt, offA = (psB if par == 0 else psA), 0

                    def qs0fn(e, qtile=qtile, Sg=Sg, psX=psX, offO=offO):
                        ins = None
                        for pp in range(hg):
                            ins = e.matmul(psX[:, offO + pp * 128:offO + (pp + 1) * 128], qtile(pp // 2),
                                           Sg[:, pp * 128:(pp + 1) * 128], start=True, stop=True)
                        return ins
                    pq.step("tensor", qs0fn, reads=[b_qkg, b_Sg], writes=regions(psX, offO, HWg))

                    def attufn(e, h_at=h_at, h_U=h_U, psAt=psAt, offA=offA):
                        ins = None
                        for pp in range(hg):
                            sl2 = slice(pp * 128, (pp + 1) * 128)
                            ins = e.matmul(psAt[:, offA + pp * 128:offA + (pp + 1) * 128],
                                           h_at[:, sl2], h_U[:, sl2], start=True, stop=True)
                        return ins
                    pq.step("tensor", attufn, reads=[b_at, b_U], writes=regions(psAt, offA, HWg))
                    pq.step("vector", lambda e, h_o=h_o, h_Aqb=h_Aqb, psX=psX, offO=offO: e.tensor_mul(
                        h_o[:, :], h_Aqb[:, :], psX[:, offO:offO + HWg]),
                        reads=[b_Aqb] + regions(psX, offO, HWg), writes=[b_o])
                    pq.step("vector", lambda e, h_o=h_o, psAt=psAt, offA=offA: e.tensor_add(
                        h_o[:, :], h_o[:, :], psAt[:, offA:offA + HWg]),
                        reads=[b_o] + regions(psAt, offA, HWg), writes=[b_o])

                    pq.step("scalar", lambda e, h_bV=h_bV, h_o=h_o: e.activation(
                        h_bV[:, :], h_o[:, :], AF.Square, bias=0.0, scale=1.0),
                        reads=[b_o], writes=[b_bV])
                    pq.step("vector", lambda e, ce=ce, h_bV=h_bV: e.tensor_reduce(
                        ce("tmp"), h_bV[:, :].rearrange("p (t c) -> p t c", c=128), AX.X, ALU.add),
                        reads=[b_bV], writes=[b_cexp])
                    pq.step("vector", lambda e, ce=ce: e.tensor_scalar(
                        ce("tmp"), ce("tmp"), 1.0 / dv, 1e-6, ALU.mult, ALU.add),
                        reads=[b_cexp], writes=[b_cexp])
                    pq.step("vector", lambda e, ce=ce: e.reciprocal(ce("tmp2"), ce("tmp")),
                           reads=[b_cexp], writes=[b_cexp])
                    pq.step("scalar", lambda e, ce=ce: e.activation(ce("rstd"), ce("tmp2"), AF.Sqrt, bias=0.0, scale=1.0),
                           reads=[b_cexp], writes=[b_cexp])
                    for pp in range(hg):
                        pq.step("scalar", lambda e, pp=pp, ce_col=ce_col, h_o3=h_o3, h_o=h_o: e.activation(
                            h_o3[:, pp * 128:(pp + 1) * 128], h_o[:, pp * 128:(pp + 1) * 128],
                            AF.Copy, bias=0.0, scale=ce_col("rstd", pp)),
                            reads=[b_o, b_cexp], writes=[b_o3])
                    pq.step("vector", lambda e, h_zt=h_zt, h_o3=h_o3: e.tensor_mul(
                        h_o3[:, :], h_o3[:, :], h_zt[:, :]),
                        reads=[b_o3, b_zt], writes=[b_o3])
                    osl = msl if emode else nsl
                    pq.dma("sync", lambda e, osl=osl, pg=pg, h_o3=h_o3: e.dma_start(
                        out=s_o3.ap()[osl, pg * HWg:(pg + 1) * HWg], in_=h_o3[:, :]),
                        reads=[b_o3], writes=[b_o3scr])

                pq.step("vector", lambda e, h_Usc=h_Usc, h_U=h_U, h_csb=h_csb: e.tensor_mul(
                    h_Usc[:, :], h_U[:, :], h_csb[:, :]),
                    reads=[b_U, b_csb], writes=[b_Usc])
                pq.step("vector", lambda e, Sg=Sg, h_Ssc=h_Ssc, h_ACfb=h_ACfb: e.tensor_mul(
                    h_Ssc[:, :], Sg[:, :], h_ACfb[:, :]),
                    reads=[b_Sg, b_ACfb], writes=[b_Ssc])
                offZ = psoff()

                def statefn(e, h_kt=h_kt, h_Usc=h_Usc, psX=psX, offZ=offZ):
                    ins = None
                    for pp in range(hg):
                        ins = e.matmul(psX[:, offZ + pp * 128:offZ + (pp + 1) * 128],
                                       h_kt[:, (pp // 2) * 128:(pp // 2 + 1) * 128],
                                       h_Usc[:, pp * 128:(pp + 1) * 128], start=True, stop=True)
                    return ins
                pq.step("tensor", statefn, reads=[b_kt, b_Usc], writes=regions(psX, offZ, HWg))
                pq.step("vector", lambda e, Sg=Sg, h_Ssc=h_Ssc, psX=psX, offZ=offZ: e.scalar_tensor_tensor(
                    Sg[:, :], psX[:, offZ:offZ + HWg], 1.0, h_Ssc[:, :], ALU.mult, ALU.add),
                    reads=regions(psX, offZ, HWg) + [b_Ssc], writes=[b_Sg])
                if DBG_DELTA:
                    taps = [(h_Ssc, b_Ssc), (h_ACfb, b_ACfb), (h_csb, b_csb)]
                    if full:
                        taps += [(h_o, b_o), (h_o3, b_o3), (h_Aqb, b_Aqb)]
                    for di, (h_d, b_d) in enumerate(taps):
                        pq.dma("gpsimd", lambda e, n_pg=n_pg, di=di, h_d=h_d: e.dma_start(
                            out=s_dbg.ap()[n_pg, di, :, 0:HWg], in_=h_d[:, :]),
                            reads=[b_d], writes=[b_dbg])
                scripts.append(deque(pq.calls))

            if max(1, PSW // HWg) < 2:
                for c_ in scripts:
                    for kind, a_, k_ in c_:
                        (p.step if kind == 0 else p.dma)(*a_, **k_)
            else:
                pend = deque(scripts)
                cur = []
                while pend or cur:
                    while len(cur) < 2 and pend:
                        cur.append(pend.popleft())
                    for c_ in list(cur):
                        if c_:
                            kind, a_, k_ = c_.popleft()
                            (p.step if kind == 0 else p.dma)(*a_, **k_)
                        if not c_:
                            cur.remove(c_)
        p.barrier()

    if A:
        with ExitStack() as ph:
            h_wo = ph.enter_context(nc.sbuf_tensor("wo", [128, VT * d], BF16))
            b_wo = p.buf("wo")
            h_o3T = ph.enter_context(nc.sbuf_tensor("o3T", [128, VT * 512], BF16))
            b_o3T = p.buf("o3T")
            h_sev = ph.enter_context(nc.sbuf_tensor("sev", [128, 512], BF16))
            b_sev = p.buf("sev")

            for vt in range(VT):
                p.dma("gpsimd", lambda e, vt=vt: e.dma_start(
                    out=h_wo[:, vt * d:(vt + 1) * d],
                    in_=t_wout.ap()[vt * 128:(vt + 1) * 128, :]), writes=[b_wo])
                p.step("vector", lambda e, vt=vt: e.tensor_scalar_mul(
                    h_wo[:, vt * d:(vt + 1) * d], h_wo[:, vt * d:(vt + 1) * d], h_normw[:, 0:1]),
                    reads=[b_wo, b_normw], writes=[b_wo])

            TWOUT = tf if emode else tb
            for t0, tw in col_strips(TWOUT):
                for vt in range(VT):
                    p.dma("sync", lambda e, t0=t0, tw=tw, vt=vt: e.dma_start_transpose(
                        h_o3T[:, vt * 512:vt * 512 + tw],
                        s_o3.ap()[t0:t0 + tw, vt * 128:(vt + 1) * 128]),
                        reads=[b_o3scr], writes=[b_o3T])
                for tt in range(tw // 128):
                    for oc0, ow in col_strips(d):
                        ps = psA if (tt + oc0 // 512) % 2 == 0 else psB

                        def smm(e, ps=ps, tt=tt, oc0=oc0, ow=ow):
                            ins = None
                            for vt in range(VT):
                                ins = e.matmul(ps[:, 0:ow],
                                               h_o3T[:, vt * 512 + tt * 128:vt * 512 + (tt + 1) * 128],
                                               h_wo[:, vt * d + oc0:vt * d + oc0 + ow],
                                               start=(vt == 0), stop=(vt == VT - 1))
                            return ins
                        p.step("tensor", smm, reads=[b_o3T, b_wo], writes=regions(ps, 0, ow))
                        p.step("vector", lambda e, ps=ps, ow=ow: e.tensor_copy(h_sev[:, 0:ow], ps[:, 0:ow]),
                               reads=regions(ps, 0, ow), writes=[b_sev])
                        wdst = s_sred if emode else s_spart
                        wdstb = b_sred if emode else b_spart
                        p.dma("sync", lambda e, t0=t0, tt=tt, oc0=oc0, ow=ow, wdst=wdst: e.dma_start(
                            out=wdst.ap()[t0 + tt * 128:t0 + (tt + 1) * 128, oc0:oc0 + ow],
                            in_=h_sev[:, 0:ow]),
                            reads=[b_sev], writes=[wdstb])

        if part == "full" and not emode:
            p.step("gpsimd", lambda e: e.collective_compute(
                "ReduceScatter", ALU.add, replica_groups=cfg["pair_groups"],
                ins=[s_spart.ap().opt()], outs=[s_sred.ap().opt()]),
                reads=[b_spart], writes=[b_sred], semkey="cc", inc=16)
        p.barrier()

    if part == "ffn":
        with ExitStack() as ph:
            h_sa = ph.enter_context(nc.sbuf_tensor("ssum_a", [128, d], BF16))
            b_sa = p.buf("ssum_a")
            h_sb = ph.enter_context(nc.sbuf_tensor("ssum_b", [128, d], BF16))
            b_sb = p.buf("ssum_b")
            h_so = ph.enter_context(nc.sbuf_tensor("ssum_o", [128, d], BF16))
            b_so = p.buf("ssum_o")
            for i in range(tf // 128):
                isl = slice(i * 128, (i + 1) * 128)
                p.dma("sync", lambda e, isl=isl: e.dma_start(out=h_sa[:, :], in_=t_s0.ap()[isl, :]),
                      writes=[b_sa])
                p.dma("sync", lambda e, isl=isl: e.dma_start(out=h_sb[:, :], in_=t_s1.ap()[isl, :]),
                      writes=[b_sb])
                p.step("vector", lambda e: e.tensor_add(h_so[:, :], h_sa[:, :], h_sb[:, :]),
                       reads=[b_sa, b_sb], writes=[b_so])
                p.dma("sync", lambda e, isl=isl: e.dma_start(out=s_sred.ap()[isl, :], in_=h_so[:, :]),
                      reads=[b_so], writes=[b_sred])
        p.barrier()

    if F:
        TS_R = tf // 128
        with ExitStack() as ph:
            h_xfT = ph.enter_context(nc.sbuf_tensor("xfT", [128, DT * tf], BF16))
            b_xfT = p.buf("xfT")
            h_wrtb = ph.enter_context(nc.sbuf_tensor("wrtb", [128, DT * 8], BF16))
            b_wrtb = p.buf("wrtb")
            h_wsgb = ph.enter_context(nc.sbuf_tensor("wsgb", [128, DT], BF16))
            b_wsgb = p.buf("wsgb")
            h_comb = ph.enter_context(nc.sbuf_tensor("comb", [128, 32], F32))
            b_comb = p.buf("comb")
            h_cT = ph.enter_context(nc.sbuf_tensor("cT", [9, tf], F32))
            b_cT = p.buf("cT")
            h_wg = [ph.enter_context(nc.sbuf_tensor(f"wgu{i}", [128, DT * 128], BF16)) for i in range(2)]
            b_wg = [p.buf(f"wgu{i}") for i in range(2)]
            h_wu = [ph.enter_context(nc.sbuf_tensor(f"wup{i}", [128, DT * 128], BF16)) for i in range(2)]
            b_wu = [p.buf(f"wup{i}") for i in range(2)]
            h_gs = ph.enter_context(nc.sbuf_tensor("gsil", [128, tf], BF16))
            b_gs = p.buf("gsil")
            h_hb = ph.enter_context(nc.sbuf_tensor("hbuf", [128, tf], BF16))
            b_hb = p.buf("hbuf")
            h_Bc = ph.enter_context(nc.sbuf_tensor("Bc", [128, tf], BF16))
            b_Bc = p.buf("Bc")
            h_crow1g = ph.enter_context(nc.sbuf_tensor("crow1g", [1, tf], F32))
            b_crow1g = p.buf("crow1g")

            for K in range(DT):
                p.dma("sync", lambda e, K=K: e.dma_start_transpose(
                    h_xfT[:, K * tf:(K + 1) * tf],
                    s_xnf.ap()[:, K * 128:(K + 1) * 128]),
                    reads=[b_xnf], writes=[b_xfT])
            p.dma("gpsimd", lambda e: e.dma_start(
                out=h_wrtb[:, :].rearrange("p (k c) -> p k c", c=8),
                in_=t_wrt.ap().rearrange("(k p) c -> p k c", p=128)), writes=[b_wrtb])
            p.dma("gpsimd", lambda e: e.dma_start(
                out=h_wsgb[:, :].rearrange("p (k c) -> p k c", c=1),
                in_=t_wsg.ap().rearrange("(k p) c -> p k c", p=128),
                allow_slow_non_contiguous=True), writes=[b_wsgb])

            for ti in range(TS_R):
                def rtfn(e, ti=ti):
                    ins = None
                    for K in range(DT):
                        ins = e.matmul(psA[:, 0:8], h_xfT[:, K * tf + ti * 128:K * tf + (ti + 1) * 128],
                                       h_wrtb[:, K * 8:(K + 1) * 8], start=(K == 0), stop=(K == DT - 1))
                    for K in range(DT):
                        ins = e.matmul(psA[:, 8:9], h_xfT[:, K * tf + ti * 128:K * tf + (ti + 1) * 128],
                                       h_wsgb[:, K:K + 1], start=(K == 0), stop=(K == DT - 1))
                    return ins
                p.step("tensor", rtfn, reads=[b_xfT, b_wrtb, b_wsgb], writes=[psA_r[0]])
                p.step("vector", lambda e: e.reduce_max(h_comb[:, 16:17], psA[:, 0:8], AX.X),
                       reads=[psA_r[0]], writes=[b_comb])
                p.step("vector", lambda e: e.scalar_tensor_tensor(
                    h_comb[:, 0:8], psA[:, 0:8], 1.0, h_comb[:, 16:17].broadcast_to([128, 8]), ALU.mult, ALU.subtract),
                    reads=[psA_r[0], b_comb], writes=[b_comb])
                p.step("scalar", lambda e: e.activation(h_comb[:, 0:8], h_comb[:, 0:8], AF.Exp, bias=0.0, scale=1.0),
                       reads=[b_comb], writes=[b_comb])
                p.step("vector", lambda e: e.reduce_sum(h_comb[:, 17:18], h_comb[:, 0:8], AX.X),
                       reads=[b_comb], writes=[b_comb])
                p.step("vector", lambda e: e.reciprocal(h_comb[:, 18:19], h_comb[:, 17:18]),
                       reads=[b_comb], writes=[b_comb])
                p.step("vector", lambda e: e.tensor_scalar_mul(h_comb[:, 0:8], h_comb[:, 0:8], h_comb[:, 18:19]),
                       reads=[b_comb], writes=[b_comb])
                p.step("vector", lambda e: e.reduce_max(h_comb[:, 19:20], h_comb[:, 0:8], AX.X),
                       reads=[b_comb], writes=[b_comb])
                p.step("vector", lambda e: e.tensor_scalar(
                    h_comb[:, 8:16], h_comb[:, 0:8], h_comb[:, 19:20], None, ALU.is_ge),
                    reads=[b_comb], writes=[b_comb])
                p.step("vector", lambda e: e.scalar_tensor_tensor(
                    h_comb[:, 0:8], h_comb[:, 8:16], NEG_BIG, h_comb[:, 0:8], ALU.mult, ALU.add),
                    reads=[b_comb], writes=[b_comb])
                p.step("vector", lambda e: e.reduce_max(h_comb[:, 20:21], h_comb[:, 0:8], AX.X),
                       reads=[b_comb], writes=[b_comb])
                p.step("vector", lambda e: e.scalar_tensor_tensor(
                    h_comb[:, 0:8], h_comb[:, 8:16], -NEG_BIG, h_comb[:, 0:8], ALU.mult, ALU.add),
                    reads=[b_comb], writes=[b_comb])
                p.step("vector", lambda e: e.tensor_scalar(
                    h_comb[:, 8:16], h_comb[:, 0:8], h_comb[:, 20:21], None, ALU.is_ge),
                    reads=[b_comb], writes=[b_comb])
                p.step("vector", lambda e: e.tensor_mul(h_comb[:, 0:8], h_comb[:, 0:8], h_comb[:, 8:16]),
                       reads=[b_comb], writes=[b_comb])
                p.step("scalar", lambda e: e.activation(h_comb[:, 8:9], psA[:, 8:9], AF.Sigmoid, bias=0.0, scale=1.0),
                       reads=[psA_r[0]], writes=[b_comb])
                p.step("tensor", lambda e: e.transpose(psB[0:16, 0:128], h_comb[:, 0:16], h_const_id_f[:, :]),
                       reads=[b_comb, b_cidf], writes=[psB_r[0]])
                p.step("vector", lambda e, ti=ti: e.tensor_copy(h_cT[:, ti * 128:(ti + 1) * 128], psB[0:9, 0:128]),
                       reads=[psB_r[0]], writes=[b_cT])

            for r in range(9):
                p.dma("sync", lambda e, r=r: e.dma_start(
                    out=s_crow.ap()[r:r + 1, :], in_=h_cT[r:r + 1, :]),
                    reads=[b_cT], writes=[b_crow])

            def gu_tile(dst_scr, b_dst, gate_ap, up_ap, it, bc=None, b_bc=None):
                wb = it % 2
                p.dma("gpsimd", lambda e, gate_ap=gate_ap, wb=wb: e.dma_start(
                    out=h_wg[wb][:, :].rearrange("p (k c) -> p k c", c=128),
                    in_=gate_ap.rearrange("(k p) c -> p k c", p=128)), writes=[b_wg[wb]])
                p.dma("gpsimd", lambda e, up_ap=up_ap, wb=wb: e.dma_start(
                    out=h_wu[wb][:, :].rearrange("p (k c) -> p k c", c=128),
                    in_=up_ap.rearrange("(k p) c -> p k c", p=128)), writes=[b_wu[wb]])

                def gmm(e, wb=wb, which=0):
                    ins = None
                    hw_ = h_wg[wb] if which == 0 else h_wu[wb]
                    ps = psA if which == 0 else psB
                    for s0, w in col_strips(tf):
                        for K in range(DT):
                            ins = e.matmul(ps[:, s0:s0 + w], hw_[:, K * 128:(K + 1) * 128],
                                           h_xfT[:, K * tf + s0:K * tf + s0 + w],
                                           start=(K == 0), stop=(K == DT - 1))
                    return ins
                p.step("tensor", lambda e, wb=wb: gmm(e, wb, 0), reads=[b_wg[wb], b_xfT], writes=regions(psA, 0, tf))
                p.step("tensor", lambda e, wb=wb: gmm(e, wb, 1), reads=[b_wu[wb], b_xfT], writes=regions(psB, 0, tf))
                p.step("scalar", lambda e: e.activation(h_gs[:, :], psA[:, 0:tf], AF.Silu, bias=0.0, scale=1.0),
                       reads=regions(psA, 0, tf), writes=[b_gs])
                if bc is not None:
                    p.step("gpsimd", lambda e, bc=bc: e.tensor_mul(h_gs[:, :], h_gs[:, :], bc[:, :]),
                           reads=[b_gs, b_bc], writes=[b_gs])
                p.step("vector", lambda e: e.tensor_mul(h_hb[:, :], h_gs[:, :], psB[:, 0:tf]),
                       reads=[b_gs] + regions(psB, 0, tf), writes=[b_hb])
                p.dma("sync", lambda e, dst_scr=dst_scr, it=it: e.dma_start(
                    out=dst_scr[it * 128:(it + 1) * 128, :], in_=h_hb[:, :]),
                    reads=[b_hb], writes=[b_dst])

            for it in range(SH_T):
                gu_tile(s_hsh.ap(), b_hsh,
                        t_wsgu.ap()[:, it * 128:(it + 1) * 128],
                        t_wsgu.ap()[:, i_sh + it * 128:i_sh + (it + 1) * 128], it)

            for ei in range(n_e):
                p.dma("sync", lambda e, ei=ei: e.dma_start(
                    out=h_crow1g[0:1, :], in_=s_crow.ap()[ei:ei + 1, :]),
                    reads=[b_crow], writes=[b_crow1g])

                def bcfn(e, ei=ei):
                    ins = None
                    for s0, w in col_strips(tf):
                        ins = e.matmul(psA[:, s0:s0 + w], h_onesr[:, :],
                                       h_crow1g[0:1, s0:s0 + w],
                                       start=True, stop=True)
                    return ins
                p.step("tensor", bcfn, reads=[b_onesr, b_crow1g], writes=regions(psA, 0, tf))
                p.step("vector", lambda e: e.tensor_copy(h_Bc[:, :], psA[:, 0:tf]),
                       reads=regions(psA, 0, tf), writes=[b_Bc])
                for it in range(IE_T):
                    gu_tile(s_he.ap()[ei], b_he,
                            t_wegu.ap()[ei][:, it * 128:(it + 1) * 128],
                            t_wegu.ap()[ei][:, i_e + it * 128:i_e + (it + 1) * 128],
                            it, bc=h_Bc, b_bc=b_Bc)
        p.barrier()

    if F:
        TQ = tf // 2
        with ExitStack() as ph:
            h_hs = ph.enter_context(nc.sbuf_tensor("hsres", [128, SH_T * TQ], BF16))
            b_hs = p.buf("hsres")
            h_wds = ph.enter_context(nc.sbuf_tensor("wds", [128, SH_T * 128], BF16))
            b_wds = p.buf("wds")
            h_Bsg = ph.enter_context(nc.sbuf_tensor("Bsg", [128, TQ], BF16))
            b_Bsg = p.buf("Bsg")
            h_ss = ph.enter_context(nc.sbuf_tensor("ssout", [128, 512], BF16))
            b_ss = p.buf("ssout")
            h_crow1s = ph.enter_context(nc.sbuf_tensor("crow1s", [1, tf], F32))
            b_crow1s = p.buf("crow1s")

            for half in range(2):
                qsl = slice(half * TQ, (half + 1) * TQ)
                p.dma("sync", lambda e, qsl=qsl: e.dma_start(
                    out=h_hs[:, :].rearrange("p (k w) -> p k w", w=TQ),
                    in_=s_hsh.ap()[:, qsl].rearrange("(k p) w -> p k w", p=128)),
                    reads=[b_hsh], writes=[b_hs])

                p.dma("sync", lambda e, half=half: e.dma_start(
                    out=h_crow1s[0:1, 0:TQ], in_=s_crow.ap()[8:9, half * TQ:(half + 1) * TQ]),
                    reads=[b_crow], writes=[b_crow1s])

                def bsgfn(e, half=half):
                    ins = None
                    for s0, w in col_strips(TQ):
                        ins = e.matmul(psB[:, s0:s0 + w], h_onesr[:, :],
                                       h_crow1s[0:1, s0:s0 + w],
                                       start=True, stop=True)
                    return ins
                p.step("tensor", bsgfn, reads=[b_onesr, b_crow1s], writes=regions(psB, 0, TQ))
                p.step("vector", lambda e: e.tensor_copy(h_Bsg[:, :], psB[:, 0:TQ]),
                       reads=regions(psB, 0, TQ), writes=[b_Bsg])

                for ot in range(DT):
                    p.dma("gpsimd", lambda e, ot=ot: e.dma_start(
                        out=h_wds[:, :].rearrange("p (k c) -> p k c", c=128),
                        in_=t_wsdn.ap()[:, ot * 128:(ot + 1) * 128].rearrange("(k p) c -> p k c", p=128)),
                        writes=[b_wds])
                    for s0, w in col_strips(TQ):
                        def sdmm(e, s0=s0, w=w):
                            ins = None
                            for K in range(SH_T):
                                ins = e.matmul(psA[:, 0:w], h_wds[:, K * 128:(K + 1) * 128],
                                               h_hs[:, K * TQ + s0:K * TQ + s0 + w],
                                               start=(K == 0), stop=(K == SH_T - 1))
                            return ins
                        p.step("tensor", sdmm, reads=[b_wds, b_hs], writes=regions(psA, 0, w))
                        p.step("vector", lambda e, s0=s0, w=w: e.tensor_mul(
                            h_ss[:, 0:w], psA[:, 0:w], h_Bsg[:, s0:s0 + w]),
                            reads=regions(psA, 0, w) + [b_Bsg], writes=[b_ss])
                        p.dma("sync", lambda e, ot=ot, half=half, s0=s0, w=w: e.dma_start(
                            out=s_shsc.ap()[ot * 128:(ot + 1) * 128, half * TQ + s0:half * TQ + s0 + w],
                            in_=h_ss[:, 0:w]),
                            reads=[b_ss], writes=[b_shsc])
        p.barrier()

    if F:
        TW = 256 if tf >= 1024 else min(tf, 256)
        OG = 4
        with ExitStack() as ph:
            def fsb2(name, shape, dt):
                hs, bs = [], []
                for i in range(2):
                    hs.append(ph.enter_context(nc.sbuf_tensor(f"{name}{i}", list(shape), dt)))
                    bs.append(p.buf(f"{name}{i}"))
                return hs, bs
            h_heqD, b_heqD = fsb2("heq", [128, n_e * IE_T * TW], BF16)
            h_wd = ph.enter_context(nc.sbuf_tensor("wdq", [128, n_e * IE_T * OG * 128], BF16))
            b_wd = p.buf("wdq")
            h_shsD, b_shsD = fsb2("shs", [128, TW], BF16)
            h_svD, b_svD = fsb2("sv", [128, TW], BF16)
            h_xt2D, b_xt2D = fsb2("xt2", [128, TW], F32)
            h_t1D, b_t1D = fsb2("t1f", [128, TW], F32)
            h_t2D, b_t2D = fsb2("t2b", [128, TW], BF16)
            h_ofD, b_ofD = fsb2("ofin", [128, TW], F32)

            for ocg in range(ceil_div(DT, OG)):
                oc0g = ocg * OG * 128
                ow_g = min(OG * 128, d - oc0g)
                not_g = ow_g // 128
                for ei in range(n_e):
                    p.dma("gpsimd", lambda e, ei=ei, oc0g=oc0g, ow_g=ow_g: e.dma_start(
                        out=h_wd[:, ei * IE_T * OG * 128:ei * IE_T * OG * 128 + IE_T * ow_g].rearrange(
                            "p (k c) -> p k c", c=ow_g),
                        in_=t_wedn.ap()[ei][:, oc0g:oc0g + ow_g].rearrange("(k p) c -> p k c", p=128)),
                        writes=[b_wd])
                for ts in range(tf // TW):
                    tsl = slice(ts * TW, (ts + 1) * TW)
                    hp = ts % 2
                    h_heq, b_heq = h_heqD[hp], b_heqD[hp]
                    for ei in range(n_e):
                        p.dma("sync", lambda e, ei=ei, tsl=tsl, h_heq=h_heq: e.dma_start(
                            out=h_heq[:, ei * IE_T * TW:(ei + 1) * IE_T * TW].rearrange("p (k w) -> p k w", w=TW),
                            in_=s_he.ap()[ei][:, tsl].rearrange("(k p) w -> p k w", p=128)),
                            reads=[b_he], writes=[b_heq])
                    for ot4 in range(not_g):
                        ot = ocg * OG + ot4
                        osl = slice(ot * 128, (ot + 1) * 128)
                        ps = psA if ot4 % 2 == 0 else psB
                        ap2 = (ts * not_g + ot4) % 2
                        h_shs, b_shs = h_shsD[ap2], b_shsD[ap2]
                        h_sv, b_sv = h_svD[ap2], b_svD[ap2]
                        h_xt2, b_xt2 = h_xt2D[ap2], b_xt2D[ap2]
                        h_t1, b_t1 = h_t1D[ap2], b_t1D[ap2]
                        h_t2, b_t2 = h_t2D[ap2], b_t2D[ap2]
                        h_of, b_of = h_ofD[ap2], b_ofD[ap2]

                        def edmm(e, ps=ps, ot4=ot4, ow_g=ow_g, h_heq=h_heq):
                            ins = None
                            first = True
                            for ei in range(n_e):
                                for K in range(IE_T):
                                    ins = e.matmul(
                                        ps[:, 0:TW],
                                        h_wd[:, ei * IE_T * OG * 128 + K * ow_g + ot4 * 128:
                                             ei * IE_T * OG * 128 + K * ow_g + ot4 * 128 + 128],
                                        h_heq[:, (ei * IE_T + K) * TW:(ei * IE_T + K + 1) * TW],
                                        start=first, stop=(ei == n_e - 1 and K == IE_T - 1))
                                    first = False
                            return ins
                        p.step("tensor", edmm, reads=[b_wd, b_heq], writes=regions(ps, 0, TW))
                        p.dma("sync", lambda e, osl=osl, tsl=tsl, h_shs=h_shs: e.dma_start(
                            out=h_shs[:, :], in_=s_shsc.ap()[osl, tsl]),
                            reads=[b_shsc], writes=[b_shs])
                        p.dma("sync", lambda e, osl=osl, tsl=tsl, h_sv=h_sv: e.dma_start_transpose(
                            h_sv[:, :], s_sred.ap()[tsl, osl]),
                            reads=[b_sred], writes=[b_sv])
                        p.dma("sync", lambda e, osl=osl, tsl=tsl, h_xt2=h_xt2: e.dma_start(
                            out=h_xt2[:, :], in_=t_xT.ap()[osl, tsl]),
                            writes=[b_xt2])
                        p.step("vector", lambda e, ps=ps, h_t1=h_t1, h_shs=h_shs: e.tensor_add(
                            h_t1[:, :], ps[:, 0:TW], h_shs[:, :]),
                            reads=regions(ps, 0, TW) + [b_shs], writes=[b_t1])
                        p.step("scalar", lambda e, h_t1=h_t1, h_t2=h_t2: e.activation(
                            h_t2[:, :], h_t1[:, :], AF.Silu, bias=0.0, scale=1.0),
                            reads=[b_t1], writes=[b_t2])
                        p.step("vector", lambda e, h_t2=h_t2, h_sv=h_sv: e.tensor_mul(
                            h_t2[:, :], h_t2[:, :], h_sv[:, :]),
                            reads=[b_t2, b_sv], writes=[b_t2])
                        p.step("vector", lambda e, h_t2=h_t2, h_xt2=h_xt2, h_of=h_of: e.tensor_add(
                            h_of[:, :], h_t2[:, :], h_xt2[:, :]),
                            reads=[b_t2, b_xt2], writes=[b_of])
                        p.dma("sync", lambda e, osl=osl, tsl=tsl, h_of=h_of: e.dma_start(
                            out=t_out.ap()[osl, tsl], in_=h_of[:, :]),
                            reads=[b_of], writes=[])

    p.barrier()
    p.finalize()
    p.close()
    stack_global.close()
    return declared_inputs

